# version 05
# baseline (speedup 1.0000x reference)
import numpy as np
import concourse.bass as bass
import concourse.mybir as mybir
from concourse.bass_utils import run_bass_kernel_spmd

F32 = mybir.dt.float32
BF16 = mybir.dt.bfloat16
I32 = mybir.dt.int32
ALU = mybir.AluOpType
ACT = mybir.ActivationFunctionType
AX = mybir.AxisListType.X

STAGE = 4
NGRUN = 8
P1PART = 6
S_SUB = 3
TERN_ON = True
NCORE = 8
B, T, D = 2, 16384, 1024
TC = T // 4
NT = TC // 128
NG = 8
PK = [0, 512, 896, 1152]
PW = [512, 384, 256, 128]
EPS1 = 1e-5
EPS_ATT = 1e-6


def build():
    nc = bass.Bass("TRN2", target_bir_lowering=False)

    x_h = nc.dram_tensor("x", [TC, D], F32, kind="ExternalInput").ap()
    wq_h = nc.dram_tensor("wq", [D, D], F32, kind="ExternalInput").ap()
    wk_h = nc.dram_tensor("wk", [D, D], F32, kind="ExternalInput").ap()
    wv_h = nc.dram_tensor("wv", [D, D], F32, kind="ExternalInput").ap()
    wo_h = nc.dram_tensor("wo", [D, D], F32, kind="ExternalInput").ap()
    wf_h = nc.dram_tensor("wf", [D, D], F32, kind="ExternalInput").ap()
    bqc_h = nc.dram_tensor("bqc", [128, 8], F32, kind="ExternalInput").ap()
    bkc_h = nc.dram_tensor("bkc", [128, 8], F32, kind="ExternalInput").ap()
    bv_h = nc.dram_tensor("bvr", [1, D], F32, kind="ExternalInput").ap()
    bo_h = nc.dram_tensor("bor", [1, D], F32, kind="ExternalInput").ap()
    gam_h = nc.dram_tensor("gamb", [128, D], F32, kind="ExternalInput").ap()
    bet_h = nc.dram_tensor("betb", [128, D], F32, kind="ExternalInput").ap()
    msel_h = nc.dram_tensor("msel", [128, 4], F32, kind="ExternalInput").ap()
    identh = nc.dram_tensor("identc", [128, 128], F32, kind="ExternalInput").ap()
    mtrih = nc.dram_tensor("mtric", [128, 128], F32, kind="ExternalInput").ap()
    out_h = nc.dram_tensor("out", [TC, D], F32, kind="ExternalOutput").ap()

    sp_q = nc.dram_tensor("sp_q", [NG * 128, 4096], BF16).ap()
    sp_v = nc.dram_tensor("sp_v", [NG * 128, 4096], BF16).ap()
    sp_a = nc.dram_tensor("sp_a", [NG * 128, 1280], BF16).ap()
    sp_x2 = nc.dram_tensor("sp_x2", [TC, D], F32).ap()
    sp_s = nc.dram_tensor("sp_s", [NG * 1024, 1024], BF16).ap()
    sp_t = nc.dram_tensor("sp_t", [1024, 1024], BF16).ap()
    cc_in_t = nc.dram_tensor("cc_in", [1025, 1024], BF16)
    cc_out_t = nc.dram_tensor("cc_out", [4 * 1025, 1024], BF16)
    mx_in_t = nc.dram_tensor("mx_in", [1, 1], F32)
    mx_out_t = nc.dram_tensor("mx_out", [1, 1], F32)
    cc_in, cc_out = cc_in_t.ap(), cc_out_t.ap()
    mx_in, mx_out = mx_in_t.ap(), mx_out_t.ap()

    SB = [
        ("w1", 8192, BF16), ("w2", 8192, BF16), ("w3", 8192, BF16),
        ("kvr", 8192, F32), ("gbc", 1024, F32), ("bbc", 1024, F32),
        ("xg0", 1024, F32), ("xg1", 1024, F32), ("xg2", 1024, F32),
        ("xnf0", 1024, F32), ("xnf1", 1024, F32),
        ("xnb0", 1024, BF16), ("xnb1", 1024, BF16),
        ("xnT0", 4096, BF16), ("xnT1", 4096, BF16),
        ("qT", 4096, BF16), ("kT", 4096, BF16),
        ("ktok", 4096, BF16), ("vtok", 4096, BF16),
        ("aT0", 1280, BF16), ("aT1", 1280, BF16),
        ("mn0", 512, F32), ("mn1", 512, F32),
        ("Rb0", 512, BF16), ("Rb1", 512, BF16), ("sgb", 512, BF16),
        ("u", 1024, F32), ("i32h", 512, I32),
        ("x2t0", 1024, F32), ("x2t1", 1024, F32),
        ("rT0", 1024, BF16), ("rT1", 1024, BF16),
        ("bqc", 8, F32), ("bkc", 8, F32), ("bvb", 1024, BF16), ("boeff", 1024, BF16),
        ("mselc", 4, F32), ("stats", 96, F32), ("scr0", 8, F32), ("scr1", 8, F32),
        ("ksg", 64, F32), ("kst", 8, F32), ("ksA", 8, F32), ("ksB", 8, F32),
        ("kscb", 8, BF16), ("amx0", 1, F32), ("amx1", 1, F32),
        ("ident", 128, F32), ("mask_ut", 128, F32),
        ("ones_cb", 1, BF16), ("ones_rb", 128, BF16),
        ("rrowb", 512, BF16), ("smf", 16, F32),
        ("tw32", 32, F32), ("tn32", 32, F32), ("tc32", 32, F32),
        ("bcs", 2, F32), ("dcol", 1, F32),
        ("ta", 256, F32), ("tb2", 256, F32), ("tc2", 256, F32), ("td", 256, F32),
        ("tf", 256, BF16),
    ]
    used = sum(w * {F32: 4, BF16: 2, I32: 4}[d] for _, w, d in SB)
    assert used <= 207000, f"SBUF over budget: {used}"

    sems = ["M", "V", "S", "C",
            "MS", "Gc", "Lx0", "Lx1", "Lx2", "Gx0", "Gx1", "WP0", "WP1", "Ta", "Qt",
            "TT00", "TT01", "TT10", "TT11", "Qx0", "Qx1", "Qq", "Qv", "KT",
            "Qa0", "Qa1", "Qs0", "Qs1", "Qm", "Lt0", "Lt1", "Qw0", "Qw1",
            "Lm", "PS", "Gk",
            "Sg0", "Sg1", "Ld0", "Ld1", "X20", "X21", "Qr0", "Qr1", "Qo0", "Qo1"]
    cnt = {s: 0 for s in sems}
    plan = {"sync": [], "scalar": [], "vector": [], "tensor": [], "gpsimd": []}
    tn = {}
    free = {}

    def ap(name, sl=None):
        t = tn[name]
        return t[:, :] if sl is None else t[:, sl]

    def rowap(name, r, sl):
        return tn[name][r:r + 1, sl]

    def emit(eng, fn):
        plan[eng].append(fn)

    def wait(eng, sem, val):
        if val > 0:
            emit(eng, lambda e, s=sem, v=val: e.wait_ge(tn["sem_" + s], v))

    def gate(eng, key):
        for s, v in free.pop(key, {}).items():
            wait(eng, s, v)

    def setfree(key, *pairs):
        d = free.setdefault(key, {})
        for s, v in pairs:
            d[s] = max(d.get(s, 0), v)

    def dma(eng, sem, out, in_):
        emit(eng, lambda e, o=out, i=in_, s=sem: e.dma_start(out=o, in_=i).then_inc(tn["sem_" + s], 16))
        cnt[sem] += 16
        return cnt[sem]

    def dmat(eng, sem, out, in_):
        emit(eng, lambda e, o=out, i=in_, s=sem: e.dma_start_transpose(o, i).then_inc(tn["sem_" + s], 16))
        cnt[sem] += 16
        return cnt[sem]

    def mm(lhsT, rhs, out, start, stop):
        if stop:
            emit("tensor", lambda e, a=lhsT, b=rhs, o=out, st=start:
                 e.matmul(o, a, b, start=st, stop=True).then_inc(tn["sem_M"], 1))
            cnt["M"] += 1
        else:
            emit("tensor", lambda e, a=lhsT, b=rhs, o=out, st=start:
                 e.matmul(o, a, b, start=st, stop=False))
        return cnt["M"]

    def vop(fn):
        wait("vector", "V", cnt["V"])
        emit("vector", lambda e, f=fn: f(e).then_inc(tn["sem_V"], 1))
        cnt["V"] += 1
        return cnt["V"]

    def sop(fn):
        wait("scalar", "S", cnt["S"])
        emit("scalar", lambda e, f=fn: f(e).then_inc(tn["sem_S"], 1))
        cnt["S"] += 1
        return cnt["S"]

    def gop(fn):
        wait("gpsimd", "C", cnt["C"])
        emit("gpsimd", lambda e, f=fn: f(e).then_inc(tn["sem_C"], 1))
        cnt["C"] += 1
        return cnt["C"]

    import contextlib
    with contextlib.ExitStack() as stack:
        for name, w, dt in SB:
            tn[name] = stack.enter_context(nc.sbuf_tensor("sb_" + name, [128, w], dt))
        for pname in ["ps_q0", "ps_q1", "ps_v0", "ps_v1", "ps_a", "ps_s0", "ps_s1"]:
            tn[pname] = stack.enter_context(nc.psum_tensor(pname, [128, 512], F32))
        for s in sems:
            tn["sem_" + s] = stack.enter_context(nc.semaphore("sem" + s))
        block = stack.enter_context(nc.Block())

        dma("sync", "MS", ap("bqc"), bqc_h)
        dma("sync", "MS", ap("bkc"), bkc_h)
        dma("sync", "MS", ap("gbc"), gam_h)
        dma("sync", "MS", ap("bbc"), bet_h)
        dma("sync", "MS", ap("mselc"), msel_h)
        dma("sync", "MS", ap("ident"), identh)
        dma("sync", "MS", ap("mask_ut"), mtrih)
        lp = cnt["MS"]
        dma("gpsimd", "Gc", tn["bvb"][0:1, :], bv_h)

        for i in range(3):
            dma("sync", f"Lx{i}", ap(f"xg{i}"), x_h[i * 128:(i + 1) * 128, :])
        wait("scalar", "MS", lp)
        vop(lambda e: e.memset(ap("ones_cb"), 1.0))
        vop(lambda e: e.memset(tn["ones_rb"][0:2, :], 1.0))
        vop(lambda e: e.memset(ap("kvr"), 0.0))
        vop(lambda e: e.memset(ap("amx0"), 0.0))
        vop(lambda e: e.memset(ap("ksA"), 0.0))
        wait("vector", "MS", lp)

        for wi, (wh, wt) in enumerate([(wq_h, "w1"), (wk_h, "w2"), (wv_h, "w3")]):
            for ob in range(8):
                par = (wi * 8 + ob) % 2
                st = f"xnb{par}"
                gate("gpsimd", st)
                gc = dma("gpsimd", f"Gx{par}", ap(st), wh[ob * 128:(ob + 1) * 128, :])
                wait("sync", f"Gx{par}", gc)
                dest = tn[wt].ap().rearrange("p (d o) -> p d o", d=8)[:, :, ob * 128:(ob + 1) * 128]
                lc = dmat("sync", f"WP{par}", dest, ap(st))
                setfree(st, (f"WP{par}", lc))
        wp_L = (cnt["WP0"], cnt["WP1"])

        mid = {}

        def tern_pass1(j):
            ob, sub = divmod(j, 4)
            gate("sync", "ta")
            ld = dma("sync", "Ta", ap("ta"), wf_h[ob * 128:(ob + 1) * 128, sub * 256:(sub + 1) * 256])
            gate("scalar", "tb2")
            wait("scalar", "Ta", ld)
            sc = sop(lambda e, o=ap("tb2"), a=ap("ta"), acc=ap("tw32", slice(j, j + 1)):
                     e.activation(o, a, ACT.Abs, accum_out=acc))
            setfree("ta", ("S", sc))
            setfree("tb2", ("S", sc))
            mid["tp1_s"] = sc

        def tern_delta():
            gate("vector", "scr0")
            wait("vector", "S", mid["tp1_s"])
            v1 = vop(lambda e, o=ap("scr0", slice(0, 1)), a=ap("tw32"): e.reduce_sum(o, a, AX))
            gate("tensor", "ps_a")
            wait("tensor", "V", v1)
            wait("tensor", "MS", lp)
            mc = mm(ap("scr0", slice(0, 1)), ap("ident"), tn["ps_a"][0:1, 0:128], True, True)
            setfree("scr0", ("M", mc))
            wait("vector", "M", mc)
            vop(lambda e, o=rowap("td", 0, slice(0, 128)), a=tn["ps_a"][0:1, 0:128]:
                e.tensor_copy(o, a))
            vop(lambda e, o=rowap("smf", 0, slice(2, 3)), a=rowap("td", 0, slice(0, 128)):
                e.reduce_sum(o, a, AX))
            v2 = vop(lambda e, o=rowap("smf", 0, slice(3, 4)), a=rowap("smf", 0, slice(2, 3)):
                     e.tensor_scalar(o, a, 0.7 / (1024.0 * 1024.0), None, ALU.mult))
            setfree("ps_a", ("V", cnt["V"]))
            gate("tensor", "ps_a")
            wait("tensor", "V", v2)
            mc2 = mm(tn["mask_ut"][0:1, :], rowap("smf", 0, slice(3, 4)), tn["ps_a"][:, 0:1], True, True)
            wait("vector", "M", mc2)
            v3 = vop(lambda e, o=ap("dcol"), a=tn["ps_a"][:, 0:1]: e.tensor_copy(o, a))
            setfree("ps_a", ("V", v3))
            mid["dcol_v"] = v3

        def tern_pass2(j):
            ob, sub = divmod(j, 4)
            gate("sync", "ta")
            ld = dma("sync", "Ta", ap("ta"), wf_h[ob * 128:(ob + 1) * 128, sub * 256:(sub + 1) * 256])
            gate("scalar", "tb2")
            wait("scalar", "Ta", ld)
            s_ab = sop(lambda e, o=ap("tb2"), a=ap("ta"): e.activation(o, a, ACT.Abs))
            wait("vector", "S", s_ab)
            wait("vector", "V", mid["dcol_v"])
            vop(lambda e, o=ap("tc2"), a=ap("tb2"), d=ap("dcol"):
                e.tensor_scalar(o, a, d, None, ALU.is_gt))
            vop(lambda e, o=ap("tc32", slice(j, j + 1)), a=ap("tc2"): e.reduce_sum(o, a, AX))
            v_d = vop(lambda e, o=ap("td"), a=ap("tb2"), b=ap("tc2"):
                      e.tensor_tensor(o, a, b, ALU.mult))
            vop(lambda e, o=ap("tn32", slice(j, j + 1)), a=ap("td"): e.reduce_sum(o, a, AX))
            wait("scalar", "V", v_d)
            s_sg = sop(lambda e, o=ap("tb2"), a=ap("ta"): e.activation(o, a, ACT.Sign))
            setfree("ta", ("S", s_sg))
            gate("vector", "tf")
            wait("vector", "S", s_sg)
            v_t = vop(lambda e, o=ap("tf"), a=ap("tb2"), b=ap("tc2"):
                      e.tensor_tensor(o, a, b, ALU.mult))
            setfree("tb2", ("V", v_t))
            wait("scalar", "V", v_t)
            q_t = dma("scalar", "Qt", sp_t[ob * 128:(ob + 1) * 128, sub * 256:(sub + 1) * 256], ap("tf"))
            setfree("tf", ("Qt", q_t))

        qk_last_v = {}
        for g in range(NGRUN):
            gp = g % 2
            xnT = f"xnT{gp}"
            for i in range(4):
                ti = g * 4 + i
                slot = f"xg{ti % 3}"
                tp = f"xnf{i % 2}"
                tb = f"xnb{i % 2}"
                x2 = f"x2t{i % 2}"
                scr = f"scr{i % 2}"
                if ti >= 3:
                    gate("sync", slot)
                    ld = dma("sync", f"Lx{ti % 3}", ap(slot), x_h[ti * 128:(ti + 1) * 128, :])
                else:
                    ld = 16
                gate("scalar", tp)
                gate("scalar", scr)
                wait("scalar", f"Lx{ti % 3}", ld)
                sop(lambda e, o=ap(tp), x=ap(slot), a=ap(scr, slice(0, 1)):
                    e.activation(o, x, ACT.Identity, accum_out=a))
                s_st = sop(lambda e, o=ap(tp), x=ap(slot), a=ap(scr, slice(1, 2)):
                           e.activation(o, x, ACT.Square, accum_out=a))
                stc = ti * 3
                mneg = ap("stats", slice(stc, stc + 1))
                r1c = ap("stats", slice(stc + 1, stc + 2))
                r2c = ap("stats", slice(stc + 2, stc + 3))
                wait("vector", "S", s_st)
                vop(lambda e, o=mneg, a=ap(scr, slice(0, 1)):
                    e.tensor_scalar(o, a, -1.0 / 1024.0, None, ALU.mult))
                vop(lambda e, o=ap(scr, slice(2, 3)), a=ap(scr, slice(1, 2)):
                    e.tensor_scalar(o, a, 1.0 / 1024.0, None, ALU.mult))
                vop(lambda e, o=ap(scr, slice(3, 4)), a=mneg:
                    e.tensor_tensor(o, a, a, ALU.mult))
                v_var = vop(lambda e, o=ap(scr, slice(4, 5)), a=ap(scr, slice(2, 3)), b=ap(scr, slice(3, 4)):
                            e.tensor_tensor(o, a, b, ALU.subtract))
                vop(lambda e, o=ap(scr, slice(6, 7)), a=ap(scr, slice(4, 5)):
                    e.tensor_scalar(o, a, EPS1, None, ALU.add))
                wait("scalar", "V", cnt["V"])
                s_q1 = sop(lambda e, o=ap(scr, slice(5, 6)), a=ap(scr, slice(6, 7)):
                           e.activation(o, a, ACT.Sqrt))
                wait("vector", "S", s_q1)
                vop(lambda e, o=r1c, a=ap(scr, slice(5, 6)): e.reciprocal(o, a))
                vop(lambda e, o=ap(scr, slice(5, 6)), a=ap(scr, slice(6, 7)):
                    e.reciprocal(o, a))
                vop(lambda e, o=ap(scr, slice(7, 8)), a=ap(scr, slice(4, 5)), b=ap(scr, slice(5, 6)):
                    e.tensor_tensor(o, a, b, ALU.mult))
                v_v2 = vop(lambda e, o=ap(scr, slice(6, 7)), a=ap(scr, slice(7, 8)):
                           e.tensor_scalar(o, a, EPS1, None, ALU.add))
                wait("scalar", "V", v_v2)
                s_r2 = sop(lambda e, o=ap(scr, slice(5, 6)), a=ap(scr, slice(6, 7)):
                           e.activation(o, a, ACT.Sqrt))
                wait("vector", "S", s_r2)
                vop(lambda e, o=r2c, a=ap(scr, slice(5, 6)): e.reciprocal(o, a))
                v_xn = vop(lambda e, o=ap(tp), x=ap(slot), m=mneg, r=r1c:
                           e.tensor_scalar(o, x, m, r, ALU.add, ALU.mult))
                gate("gpsimd", tb)
                wait("gpsimd", "V", v_xn)
                g_c = dma("gpsimd", f"Gx{i % 2}", ap(tb), ap(tp))
                if i == 0:
                    gate("sync", xnT)
                wait("sync", f"Gx{i % 2}", g_c)
                dest = tn[xnT].ap().rearrange("p (d t) -> p d t", d=8)[:, :, i * 128:(i + 1) * 128]
                ttsem = f"TT{g % 2}{i % 2}"
                dmat("sync", ttsem, dest, ap(tb))
                setfree(tb, (ttsem, cnt[ttsem]))
                if i == 3:
                    mid[("tp_done", g)] = [(f"TT{g % 2}0", cnt[f"TT{g % 2}0"]),
                                           (f"TT{g % 2}1", cnt[f"TT{g % 2}1"])]
                v_u = vop(lambda e, o=ap("u"), a=ap(tp), b=ap("gbc"):
                          e.tensor_tensor(o, a, b, ALU.mult))
                gate("vector", x2)
                v_x2 = vop(lambda e, o=ap(x2), a=ap("u"), r=r2c, b=ap("bbc"):
                           e.scalar_tensor_tensor(o, a, r, b, ALU.mult, ALU.add))
                setfree(tp, (f"Gx{i % 2}", g_c), ("V", v_u))
                vop(lambda e, o=ap(scr, slice(0, 1)), a=ap(x2):
                    e.reduce_max(o, a, AX, apply_absolute_value=True))
                v_am = vop(lambda e, o=ap(f"amx{(ti + 1) % 2}"), a=ap(scr, slice(0, 1)), p=ap(f"amx{ti % 2}"):
                           e.tensor_scalar(o, a, p, None, ALU.max))
                setfree(scr, ("S", s_r2), ("V", v_am))
                wait("scalar", "V", v_x2)
                q_sp = dma("scalar", f"Qx{i % 2}", sp_x2[ti * 128:(ti + 1) * 128, :], ap(x2))
                setfree(x2, (f"Qx{i % 2}", q_sp))
                setfree(slot, ("V", v_xn), ("S", s_st))
            if TERN_ON:
                if g < NGRUN // 2 or NGRUN == 1:
                    for j in range(g * 8, g * 8 + 8):
                        tern_pass1(j)
                    if g == max(NGRUN // 2, 1) - 1:
                        tern_delta()
                else:
                    for j in range((g - NGRUN // 2) * 8, (g - NGRUN // 2) * 8 + 8):
                        tern_pass2(j)
            if P1PART >= 2:
                for proj, (wt, dst, bc_) in enumerate([("w1", "qT", "bqc"), ("w2", "kT", "bkc")]):
                    for ob in range(8):
                        pb = f"ps_q{ob % 2}"
                        gate("tensor", pb)
                        if ob == 0:
                            for sm, vv in mid[("tp_done", g)]:
                                wait("tensor", sm, vv)
                            if g == 0 and proj == 0:
                                wait("tensor", "WP0", wp_L[0])
                                wait("tensor", "WP1", wp_L[1])
                        for dc in range(8):
                            mmc = mm(tn[wt][:, dc * 1024 + ob * 128: dc * 1024 + (ob + 1) * 128],
                                     tn[xnT][:, dc * 512:(dc + 1) * 512],
                                     ap(pb), start=(dc == 0), stop=(dc == 7))
                        mnb = f"mn{ob % 2}"
                        rbb = f"Rb{ob % 2}"
                        gate("vector", mnb)
                        wait("vector", "M", mmc)
                        v_mn = vop(lambda e, o=ap(mnb), p=ap(pb), b=ap(bc_, slice(ob, ob + 1)):
                                   e.tensor_scalar(o, p, b, 0.0, ALU.add, ALU.min))
                        if ob == 0:
                            gate("scalar", dst)
                        wait("scalar", "V", v_mn)
                        s_e = sop(lambda e, o=tn[dst][:, ob * 512:(ob + 1) * 512], a=ap(mnb):
                                  e.activation(o, a, ACT.Exp))
                        setfree(mnb, ("S", s_e))
                        gate("scalar", rbb)
                        wait("scalar", "M", mmc)
                        s_r = sop(lambda e, o=ap(rbb), p=ap(pb), b=ap(bc_, slice(ob, ob + 1)):
                                  e.activation(o, p, ACT.Relu, bias=b, scale=1.0))
                        setfree(pb, ("V", v_mn), ("S", s_r))
                        wait("vector", "S", s_r)
                        v_q = vop(lambda e, o=tn[dst][:, ob * 512:(ob + 1) * 512], a=ap(rbb):
                                  e.tensor_tensor(o, o, a, ALU.add))
                        setfree(rbb, ("V", v_q))
                        qk_last_v[(g, dst)] = v_q
            if P1PART >= 3:
                for tb4 in range(4):
                    for eh in range(2):
                        idx = tb4 * 2 + eh
                        pb = f"ps_v{idx % 2}"
                        gate("tensor", pb)
                        if g == 0 and idx == 0:
                            wait("tensor", "Gc", 16)
                        for dc in range(8):
                            mm(tn[xnT][:, dc * 512 + tb4 * 128: dc * 512 + (tb4 + 1) * 128],
                               tn["w3"][:, dc * 1024 + eh * 512: dc * 1024 + (eh + 1) * 512],
                               ap(pb), start=(dc == 0), stop=False)
                        mmc = mm(tn["ones_rb"][0:1, :],
                                 tn["bvb"][0:1, eh * 512:(eh + 1) * 512],
                                 ap(pb), start=False, stop=True)
                        if idx == 0:
                            gate("scalar", "vtok")
                        wait("scalar", "M", mmc)
                        s_v = sop(lambda e, o=tn["vtok"][:, tb4 * 1024 + eh * 512: tb4 * 1024 + (eh + 1) * 512],
                                  p=ap(pb): e.copy(o, p))
                        setfree(pb, ("S", s_v))
                v_evac_s = cnt["S"]
                mid[("xnT_free", g)] = cnt["M"]
                setfree(xnT, ("M", cnt["M"]))
            if P1PART >= 4:
                for dkb in range(8):
                    vop(lambda e, o=ap("ksg", slice(g * 8 + dkb, g * 8 + dkb + 1)),
                        a=tn["kT"][:, dkb * 512:(dkb + 1) * 512]:
                        e.reduce_sum(o, a, AX))
                ks_v = cnt["V"]
                wait("scalar", "V", qk_last_v[(g, "qT")])
                q_spq = dma("scalar", "Qq", sp_q[g * 128:(g + 1) * 128, :], ap("qT"))
                wait("scalar", "S", v_evac_s)
                q_spv = dma("scalar", "Qv", sp_v[g * 128:(g + 1) * 128, :], ap("vtok"))
                if g == 0:
                    gate("sync", "ktok")
                wait("sync", "V", qk_last_v[(g, "kT")])
                for dkb in range(8):
                    dest = tn["ktok"].ap().rearrange("p (s d) -> p s d", s=4)[:, :, dkb * 128:(dkb + 1) * 128]
                    dmat("sync", "KT", dest, tn["kT"][:, dkb * 512:(dkb + 1) * 512])
                kt_L = cnt["KT"]
            if P1PART >= 5:
                aT = f"aT{gp}"
                gate("vector", aT)
                a_evac = []
                for sb in range(4):
                    pb = "ps_a" if sb % 2 == 0 else "ps_s0"
                    gate("tensor", pb)
                    if sb == 0:
                        wait("tensor", "V", qk_last_v[(g, "qT")])
                        wait("tensor", "V", qk_last_v[(g, "kT")])
                    for dc in range(8):
                        mmc = mm(tn["kT"][:, dc * 512 + sb * 128: dc * 512 + (sb + 1) * 128],
                                 tn["qT"][:, dc * 512 + sb * 128: (dc + 1) * 512],
                                 tn[pb][:, sb * 128:512], start=(dc == 0), stop=(dc == 7))
                    wait("vector", "M", mmc)
                    vop(lambda e, o=tn[aT][:, PK[sb]:PK[sb] + 128],
                        p=tn[pb][:, sb * 128:sb * 128 + 128], m=ap("mask_ut"):
                        e.tensor_tensor(o, p, m, ALU.mult))
                    if sb < 3:
                        v_a = vop(lambda e, o=tn[aT][:, PK[sb] + 128:PK[sb] + PW[sb]],
                                  p=tn[pb][:, sb * 128 + 128:512]:
                                  e.tensor_copy(o, p))
                    else:
                        v_a = cnt["V"]
                    setfree(pb, ("V", v_a))
                    a_evac.append(v_a)
                wait("scalar", "V", a_evac[-1])
                q_spa = dma("scalar", f"Qa{g % 2}", sp_a[g * 128:(g + 1) * 128, :], tn[aT][:, :])
                setfree(aT, (f"Qa{g % 2}", q_spa))
                setfree("qT", ("Qq", q_spq), ("M", cnt["M"]))
                setfree("kT", ("M", cnt["M"]), ("V", ks_v), ("KT", kt_L))
            if P1PART >= 6:
                for dkb in range(8):
                    for eh in range(2):
                        idx = dkb * 2 + eh
                        pb = f"ps_s{idx % 2}"
                        gate("tensor", pb)
                        if idx == 0:
                            wait("tensor", "KT", kt_L)
                            wait("tensor", "S", v_evac_s)
                        for sb in range(4):
                            mmc = mm(tn["ktok"][:, sb * 1024 + dkb * 128: sb * 1024 + (dkb + 1) * 128],
                                     tn["vtok"][:, sb * 1024 + eh * 512: sb * 1024 + (eh + 1) * 512],
                                     ap(pb), start=(sb == 0), stop=(sb == 3))
                        fr = []
                        if S_SUB >= 1:
                            wait("vector", "M", mmc)
                            v_kv = vop(lambda e, o=tn["kvr"][:, dkb * 1024 + eh * 512: dkb * 1024 + (eh + 1) * 512],
                                       p=ap(pb): e.tensor_tensor(o, o, p, ALU.add))
                            fr.append(("V", v_kv))
                        if S_SUB >= 2:
                            sg = f"rT{idx % 2}"
                            gate("vector", sg + "_s")
                            wait("vector", "M", mmc)
                            s_c = vop(lambda e, o=tn[sg][:, 0:512], p=ap(pb): e.tensor_copy(o, p))
                            fr.append(("V", s_c))
                        setfree(pb, *fr)
                        if S_SUB >= 3:
                            wait("scalar", "V", s_c)
                            q_s = dma("scalar", f"Qs{idx % 2}",
                                      sp_s[g * 1024 + dkb * 128: g * 1024 + (dkb + 1) * 128,
                                           eh * 512:(eh + 1) * 512],
                                      tn[sg][:, 0:512])
                            setfree(sg + "_s", (f"Qs{idx % 2}", q_s))
                setfree("ktok", ("M", cnt["M"]))
                setfree("vtok", ("M", cnt["M"]), ("Qv", q_spv))
                mid[("kvr_v", g)] = cnt["V"]

        if STAGE >= 3:
            vop(lambda e, o=ap("kst"), a=ap("ksg", slice(0, 8)), b=ap("ksg", slice(8, 16)):
                e.tensor_tensor(o, a, b, ALU.add))
            for g2 in range(2, 8):
                vop(lambda e, o=ap("kst"), a=ap("kst"), b=ap("ksg", slice(g2 * 8, g2 * 8 + 8)):
                    e.tensor_tensor(o, a, b, ALU.add))
            kst_v = cnt["V"]
            gate("tensor", "ps_a")
            wait("tensor", "V", cnt["V"])
            mc = mm(ap("amx0"), ap("ident"), tn["ps_a"][0:1, 0:128], True, True)
            wait("vector", "M", mc)
            vop(lambda e, o=rowap("td", 0, slice(0, 128)), a=tn["ps_a"][0:1, 0:128]:
                e.tensor_copy(o, a))
            v_mx = vop(lambda e, o=rowap("smf", 0, slice(0, 1)), a=rowap("td", 0, slice(0, 128)):
                       e.reduce_max(o, a, AX))
            setfree("ps_a", ("V", cnt["V"]))
            wait("scalar", "V", v_mx)
            q_mx = dma("scalar", "Qm", mx_in, rowap("smf", 0, slice(0, 1)))
            gate("vector", "scr0")
            vop(lambda e, o=ap("scr0", slice(0, 1)), a=ap("tn32"): e.reduce_sum(o, a, AX))
            v_nc = vop(lambda e, o=ap("scr0", slice(1, 2)), a=ap("tc32"): e.reduce_sum(o, a, AX))
            gate("tensor", "ps_a")
            wait("tensor", "V", v_nc)
            mm(ap("scr0", slice(0, 1)), ap("ident"), tn["ps_a"][0:1, 0:128], True, True)
            mc2 = mm(ap("scr0", slice(1, 2)), ap("ident"), tn["ps_a"][0:1, 128:256], True, True)
            wait("vector", "M", mc2)
            vop(lambda e, o=rowap("td", 0, slice(0, 256)), a=tn["ps_a"][0:1, 0:256]:
                e.tensor_copy(o, a))
            vop(lambda e, o=rowap("smf", 0, slice(5, 6)), a=rowap("td", 0, slice(0, 128)):
                e.reduce_sum(o, a, AX))
            vop(lambda e, o=rowap("smf", 0, slice(6, 7)), a=rowap("td", 0, slice(128, 256)):
                e.reduce_sum(o, a, AX))
            setfree("ps_a", ("V", cnt["V"]))
            vop(lambda e, o=rowap("smf", 0, slice(7, 8)), a=rowap("smf", 0, slice(6, 7)):
                e.tensor_scalar(o, a, 1.0, None, ALU.max))
            vop(lambda e, o=rowap("smf", 0, slice(8, 9)), a=rowap("smf", 0, slice(7, 8)):
                e.reciprocal(o, a))
            v_al = vop(lambda e, o=rowap("smf", 0, slice(9, 10)), a=rowap("smf", 0, slice(5, 6)),
                       b=rowap("smf", 0, slice(8, 9)): e.tensor_tensor(o, a, b, ALU.mult))
            wait("gpsimd", "V", mid[("kvr_v", NGRUN - 1)])
            for dkb in range(8):
                dma("gpsimd", "Gc", cc_in[dkb * 128:(dkb + 1) * 128, 0:1024],
                    tn["kvr"][:, dkb * 1024:(dkb + 1) * 1024])
            gate("tensor", "ps_s1")
            wait("tensor", "V", kst_v)
            mc_ks = mm(ap("kst"), ap("ident"), tn["ps_s1"][0:8, 0:128], True, True)
            wait("vector", "M", mc_ks)
            gate("vector", "rT0_s")
            v_ksr = vop(lambda e, o=tn["rT0"][0:8, 0:128], a=tn["ps_s1"][0:8, 0:128]:
                        e.tensor_copy(o, a))
            setfree("ps_s1", ("V", v_ksr))
            wait("gpsimd", "V", v_ksr)
            dma("gpsimd", "Gc",
                cc_in[1024:1025, :].rearrange("r (d p) -> (r d) p", d=8),
                tn["rT0"][0:8, 0:128])
            g_cc = cnt["Gc"]
            setfree("rT0_s", ("Gc", g_cc))
            wait("gpsimd", "Gc", g_cc)
            wait("gpsimd", "Qm", q_mx)
            ag_done = gop(lambda e: e.collective_compute(
                "AllGather", ALU.bypass,
                replica_groups=[[0, 1, 2, 3], [4, 5, 6, 7]],
                ins=[cc_in_t.ap()], outs=[cc_out_t.ap()]))
            cc_done = gop(lambda e: e.collective_compute(
                "AllReduce", ALU.max,
                replica_groups=[list(range(8))],
                ins=[mx_in_t.ap()], outs=[mx_out_t.ap()]))

            for ob in range(8):
                par = ob % 2
                st = f"xnb{par}"
                gate("gpsimd", st)
                gc = dma("gpsimd", f"Gx{par}", ap(st), wo_h[ob * 128:(ob + 1) * 128, :])
                wait("sync", f"Gx{par}", gc)
                wait("sync", "M", mid[("xnT_free", NGRUN - 1)])
                dest = tn["w1"].ap().rearrange("p (d o) -> p d o", d=8)[:, :, ob * 128:(ob + 1) * 128]
                lc = dmat("sync", f"WP{par}", dest, ap(st))
                setfree(st, (f"WP{par}", lc))
            wo_WP = (cnt["WP0"], cnt["WP1"])
            for ob in range(8):
                par = ob % 2
                st = f"xnb{par}"
                gate("sync", st)
                wait("sync", "Qt", cnt["Qt"])
                ld = dma("sync", f"Lt{par}", ap(st), sp_t[ob * 128:(ob + 1) * 128, :])
                wait("scalar", f"Lt{par}", ld)
                dest = tn["w2"].ap().rearrange("p (d o) -> p d o", d=8)[:, :, ob * 128:(ob + 1) * 128]
                lc = dmat("scalar", f"Qw{par}", dest, ap(st))
                setfree(st, (f"Qw{par}", lc))
            w2_Q = (cnt["Qw0"], cnt["Qw1"])

            wait("sync", "C", cc_done)
            l_mx = dma("sync", "Lm", rowap("smf", 0, slice(10, 11)), mx_out)
            wait("vector", "Lm", l_mx)
            vop(lambda e, o=rowap("smf", 0, slice(11, 12)), a=rowap("smf", 0, slice(10, 11)):
                e.tensor_scalar(o, a, 1e-8, None, ALU.max))
            vop(lambda e, o=rowap("smf", 0, slice(12, 13)), a=rowap("smf", 0, slice(11, 12)):
                e.tensor_scalar(o, a, 1.0 / 127.0, None, ALU.mult))
            vop(lambda e, o=rowap("smf", 0, slice(13, 14)), a=rowap("smf", 0, slice(12, 13)):
                e.reciprocal(o, a))
            wait("vector", "V", v_al)
            vop(lambda e, o=rowap("smf", 0, slice(14, 15)), a=rowap("smf", 0, slice(9, 10)),
                b=rowap("smf", 0, slice(12, 13)): e.tensor_tensor(o, a, b, ALU.mult))
            v_as = vop(lambda e, o=rowap("smf", 0, slice(15, 16)), a=rowap("smf", 0, slice(14, 15)):
                       e.reciprocal(o, a))
            gate("tensor", "ps_s1")
            wait("tensor", "V", v_as)
            mc3 = mm(tn["mask_ut"][0:1, :], rowap("smf", 0, slice(13, 15)), tn["ps_s1"][:, 0:2], True, True)
            wait("vector", "M", mc3)
            v_bcs = vop(lambda e, o=ap("bcs"), a=tn["ps_s1"][:, 0:2]: e.tensor_copy(o, a))
            setfree("ps_s1", ("V", v_bcs))
            l_bo = dma("sync", "MS", rowap("xnf0", 0, slice(0, 1024)), bo_h)
            wait("vector", "MS", l_bo)
            vop(lambda e, o=rowap("xnf1", 0, slice(0, 1024)), a=rowap("xnf0", 0, slice(0, 1024)),
                s=rowap("smf", 0, slice(15, 16)): e.tensor_scalar(o, a, s, None, ALU.mult))
            vop(lambda e, o=tn["boeff"][0:1, :], a=rowap("xnf1", 0, slice(0, 1024)):
                e.tensor_copy(o, a))
            vop(lambda e, o=rowap("xnf0", 0, slice(0, 1024)), a=tn["boeff"][0:1, :]:
                e.tensor_copy(o, a))
            v_bo = vop(lambda e, o=tn["bvb"][0:1, :], a=rowap("xnf1", 0, slice(0, 1024)),
                       b=rowap("xnf0", 0, slice(0, 1024)): e.tensor_tensor(o, a, b, ALU.subtract))
            wait("sync", "M", mid[("xnT_free", NGRUN - 1)])
            wait("sync", "C", ag_done)
            wait("gpsimd", "C", ag_done)
            wait("gpsimd", "M", mid[("xnT_free", NGRUN - 1)])
            gate("sync", "qT")
            wait("vector", "Gx0", cnt["Gx0"])
            wait("vector", "Gx1", cnt["Gx1"])
            v_kv = 0
            for dkb in range(8):
                if dkb > 0:
                    wait("sync", "V", v_kv)
                for j in range(4):
                    dma("sync", "PS", tn["qT"][:, j * 1024:(j + 1) * 1024],
                        cc_out[j * 1025 + dkb * 128: j * 1025 + (dkb + 1) * 128, 0:1024])
                l_g = cnt["PS"]
                wait("vector", "PS", l_g)
                vop(lambda e, o=ap("u"), a=tn["qT"][:, 0:1024], m=ap("mselc", slice(0, 1)):
                    e.tensor_scalar(o, a, m, None, ALU.mult))
                vop(lambda e, o=ap("xnf0"), a=tn["qT"][:, 1024:2048], m=ap("mselc", slice(1, 2)), c=ap("u"):
                    e.scalar_tensor_tensor(o, a, m, c, ALU.mult, ALU.add))
                vop(lambda e, o=ap("u"), a=tn["qT"][:, 2048:3072], m=ap("mselc", slice(2, 3)), c=ap("xnf0"):
                    e.scalar_tensor_tensor(o, a, m, c, ALU.mult, ALU.add))
                v_kv = vop(lambda e, o=tn["kvr"][:, dkb * 1024:(dkb + 1) * 1024],
                           a=tn["qT"][:, 3072:4096], m=ap("mselc", slice(3, 4)), c=ap("u"):
                           e.scalar_tensor_tensor(o, a, m, c, ALU.mult, ALU.add))
                wait("gpsimd", "V", v_kv)
                dma("gpsimd", "Gk", tn["w3"][:, dkb * 1024:(dkb + 1) * 1024],
                    tn["kvr"][:, dkb * 1024:(dkb + 1) * 1024])
            kvb_G = cnt["Gk"]
            setfree("qT", ("V", cnt["V"]))
            gate("sync", "kT")
            wait("sync", "V", v_kv)
            for j in range(4):
                dma("sync", "PS", tn["kT"][0:1, j * 1024:(j + 1) * 1024],
                    cc_out[j * 1025 + 1024: j * 1025 + 1025, 0:1024])
            ksr_L = cnt["PS"]
            gate("tensor", "ps_s1")
            wait("tensor", "PS", ksr_L)
            for j in range(4):
                for dkb in range(8):
                    col = j * 8 + dkb
                    mc_k = mm(tn["kT"][0:1, j * 1024 + dkb * 128: j * 1024 + dkb * 128 + 128],
                              tn["ones_cb"][0:1, 0:1],
                              tn["ps_s1"][:, col:col + 1], True, True)
            setfree("kT", ("M", mc_k))
            wait("vector", "M", mc_k)
            v_ksl = vop(lambda e, o=ap("tc32"), a=tn["ps_s1"][:, 0:32]: e.tensor_copy(o, a))
            setfree("ps_s1", ("V", v_ksl))
            vop(lambda e, o=ap("ksB"), a=ap("tc32", slice(0, 8)), m=ap("mselc", slice(0, 1)):
                e.tensor_scalar(o, a, m, None, ALU.mult))
            vop(lambda e, o=ap("ksA"), a=ap("tc32", slice(8, 16)), m=ap("mselc", slice(1, 2)), c=ap("ksB"):
                e.scalar_tensor_tensor(o, a, m, c, ALU.mult, ALU.add))
            vop(lambda e, o=ap("ksB"), a=ap("tc32", slice(16, 24)), m=ap("mselc", slice(2, 3)), c=ap("ksA"):
                e.scalar_tensor_tensor(o, a, m, c, ALU.mult, ALU.add))
            vop(lambda e, o=ap("ksA"), a=ap("tc32", slice(24, 32)), m=ap("mselc", slice(3, 4)), c=ap("ksB"):
                e.scalar_tensor_tensor(o, a, m, c, ALU.mult, ALU.add))
            v_ksc = vop(lambda e, o=ap("kscb"), a=ap("ksA"): e.tensor_copy(o, a))

            bnd = dict(cnt)
            bnd_free = {
                "kT": ["M", "KT", "V"], "ktok": ["M"], "vtok": ["M", "Qv"],
                "aT0": ["Qa0"], "aT1": ["Qa1"], "xnT0": ["M"], "xnT1": ["M"],
                "xnf0": ["V", "Gx0"], "xnf1": ["V", "Gx1"],
                "xnb0": ["WP0", "Qw0"], "xnb1": ["WP1", "Qw1"],
                "x2t0": ["Qx0"], "x2t1": ["Qx1"],
                "xg0": ["S", "V"], "xg1": ["S", "V"], "xg2": ["S", "V"],
                "rT0": ["Qs0", "Gc"], "rT1": ["Qs1"],
                "rT0_s": ["Qs0", "Gc"], "rT1_s": ["Qs1"],
                "mn0": ["S"], "mn1": ["S"], "Rb0": ["V"], "Rb1": ["V"],
            }
            for key, ss in bnd_free.items():
                free[key] = {s: bnd[s] for s in ss}

        denr = rowap("u", 0, slice(0, 512))
        recr = rowap("u", 0, slice(512, 1024))
        inv_s_ap = ap("bcs", slice(0, 1))
        as_ap = ap("bcs", slice(1, 2))
        invas_ap = rowap("smf", 0, slice(15, 16))
        last_yq_M = 0
        attnT_v = {}
        for g in (range(NG) if STAGE >= 4 else []):
            pb2 = (g + 1) % 2
            qb = ["qT", "kT"][pb2]
            vb = ["ktok", "vtok"][pb2]
            ab = ["aT0", "aT1"][pb2]
            at = f"xnT{g % 2}"
            if g >= 1:
                for u16 in range(16):
                    dkb, eh = divmod(u16, 2)
                    sg = f"Rb{u16 % 2}"
                    gate("sync", sg)
                    ldS = dma("sync", f"Sg{u16 % 2}", tn[sg][:, 0:512],
                              sp_s[(g - 1) * 1024 + dkb * 128: (g - 1) * 1024 + (dkb + 1) * 128,
                                   eh * 512:(eh + 1) * 512])
                    wait("vector", f"Sg{u16 % 2}", ldS)
                    v_ad = vop(lambda e, o=tn["kvr"][:, dkb * 1024 + eh * 512: dkb * 1024 + (eh + 1) * 512],
                               a=tn[sg][:, 0:512]: e.tensor_tensor(o, o, a, ALU.add))
                    setfree(sg, ("V", v_ad))
                kv_adds_v = cnt["V"]
                src, dst_ = ("ksA", "ksB") if g % 2 == 1 else ("ksB", "ksA")
                vop(lambda e, o=ap(dst_), a=ap(src), b=ap("ksg", slice((g - 1) * 8, g * 8)):
                    e.tensor_tensor(o, a, b, ALU.add))
                gate("vector", "kscb")
                v_ksc = vop(lambda e, o=ap("kscb"), a=ap(dst_): e.tensor_copy(o, a))
                wait("gpsimd", "V", kv_adds_v)
                wait("gpsimd", "M", last_yq_M)
                for dkb in range(8):
                    dma("gpsimd", "Gk", tn["w3"][:, dkb * 1024:(dkb + 1) * 1024],
                        tn["kvr"][:, dkb * 1024:(dkb + 1) * 1024])
                kvb_G = cnt["Gk"]
            ldsem = f"Ld{pb2}"
            gate("sync", qb)
            l_q = dma("sync", ldsem, ap(qb), sp_q[g * 128:(g + 1) * 128, :])
            gate("sync", vb)
            l_v = dma("sync", ldsem, ap(vb), sp_v[g * 128:(g + 1) * 128, :])
            gate("sync", ab)
            l_a = dma("sync", ldsem, ap(ab), sp_a[g * 128:(g + 1) * 128, :])
            gate("tensor", "ps_a")
            wait("tensor", ldsem, l_a)
            wait("tensor", "V", v_ksc)
            for sb in range(4):
                mm(tn["ones_cb"][:, 0:1], tn[ab][:, PK[sb]:PK[sb] + PW[sb]],
                   tn["ps_a"][0:1, sb * 128:512], start=(sb == 0), stop=False)
            for dc in range(8):
                mmc = mm(ap("kscb", slice(dc, dc + 1)), tn[qb][:, dc * 512:(dc + 1) * 512],
                         tn["ps_a"][0:1, 0:512], start=False, stop=(dc == 7))
            setfree("kscb", ("M", mmc))
            wait("vector", "M", mmc)
            vop(lambda e, o=denr, a=tn["ps_a"][0:1, 0:512]:
                e.tensor_scalar(o, a, EPS_ATT, None, ALU.add))
            vop(lambda e, o=recr, a=denr: e.reciprocal(o, a))
            gate("vector", "rrowb")
            gate("vector", "sgb")
            vop(lambda e, o=denr, a=recr, s=invas_ap:
                e.tensor_scalar(o, a, s, None, ALU.mult))
            vop(lambda e, o=tn["rrowb"][0:1, :], a=denr: e.tensor_copy(o, a))
            vop(lambda e, o=recr, a=tn["rrowb"][0:1, :]: e.tensor_copy(o, a))
            v_rr = vop(lambda e, o=tn["sgb"][0:1, 0:512], a=denr, b=recr:
                       e.tensor_tensor(o, a, b, ALU.subtract))
            setfree("ps_a", ("V", cnt["V"]))
            gate("tensor", "ps_s0")
            wait("tensor", "V", v_rr)
            mm(tn["ones_rb"][0:1, :], tn["rrowb"][0:1, :], ap("ps_s0"), True, False)
            mcb = mm(tn["ones_rb"][0:1, :], tn["sgb"][0:1, 0:512], ap("ps_s0"), False, True)
            setfree("rrowb", ("M", mcb))
            setfree("sgb", ("M", mcb))
            wait("vector", "M", mcb)
            gate("vector", "mn0")
            v_bc = vop(lambda e, o=ap("mn0"), a=ap("ps_s0"): e.tensor_copy(o, a))
            setfree("ps_s0", ("V", v_bc))
            for eb in range(8):
                pq = f"ps_q{eb % 2}"
                gate("tensor", pq)
                if eb == 0:
                    wait("tensor", "Gk", kvb_G)
                for dc in range(8):
                    mm(tn["w3"][:, dc * 1024 + eb * 128: dc * 1024 + (eb + 1) * 128],
                       tn[qb][:, dc * 512:(dc + 1) * 512],
                       ap(pq), start=(dc == 0), stop=False)
                for sb in range(4):
                    mmc = mm(tn[vb][:, sb * 1024 + eb * 128: sb * 1024 + (eb + 1) * 128],
                             tn[ab][:, PK[sb]:PK[sb] + PW[sb]],
                             tn[pq][:, sb * 128:512], start=False, stop=(sb == 3))
                wait("vector", "M", mmc)
                if eb == 0:
                    gate("vector", at)
                v_at = vop(lambda e, o=tn[at][:, eb * 512:(eb + 1) * 512], p=ap(pq), b=ap("mn0"):
                           e.tensor_tensor(o, p, b, ALU.mult))
                setfree(pq, ("V", v_at))
                attnT_v[(g, eb)] = v_at
            last_yq_M = cnt["M"]
            setfree(qb, ("M", cnt["M"]))
            setfree(vb, ("M", cnt["M"]))
            setfree(ab, ("M", cnt["M"]))
            for i in range(4):
                ti = g * 4 + i
                x2 = f"x2t{i % 2}"
                rb = f"xnb{i % 2}"
                rt = f"rT{i % 2}"
                slot = f"xg{ti % 3}"
                gate("sync", x2)
                l_x2 = dma("sync", f"X2{i % 2}", ap(x2), sp_x2[ti * 128:(ti + 1) * 128, :])
                gate("vector", rb)
                wait("vector", f"X2{i % 2}", l_x2)
                for h in range(2):
                    v_t1 = vop(lambda e, o=ap("i32h"), a=tn[x2][:, h * 512:(h + 1) * 512], s=inv_s_ap:
                               e.tensor_scalar(o, a, s, None, ALU.mult))
                    v_rb = vop(lambda e, o=tn[rb][:, h * 512:(h + 1) * 512], a=ap("i32h"):
                               e.tensor_copy(o, a))
                setfree(x2, ("V", v_rb))
                gate("scalar", rt)
                wait("scalar", "V", v_rb)
                dest = tn[rt].ap().rearrange("p (d t) -> p d t", d=8)[:, :, 0:128]
                q_rt = dmat("scalar", f"Qr{i % 2}", dest, ap(rb))
                setfree(rb, (f"Qr{i % 2}", q_rt))
                gate("sync", slot)
                l_x = dma("sync", f"Lx{ti % 3}", ap(slot), x_h[ti * 128:(ti + 1) * 128, :])
                for oh in range(2):
                    idx = (2 * i + oh) % 2
                    pv = f"ps_v{idx}"
                    gate("tensor", pv)
                    if i == 0 and oh == 0:
                        for eb in range(8):
                            wait("tensor", "V", attnT_v[(g, eb)])
                        wait("tensor", "Qw0", w2_Q[0])
                        wait("tensor", "Qw1", w2_Q[1])
                        wait("tensor", "WP0", wo_WP[0])
                        wait("tensor", "WP1", wo_WP[1])
                        wait("tensor", "V", v_bo)
                    if oh == 0:
                        wait("tensor", f"Qr{i % 2}", q_rt)
                    for ac in range(8):
                        mm(tn[at][:, ac * 512 + i * 128: ac * 512 + (i + 1) * 128],
                           tn["w1"][:, ac * 1024 + oh * 512: ac * 1024 + (oh + 1) * 512],
                           ap(pv), start=(ac == 0), stop=False)
                    for ic in range(8):
                        mm(tn[rt][:, ic * 128:(ic + 1) * 128],
                           tn["w2"][:, ic * 1024 + oh * 512: ic * 1024 + (oh + 1) * 512],
                           ap(pv), start=False, stop=False)
                    mm(tn["ones_rb"][0:1, :], tn["boeff"][0:1, oh * 512:(oh + 1) * 512],
                       ap(pv), start=False, stop=False)
                    mmc = mm(tn["ones_rb"][0:1, :], tn["bvb"][0:1, oh * 512:(oh + 1) * 512],
                             ap(pv), start=False, stop=True)
                    if oh == 0:
                        gate("vector", f"xnf{i % 2}")
                    wait("vector", "M", mmc)
                    wait("vector", f"Lx{ti % 3}", l_x)
                    v_out = vop(lambda e, o=tn[f"xnf{i % 2}"][:, oh * 512:(oh + 1) * 512],
                                p=ap(pv), s=as_ap, x=tn[slot][:, oh * 512:(oh + 1) * 512]:
                                e.scalar_tensor_tensor(o, p, s, x, ALU.mult, ALU.add))
                    setfree(pv, ("V", v_out))
                setfree(slot, ("V", v_out))
                wait("scalar", "V", v_out)
                q_o = dma("scalar", f"Qo{i % 2}", out_h[ti * 128:(ti + 1) * 128, :], ap(f"xnf{i % 2}"))
                setfree(f"xnf{i % 2}", (f"Qo{i % 2}", q_o))
                setfree(rt, ("M", cnt["M"]))
            setfree(at, ("M", cnt["M"]))

        wait("gpsimd", "Qo0", cnt["Qo0"])
        wait("gpsimd", "Qo1", cnt["Qo1"])

        @block.sync
        def _(eng):
            for f in plan["sync"]:
                f(eng)

        @block.scalar
        def _(eng):
            for f in plan["scalar"]:
                f(eng)

        @block.vector
        def _(eng):
            for f in plan["vector"]:
                f(eng)

        @block.tensor
        def _(eng):
            for f in plan["tensor"]:
                f(eng)

        @block.gpsimd
        def _(eng):
            for f in plan["gpsimd"]:
                f(eng)

    return nc


_NC_CACHE = None


def kernel(L0_attn_chunk, L0_attn_Wq, L0_attn_bq, L0_attn_Wk, L0_attn_bk,
           L0_attn_Wv, L0_attn_bv, L0_attn_Wo, L0_attn_bo,
           L0_ffn_weight, L0_ffn_gamma, L0_ffn_beta):
    global _NC_CACHE
    if _NC_CACHE is None:
        _NC_CACHE = build()
    nc = _NC_CACHE
    c = np.ascontiguousarray

    x = np.asarray(L0_attn_chunk, dtype=np.float32)
    wq = c(np.asarray(L0_attn_Wq, np.float32))
    wk = c(np.asarray(L0_attn_Wk, np.float32))
    wv = c(np.asarray(L0_attn_Wv, np.float32))
    wo = c(np.asarray(L0_attn_Wo, np.float32))
    wf = c(np.asarray(L0_ffn_weight, np.float32))
    bqc = c(np.asarray(L0_attn_bq, np.float32).reshape(8, 128).T)
    bkc = c(np.asarray(L0_attn_bk, np.float32).reshape(8, 128).T)
    bvr = c(np.asarray(L0_attn_bv, np.float32).reshape(1, D))
    bor = c(np.asarray(L0_attn_bo, np.float32).reshape(1, D))
    gamb = c(np.broadcast_to(np.asarray(L0_ffn_gamma, np.float32).reshape(1, D), (128, D)))
    betb = c(np.broadcast_to(np.asarray(L0_ffn_beta, np.float32).reshape(1, D), (128, D)))

    identc = np.eye(128, dtype=np.float32)
    mtric = c(np.triu(np.ones((128, 128), np.float32)))
    in_maps = []
    for core in range(NCORE):
        b, q = core // 4, core % 4
        msel = np.zeros((128, 4), np.float32)
        msel[:, :q] = 1.0
        in_maps.append({
            "x": c(x[b, q * TC:(q + 1) * TC, :]),
            "wq": wq, "wk": wk, "wv": wv, "wo": wo, "wf": wf,
            "bqc": bqc, "bkc": bkc, "bvr": bvr, "bor": bor,
            "gamb": gamb, "betb": betb, "msel": msel,
            "identc": identc, "mtric": mtric,
        })
    res = run_bass_kernel_spmd(nc, in_maps, core_ids=list(range(NCORE)))
    out = np.empty((B, T, D), np.float32)
    for core in range(NCORE):
        b, q = core // 4, core % 4
        out[b, q * TC:(q + 1) * TC, :] = res.results[core]["out"]
    return out


# version 06
# speedup vs baseline: 1.0118x; 1.0118x over previous
import numpy as np
import concourse.bass as bass
import concourse.mybir as mybir
from concourse.bass_utils import run_bass_kernel_spmd

F32 = mybir.dt.float32
BF16 = mybir.dt.bfloat16
I32 = mybir.dt.int32
ALU = mybir.AluOpType
ACT = mybir.ActivationFunctionType
AX = mybir.AxisListType.X

STAGE = 4
NGRUN = 8
P1PART = 6
S_SUB = 3
TERN_ON = True
NCORE = 8
B, T, D = 2, 16384, 1024
TC = T // 4
NT = TC // 128
NG = 8
PK = [0, 512, 896, 1152]
PW = [512, 384, 256, 128]
EPS1 = 1e-5
EPS_ATT = 1e-6


def build():
    nc = bass.Bass("TRN2", target_bir_lowering=False)

    x_h = nc.dram_tensor("x", [TC, D], F32, kind="ExternalInput").ap()
    wq_h = nc.dram_tensor("wq", [D, D], F32, kind="ExternalInput").ap()
    wk_h = nc.dram_tensor("wk", [D, D], F32, kind="ExternalInput").ap()
    wv_h = nc.dram_tensor("wv", [D, D], F32, kind="ExternalInput").ap()
    wo_h = nc.dram_tensor("wo", [D, D], F32, kind="ExternalInput").ap()
    wf_h = nc.dram_tensor("wf", [D, D], F32, kind="ExternalInput").ap()
    bqc_h = nc.dram_tensor("bqc", [128, 8], F32, kind="ExternalInput").ap()
    bkc_h = nc.dram_tensor("bkc", [128, 8], F32, kind="ExternalInput").ap()
    bv_h = nc.dram_tensor("bvr", [1, D], F32, kind="ExternalInput").ap()
    bo_h = nc.dram_tensor("bor", [1, D], F32, kind="ExternalInput").ap()
    gam_h = nc.dram_tensor("gamb", [128, D], F32, kind="ExternalInput").ap()
    bet_h = nc.dram_tensor("betb", [128, D], F32, kind="ExternalInput").ap()
    msel_h = nc.dram_tensor("msel", [128, 4], F32, kind="ExternalInput").ap()
    identh = nc.dram_tensor("identc", [128, 128], F32, kind="ExternalInput").ap()
    mtrih = nc.dram_tensor("mtric", [128, 128], F32, kind="ExternalInput").ap()
    out_h = nc.dram_tensor("out", [TC, D], F32, kind="ExternalOutput").ap()

    sp_q = nc.dram_tensor("sp_q", [NG * 128, 4096], BF16).ap()
    sp_v = nc.dram_tensor("sp_v", [NG * 128, 4096], BF16).ap()
    sp_a = nc.dram_tensor("sp_a", [NG * 128, 1280], BF16).ap()
    sp_x2 = nc.dram_tensor("sp_x2", [TC, D], F32).ap()
    sp_s = nc.dram_tensor("sp_s", [NG * 1024, 1024], BF16).ap()
    sp_t = nc.dram_tensor("sp_t", [1024, 1024], BF16).ap()
    cc_in_t = nc.dram_tensor("cc_in", [1025, 1024], BF16)
    cc_out_t = nc.dram_tensor("cc_out", [4 * 1025, 1024], BF16)
    mx_in_t = nc.dram_tensor("mx_in", [1, 1], F32)
    mx_out_t = nc.dram_tensor("mx_out", [1, 1], F32)
    cc_in, cc_out = cc_in_t.ap(), cc_out_t.ap()
    mx_in, mx_out = mx_in_t.ap(), mx_out_t.ap()

    SB = [
        ("w1", 8192, BF16), ("w2", 8192, BF16), ("w3", 8192, BF16),
        ("kvr", 8192, F32), ("gbc", 1024, F32), ("bbc", 1024, F32),
        ("xg0", 1024, F32), ("xg1", 1024, F32), ("xg2", 1024, F32),
        ("xnf0", 1024, F32), ("xnf1", 1024, F32),
        ("xnb0", 1024, BF16), ("xnb1", 1024, BF16),
        ("xnT0", 4096, BF16), ("xnT1", 4096, BF16),
        ("qT", 4096, BF16), ("kT", 4096, BF16),
        ("ktok", 4096, BF16), ("vtok", 4096, BF16),
        ("aT0", 1280, BF16), ("aT1", 1280, BF16),
        ("mn0", 512, F32), ("mn1", 512, F32),
        ("Rb0", 512, BF16), ("Rb1", 512, BF16), ("sgb", 512, BF16),
        ("u", 1024, F32), ("i32h", 512, I32),
        ("x2t0", 1024, F32), ("x2t1", 1024, F32),
        ("rT0", 1024, BF16), ("rT1", 1024, BF16),
        ("bqc", 8, F32), ("bkc", 8, F32), ("bvb", 1024, BF16), ("boeff", 1024, BF16),
        ("mselc", 4, F32), ("stats", 96, F32), ("scr0", 8, F32), ("scr1", 8, F32),
        ("ksg", 64, F32), ("kst", 8, F32), ("ksA", 8, F32), ("ksB", 8, F32),
        ("kscb", 8, BF16), ("amx0", 1, F32), ("amx1", 1, F32),
        ("ident", 128, F32), ("mask_ut", 128, F32),
        ("ones_cb", 1, BF16), ("ones_rb", 128, BF16),
        ("rrowb", 512, BF16), ("smf", 16, F32),
        ("tw32", 32, F32), ("tn32", 32, F32), ("tc32", 32, F32),
        ("bcs", 2, F32), ("dcol", 1, F32),
        ("ta", 256, F32), ("tb2", 256, F32), ("tc2", 256, F32), ("td", 256, F32),
        ("tf", 256, BF16),
    ]
    used = sum(w * {F32: 4, BF16: 2, I32: 4}[d] for _, w, d in SB)
    assert used <= 207000, f"SBUF over budget: {used}"

    sems = ["M", "V", "S", "C",
            "MS", "Gc", "Lx0", "Lx1", "Lx2", "Gx0", "Gx1", "WP0", "WP1", "Ta", "Qt",
            "TT00", "TT01", "TT10", "TT11", "Qx0", "Qx1", "Qq", "Qv", "KT",
            "Qa0", "Qa1", "Qs0", "Qs1", "Qm", "Lt0", "Lt1", "Qw0", "Qw1",
            "Lm", "PS", "Gk",
            "Sg0", "Sg1", "Ld0", "Ld1", "X20", "X21", "Qr0", "Qr1", "Qo0", "Qo1"]
    cnt = {s: 0 for s in sems}
    plan = {"sync": [], "scalar": [], "vector": [], "tensor": [], "gpsimd": []}
    tn = {}
    free = {}

    def ap(name, sl=None):
        t = tn[name]
        return t[:, :] if sl is None else t[:, sl]

    def rowap(name, r, sl):
        return tn[name][r:r + 1, sl]

    def emit(eng, fn):
        plan[eng].append(fn)

    def wait(eng, sem, val):
        if val > 0:
            emit(eng, lambda e, s=sem, v=val: e.wait_ge(tn["sem_" + s], v))

    def gate(eng, key):
        for s, v in free.pop(key, {}).items():
            wait(eng, s, v)

    def setfree(key, *pairs):
        d = free.setdefault(key, {})
        for s, v in pairs:
            d[s] = max(d.get(s, 0), v)

    def dma(eng, sem, out, in_):
        emit(eng, lambda e, o=out, i=in_, s=sem: e.dma_start(out=o, in_=i).then_inc(tn["sem_" + s], 16))
        cnt[sem] += 16
        return cnt[sem]

    def dmat(eng, sem, out, in_):
        emit(eng, lambda e, o=out, i=in_, s=sem: e.dma_start_transpose(o, i).then_inc(tn["sem_" + s], 16))
        cnt[sem] += 16
        return cnt[sem]

    def mm(lhsT, rhs, out, start, stop):
        if stop:
            emit("tensor", lambda e, a=lhsT, b=rhs, o=out, st=start:
                 e.matmul(o, a, b, start=st, stop=True).then_inc(tn["sem_M"], 1))
            cnt["M"] += 1
        else:
            emit("tensor", lambda e, a=lhsT, b=rhs, o=out, st=start:
                 e.matmul(o, a, b, start=st, stop=False))
        return cnt["M"]

    def vop(fn):
        wait("vector", "V", cnt["V"])
        emit("vector", lambda e, f=fn: f(e).then_inc(tn["sem_V"], 1))
        cnt["V"] += 1
        return cnt["V"]

    def sop(fn):
        wait("scalar", "S", cnt["S"])
        emit("scalar", lambda e, f=fn: f(e).then_inc(tn["sem_S"], 1))
        cnt["S"] += 1
        return cnt["S"]

    def gop(fn):
        wait("gpsimd", "C", cnt["C"])
        emit("gpsimd", lambda e, f=fn: f(e).then_inc(tn["sem_C"], 1))
        cnt["C"] += 1
        return cnt["C"]

    import contextlib
    with contextlib.ExitStack() as stack:
        for name, w, dt in SB:
            tn[name] = stack.enter_context(nc.sbuf_tensor("sb_" + name, [128, w], dt))
        for pname in ["ps_q0", "ps_q1", "ps_v0", "ps_v1", "ps_a", "ps_s0", "ps_s1"]:
            tn[pname] = stack.enter_context(nc.psum_tensor(pname, [128, 512], F32))
        for s in sems:
            tn["sem_" + s] = stack.enter_context(nc.semaphore("sem" + s))
        block = stack.enter_context(nc.Block())

        dma("sync", "MS", ap("bqc"), bqc_h)
        dma("sync", "MS", ap("bkc"), bkc_h)
        dma("sync", "MS", ap("gbc"), gam_h)
        dma("sync", "MS", ap("bbc"), bet_h)
        dma("sync", "MS", ap("mselc"), msel_h)
        dma("sync", "MS", ap("ident"), identh)
        dma("sync", "MS", ap("mask_ut"), mtrih)
        lp = cnt["MS"]
        dma("gpsimd", "Gc", tn["bvb"][0:1, :], bv_h)

        for i in range(3):
            dma("sync", f"Lx{i}", ap(f"xg{i}"), x_h[i * 128:(i + 1) * 128, :])
        wait("scalar", "MS", lp)
        vop(lambda e: e.memset(ap("ones_cb"), 1.0))
        vop(lambda e: e.memset(tn["ones_rb"][0:2, :], 1.0))
        vop(lambda e: e.memset(ap("kvr"), 0.0))
        vop(lambda e: e.memset(ap("amx0"), 0.0))
        vop(lambda e: e.memset(ap("ksA"), 0.0))
        wait("vector", "MS", lp)

        for wi, (wh, wt) in enumerate([(wq_h, "w1"), (wk_h, "w2"), (wv_h, "w3")]):
            for ob in range(8):
                par = (wi * 8 + ob) % 2
                st = f"xnb{par}"
                gate("gpsimd", st)
                gc = dma("gpsimd", f"Gx{par}", ap(st), wh[ob * 128:(ob + 1) * 128, :])
                wait("sync", f"Gx{par}", gc)
                dest = tn[wt].ap().rearrange("p (d o) -> p d o", d=8)[:, :, ob * 128:(ob + 1) * 128]
                lc = dmat("sync", f"WP{par}", dest, ap(st))
                setfree(st, (f"WP{par}", lc))
        wp_L = (cnt["WP0"], cnt["WP1"])

        mid = {}

        def tern_pass1(j):
            ob, sub = divmod(j, 4)
            gate("sync", "ta")
            ld = dma("sync", "Ta", ap("ta"), wf_h[ob * 128:(ob + 1) * 128, sub * 256:(sub + 1) * 256])
            gate("scalar", "tb2")
            wait("scalar", "Ta", ld)
            sc = sop(lambda e, o=ap("tb2"), a=ap("ta"), acc=ap("tw32", slice(j, j + 1)):
                     e.activation(o, a, ACT.Abs, accum_out=acc))
            setfree("ta", ("S", sc))
            setfree("tb2", ("S", sc))
            mid["tp1_s"] = sc

        def tern_delta():
            gate("vector", "scr0")
            wait("vector", "S", mid["tp1_s"])
            v1 = vop(lambda e, o=ap("scr0", slice(0, 1)), a=ap("tw32"): e.reduce_sum(o, a, AX))
            gate("tensor", "ps_a")
            wait("tensor", "V", v1)
            wait("tensor", "MS", lp)
            mc = mm(ap("scr0", slice(0, 1)), ap("ident"), tn["ps_a"][0:1, 0:128], True, True)
            setfree("scr0", ("M", mc))
            wait("vector", "M", mc)
            vop(lambda e, o=rowap("td", 0, slice(0, 128)), a=tn["ps_a"][0:1, 0:128]:
                e.tensor_copy(o, a))
            vop(lambda e, o=rowap("smf", 0, slice(2, 3)), a=rowap("td", 0, slice(0, 128)):
                e.reduce_sum(o, a, AX))
            v2 = vop(lambda e, o=rowap("smf", 0, slice(3, 4)), a=rowap("smf", 0, slice(2, 3)):
                     e.tensor_scalar(o, a, 0.7 / (1024.0 * 1024.0), None, ALU.mult))
            setfree("ps_a", ("V", cnt["V"]))
            gate("tensor", "ps_a")
            wait("tensor", "V", v2)
            mc2 = mm(tn["mask_ut"][0:1, :], rowap("smf", 0, slice(3, 4)), tn["ps_a"][:, 0:1], True, True)
            wait("vector", "M", mc2)
            v3 = vop(lambda e, o=ap("dcol"), a=tn["ps_a"][:, 0:1]: e.tensor_copy(o, a))
            setfree("ps_a", ("V", v3))
            mid["dcol_v"] = v3

        def tern_pass2(j):
            ob, sub = divmod(j, 4)
            gate("sync", "ta")
            ld = dma("sync", "Ta", ap("ta"), wf_h[ob * 128:(ob + 1) * 128, sub * 256:(sub + 1) * 256])
            gate("scalar", "tb2")
            wait("scalar", "Ta", ld)
            s_ab = sop(lambda e, o=ap("tb2"), a=ap("ta"): e.activation(o, a, ACT.Abs))
            wait("vector", "S", s_ab)
            wait("vector", "V", mid["dcol_v"])
            vop(lambda e, o=ap("tc2"), a=ap("tb2"), d=ap("dcol"):
                e.tensor_scalar(o, a, d, None, ALU.is_gt))
            vop(lambda e, o=ap("tc32", slice(j, j + 1)), a=ap("tc2"): e.reduce_sum(o, a, AX))
            v_d = vop(lambda e, o=ap("td"), a=ap("tb2"), b=ap("tc2"):
                      e.tensor_tensor(o, a, b, ALU.mult))
            vop(lambda e, o=ap("tn32", slice(j, j + 1)), a=ap("td"): e.reduce_sum(o, a, AX))
            wait("scalar", "V", v_d)
            s_sg = sop(lambda e, o=ap("tb2"), a=ap("ta"): e.activation(o, a, ACT.Sign))
            setfree("ta", ("S", s_sg))
            gate("vector", "tf")
            wait("vector", "S", s_sg)
            v_t = vop(lambda e, o=ap("tf"), a=ap("tb2"), b=ap("tc2"):
                      e.tensor_tensor(o, a, b, ALU.mult))
            setfree("tb2", ("V", v_t))
            wait("scalar", "V", v_t)
            q_t = dma("scalar", "Qt", sp_t[ob * 128:(ob + 1) * 128, sub * 256:(sub + 1) * 256], ap("tf"))
            setfree("tf", ("Qt", q_t))

        qk_last_v = {}
        for g in range(NGRUN):
            gp = g % 2
            xnT = f"xnT{gp}"
            for i in range(4):
                ti = g * 4 + i
                slot = f"xg{ti % 3}"
                tp = f"xnf{i % 2}"
                tb = f"xnb{i % 2}"
                x2 = f"x2t{i % 2}"
                scr = f"scr{i % 2}"
                if ti >= 3:
                    gate("sync", slot)
                    ld = dma("sync", f"Lx{ti % 3}", ap(slot), x_h[ti * 128:(ti + 1) * 128, :])
                else:
                    ld = 16
                gate("scalar", tp)
                gate("scalar", scr)
                wait("scalar", f"Lx{ti % 3}", ld)
                sop(lambda e, o=ap(tp), x=ap(slot), a=ap(scr, slice(0, 1)):
                    e.activation(o, x, ACT.Identity, accum_out=a))
                s_st = sop(lambda e, o=ap(tp), x=ap(slot), a=ap(scr, slice(1, 2)):
                           e.activation(o, x, ACT.Square, accum_out=a))
                stc = ti * 3
                mneg = ap("stats", slice(stc, stc + 1))
                r1c = ap("stats", slice(stc + 1, stc + 2))
                r2c = ap("stats", slice(stc + 2, stc + 3))
                wait("vector", "S", s_st)
                vop(lambda e, o=mneg, a=ap(scr, slice(0, 1)):
                    e.tensor_scalar(o, a, -1.0 / 1024.0, None, ALU.mult))
                vop(lambda e, o=ap(scr, slice(2, 3)), a=ap(scr, slice(1, 2)):
                    e.tensor_scalar(o, a, 1.0 / 1024.0, None, ALU.mult))
                vop(lambda e, o=ap(scr, slice(3, 4)), a=mneg:
                    e.tensor_tensor(o, a, a, ALU.mult))
                v_var = vop(lambda e, o=ap(scr, slice(4, 5)), a=ap(scr, slice(2, 3)), b=ap(scr, slice(3, 4)):
                            e.tensor_tensor(o, a, b, ALU.subtract))
                vop(lambda e, o=ap(scr, slice(6, 7)), a=ap(scr, slice(4, 5)):
                    e.tensor_scalar(o, a, EPS1, None, ALU.add))
                wait("scalar", "V", cnt["V"])
                s_q1 = sop(lambda e, o=ap(scr, slice(5, 6)), a=ap(scr, slice(6, 7)):
                           e.activation(o, a, ACT.Sqrt))
                wait("vector", "S", s_q1)
                vop(lambda e, o=r1c, a=ap(scr, slice(5, 6)): e.reciprocal(o, a))
                vop(lambda e, o=ap(scr, slice(5, 6)), a=ap(scr, slice(6, 7)):
                    e.reciprocal(o, a))
                vop(lambda e, o=ap(scr, slice(7, 8)), a=ap(scr, slice(4, 5)), b=ap(scr, slice(5, 6)):
                    e.tensor_tensor(o, a, b, ALU.mult))
                v_v2 = vop(lambda e, o=ap(scr, slice(6, 7)), a=ap(scr, slice(7, 8)):
                           e.tensor_scalar(o, a, EPS1, None, ALU.add))
                wait("scalar", "V", v_v2)
                s_r2 = sop(lambda e, o=ap(scr, slice(5, 6)), a=ap(scr, slice(6, 7)):
                           e.activation(o, a, ACT.Sqrt))
                wait("vector", "S", s_r2)
                vop(lambda e, o=r2c, a=ap(scr, slice(5, 6)): e.reciprocal(o, a))
                v_xn = vop(lambda e, o=ap(tp), x=ap(slot), m=mneg, r=r1c:
                           e.tensor_scalar(o, x, m, r, ALU.add, ALU.mult))
                gate("gpsimd", tb)
                wait("gpsimd", "V", v_xn)
                g_c = dma("gpsimd", f"Gx{i % 2}", ap(tb), ap(tp))
                if i == 0:
                    gate("sync", xnT)
                wait("sync", f"Gx{i % 2}", g_c)
                dest = tn[xnT].ap().rearrange("p (d t) -> p d t", d=8)[:, :, i * 128:(i + 1) * 128]
                ttsem = f"TT{g % 2}{i % 2}"
                dmat("sync", ttsem, dest, ap(tb))
                setfree(tb, (ttsem, cnt[ttsem]))
                if i == 3:
                    mid[("tp_done", g)] = [(f"TT{g % 2}0", cnt[f"TT{g % 2}0"]),
                                           (f"TT{g % 2}1", cnt[f"TT{g % 2}1"])]
                v_u = vop(lambda e, o=ap("u"), a=ap(tp), b=ap("gbc"):
                          e.tensor_tensor(o, a, b, ALU.mult))
                gate("vector", x2)
                v_x2 = vop(lambda e, o=ap(x2), a=ap("u"), r=r2c, b=ap("bbc"):
                           e.scalar_tensor_tensor(o, a, r, b, ALU.mult, ALU.add))
                setfree(tp, (f"Gx{i % 2}", g_c), ("V", v_u))
                vop(lambda e, o=ap(scr, slice(0, 1)), a=ap(x2):
                    e.reduce_max(o, a, AX, apply_absolute_value=True))
                v_am = vop(lambda e, o=ap(f"amx{(ti + 1) % 2}"), a=ap(scr, slice(0, 1)), p=ap(f"amx{ti % 2}"):
                           e.tensor_scalar(o, a, p, None, ALU.max))
                setfree(scr, ("S", s_r2), ("V", v_am))
                wait("scalar", "V", v_x2)
                q_sp = dma("scalar", f"Qx{i % 2}", sp_x2[ti * 128:(ti + 1) * 128, :], ap(x2))
                setfree(x2, (f"Qx{i % 2}", q_sp))
                setfree(slot, ("V", v_xn), ("S", s_st))
            if TERN_ON:
                if g < NGRUN // 2 or NGRUN == 1:
                    for j in range(g * 8, g * 8 + 8):
                        tern_pass1(j)
                    if g == max(NGRUN // 2, 1) - 1:
                        tern_delta()
                else:
                    for j in range((g - NGRUN // 2) * 8, (g - NGRUN // 2) * 8 + 8):
                        tern_pass2(j)
            if P1PART >= 2:
                for proj, (wt, dst, bc_) in enumerate([("w1", "qT", "bqc"), ("w2", "kT", "bkc")]):
                    for ob in range(8):
                        pb = f"ps_q{ob % 2}"
                        gate("tensor", pb)
                        if ob == 0:
                            for sm, vv in mid[("tp_done", g)]:
                                wait("tensor", sm, vv)
                            if g == 0 and proj == 0:
                                wait("tensor", "WP0", wp_L[0])
                                wait("tensor", "WP1", wp_L[1])
                        for dc in range(8):
                            mmc = mm(tn[wt][:, dc * 1024 + ob * 128: dc * 1024 + (ob + 1) * 128],
                                     tn[xnT][:, dc * 512:(dc + 1) * 512],
                                     ap(pb), start=(dc == 0), stop=(dc == 7))
                        mnb = f"mn{ob % 2}"
                        rbb = f"Rb{ob % 2}"
                        gate("vector", mnb)
                        wait("vector", "M", mmc)
                        v_mn = vop(lambda e, o=ap(mnb), p=ap(pb), b=ap(bc_, slice(ob, ob + 1)):
                                   e.tensor_scalar(o, p, b, 0.0, ALU.add, ALU.min))
                        if ob == 0:
                            gate("scalar", dst)
                        wait("scalar", "V", v_mn)
                        s_e = sop(lambda e, o=tn[dst][:, ob * 512:(ob + 1) * 512], a=ap(mnb):
                                  e.activation(o, a, ACT.Exp))
                        setfree(mnb, ("S", s_e))
                        gate("scalar", rbb)
                        wait("scalar", "M", mmc)
                        s_r = sop(lambda e, o=ap(rbb), p=ap(pb), b=ap(bc_, slice(ob, ob + 1)):
                                  e.activation(o, p, ACT.Relu, bias=b, scale=1.0))
                        setfree(pb, ("V", v_mn), ("S", s_r))
                        wait("vector", "S", s_r)
                        v_q = vop(lambda e, o=tn[dst][:, ob * 512:(ob + 1) * 512], a=ap(rbb):
                                  e.tensor_tensor(o, o, a, ALU.add))
                        setfree(rbb, ("V", v_q))
                        qk_last_v[(g, dst)] = v_q
            if P1PART >= 3:
                for tb4 in range(4):
                    for eh in range(2):
                        idx = tb4 * 2 + eh
                        pb = f"ps_v{idx % 2}"
                        gate("tensor", pb)
                        if g == 0 and idx == 0:
                            wait("tensor", "Gc", 16)
                        for dc in range(8):
                            mm(tn[xnT][:, dc * 512 + tb4 * 128: dc * 512 + (tb4 + 1) * 128],
                               tn["w3"][:, dc * 1024 + eh * 512: dc * 1024 + (eh + 1) * 512],
                               ap(pb), start=(dc == 0), stop=False)
                        mmc = mm(tn["ones_rb"][0:1, :],
                                 tn["bvb"][0:1, eh * 512:(eh + 1) * 512],
                                 ap(pb), start=False, stop=True)
                        if idx == 0:
                            gate("scalar", "vtok")
                        wait("scalar", "M", mmc)
                        s_v = sop(lambda e, o=tn["vtok"][:, tb4 * 1024 + eh * 512: tb4 * 1024 + (eh + 1) * 512],
                                  p=ap(pb): e.copy(o, p))
                        setfree(pb, ("S", s_v))
                v_evac_s = cnt["S"]
                mid[("xnT_free", g)] = cnt["M"]
                setfree(xnT, ("M", cnt["M"]))
            if P1PART >= 4:
                for dkb in range(8):
                    vop(lambda e, o=ap("ksg", slice(g * 8 + dkb, g * 8 + dkb + 1)),
                        a=tn["kT"][:, dkb * 512:(dkb + 1) * 512]:
                        e.reduce_sum(o, a, AX))
                ks_v = cnt["V"]
                wait("scalar", "V", qk_last_v[(g, "qT")])
                q_spq = dma("scalar", "Qq", sp_q[g * 128:(g + 1) * 128, :], ap("qT"))
                wait("scalar", "S", v_evac_s)
                q_spv = dma("scalar", "Qv", sp_v[g * 128:(g + 1) * 128, :], ap("vtok"))
                if g == 0:
                    gate("sync", "ktok")
                wait("sync", "V", qk_last_v[(g, "kT")])
                for dkb in range(8):
                    dest = tn["ktok"].ap().rearrange("p (s d) -> p s d", s=4)[:, :, dkb * 128:(dkb + 1) * 128]
                    dmat("sync", "KT", dest, tn["kT"][:, dkb * 512:(dkb + 1) * 512])
                kt_L = cnt["KT"]
            if P1PART >= 5:
                aT = f"aT{gp}"
                gate("vector", aT)
                a_evac = []
                for sb in range(4):
                    pb = "ps_a" if sb % 2 == 0 else "ps_s0"
                    gate("tensor", pb)
                    if sb == 0:
                        wait("tensor", "V", qk_last_v[(g, "qT")])
                        wait("tensor", "V", qk_last_v[(g, "kT")])
                    for dc in range(8):
                        mmc = mm(tn["kT"][:, dc * 512 + sb * 128: dc * 512 + (sb + 1) * 128],
                                 tn["qT"][:, dc * 512 + sb * 128: (dc + 1) * 512],
                                 tn[pb][:, sb * 128:512], start=(dc == 0), stop=(dc == 7))
                    wait("vector", "M", mmc)
                    vop(lambda e, o=tn[aT][:, PK[sb]:PK[sb] + 128],
                        p=tn[pb][:, sb * 128:sb * 128 + 128], m=ap("mask_ut"):
                        e.tensor_tensor(o, p, m, ALU.mult))
                    if sb < 3:
                        v_a = vop(lambda e, o=tn[aT][:, PK[sb] + 128:PK[sb] + PW[sb]],
                                  p=tn[pb][:, sb * 128 + 128:512]:
                                  e.tensor_copy(o, p))
                    else:
                        v_a = cnt["V"]
                    setfree(pb, ("V", v_a))
                    a_evac.append(v_a)
                wait("scalar", "V", a_evac[-1])
                q_spa = dma("scalar", f"Qa{g % 2}", sp_a[g * 128:(g + 1) * 128, :], tn[aT][:, :])
                setfree(aT, (f"Qa{g % 2}", q_spa))
                setfree("qT", ("Qq", q_spq), ("M", cnt["M"]))
                setfree("kT", ("M", cnt["M"]), ("V", ks_v), ("KT", kt_L))
            if P1PART >= 6:
                for dkb in range(8):
                    for eh in range(2):
                        idx = dkb * 2 + eh
                        pb = f"ps_s{idx % 2}"
                        gate("tensor", pb)
                        if idx == 0:
                            wait("tensor", "KT", kt_L)
                            wait("tensor", "S", v_evac_s)
                        for sb in range(4):
                            mmc = mm(tn["ktok"][:, sb * 1024 + dkb * 128: sb * 1024 + (dkb + 1) * 128],
                                     tn["vtok"][:, sb * 1024 + eh * 512: sb * 1024 + (eh + 1) * 512],
                                     ap(pb), start=(sb == 0), stop=(sb == 3))
                        fr = []
                        if S_SUB >= 1:
                            wait("vector", "M", mmc)
                            v_kv = vop(lambda e, o=tn["kvr"][:, dkb * 1024 + eh * 512: dkb * 1024 + (eh + 1) * 512],
                                       p=ap(pb): e.tensor_tensor(o, o, p, ALU.add))
                            fr.append(("V", v_kv))
                        if S_SUB >= 2:
                            sg = f"rT{idx % 2}"
                            gate("vector", sg + "_s")
                            wait("vector", "M", mmc)
                            s_c = vop(lambda e, o=tn[sg][:, 0:512], p=ap(pb): e.tensor_copy(o, p))
                            fr.append(("V", s_c))
                        setfree(pb, *fr)
                        if S_SUB >= 3:
                            wait("scalar", "V", s_c)
                            q_s = dma("scalar", f"Qs{idx % 2}",
                                      sp_s[g * 1024 + dkb * 128: g * 1024 + (dkb + 1) * 128,
                                           eh * 512:(eh + 1) * 512],
                                      tn[sg][:, 0:512])
                            setfree(sg + "_s", (f"Qs{idx % 2}", q_s))
                setfree("ktok", ("M", cnt["M"]))
                setfree("vtok", ("M", cnt["M"]), ("Qv", q_spv))
                mid[("kvr_v", g)] = cnt["V"]

        if STAGE >= 3:
            vop(lambda e, o=ap("kst"), a=ap("ksg", slice(0, 8)), b=ap("ksg", slice(8, 16)):
                e.tensor_tensor(o, a, b, ALU.add))
            for g2 in range(2, 8):
                vop(lambda e, o=ap("kst"), a=ap("kst"), b=ap("ksg", slice(g2 * 8, g2 * 8 + 8)):
                    e.tensor_tensor(o, a, b, ALU.add))
            kst_v = cnt["V"]
            gate("tensor", "ps_a")
            wait("tensor", "V", cnt["V"])
            mc = mm(ap("amx0"), ap("ident"), tn["ps_a"][0:1, 0:128], True, True)
            wait("vector", "M", mc)
            vop(lambda e, o=rowap("td", 0, slice(0, 128)), a=tn["ps_a"][0:1, 0:128]:
                e.tensor_copy(o, a))
            v_mx = vop(lambda e, o=rowap("smf", 0, slice(0, 1)), a=rowap("td", 0, slice(0, 128)):
                       e.reduce_max(o, a, AX))
            setfree("ps_a", ("V", cnt["V"]))
            wait("scalar", "V", v_mx)
            q_mx = dma("scalar", "Qm", mx_in, rowap("smf", 0, slice(0, 1)))
            gate("vector", "scr0")
            vop(lambda e, o=ap("scr0", slice(0, 1)), a=ap("tn32"): e.reduce_sum(o, a, AX))
            v_nc = vop(lambda e, o=ap("scr0", slice(1, 2)), a=ap("tc32"): e.reduce_sum(o, a, AX))
            gate("tensor", "ps_a")
            wait("tensor", "V", v_nc)
            mm(ap("scr0", slice(0, 1)), ap("ident"), tn["ps_a"][0:1, 0:128], True, True)
            mc2 = mm(ap("scr0", slice(1, 2)), ap("ident"), tn["ps_a"][0:1, 128:256], True, True)
            wait("vector", "M", mc2)
            vop(lambda e, o=rowap("td", 0, slice(0, 256)), a=tn["ps_a"][0:1, 0:256]:
                e.tensor_copy(o, a))
            vop(lambda e, o=rowap("smf", 0, slice(5, 6)), a=rowap("td", 0, slice(0, 128)):
                e.reduce_sum(o, a, AX))
            vop(lambda e, o=rowap("smf", 0, slice(6, 7)), a=rowap("td", 0, slice(128, 256)):
                e.reduce_sum(o, a, AX))
            setfree("ps_a", ("V", cnt["V"]))
            vop(lambda e, o=rowap("smf", 0, slice(7, 8)), a=rowap("smf", 0, slice(6, 7)):
                e.tensor_scalar(o, a, 1.0, None, ALU.max))
            vop(lambda e, o=rowap("smf", 0, slice(8, 9)), a=rowap("smf", 0, slice(7, 8)):
                e.reciprocal(o, a))
            v_al = vop(lambda e, o=rowap("smf", 0, slice(9, 10)), a=rowap("smf", 0, slice(5, 6)),
                       b=rowap("smf", 0, slice(8, 9)): e.tensor_tensor(o, a, b, ALU.mult))
            wait("gpsimd", "V", mid[("kvr_v", NGRUN - 1)])
            for dkb in range(8):
                dma("gpsimd", "Gc", cc_in[dkb * 128:(dkb + 1) * 128, 0:1024],
                    tn["kvr"][:, dkb * 1024:(dkb + 1) * 1024])
            gate("tensor", "ps_s1")
            wait("tensor", "V", kst_v)
            mc_ks = mm(ap("kst"), ap("ident"), tn["ps_s1"][0:8, 0:128], True, True)
            wait("vector", "M", mc_ks)
            gate("vector", "rT0_s")
            v_ksr = vop(lambda e, o=tn["rT0"][0:8, 0:128], a=tn["ps_s1"][0:8, 0:128]:
                        e.tensor_copy(o, a))
            setfree("ps_s1", ("V", v_ksr))
            wait("gpsimd", "V", v_ksr)
            dma("gpsimd", "Gc",
                cc_in[1024:1025, :].rearrange("r (d p) -> (r d) p", d=8),
                tn["rT0"][0:8, 0:128])
            g_cc = cnt["Gc"]
            setfree("rT0_s", ("Gc", g_cc))
            wait("gpsimd", "Gc", g_cc)
            wait("gpsimd", "Qm", q_mx)
            cc_done = gop(lambda e: e.collective_compute(
                "AllReduce", ALU.max,
                replica_groups=[list(range(8))],
                ins=[mx_in_t.ap()], outs=[mx_out_t.ap()]))
            emit("gpsimd", lambda e: e.collective_compute(
                "AllGather", ALU.bypass,
                replica_groups=[[0, 1, 2, 3], [4, 5, 6, 7]],
                ins=[cc_in_t.ap()], outs=[cc_out_t.ap()]).then_inc(tn["sem_C"], 1))
            cnt["C"] += 1
            ag_done = cnt["C"]

            for ob in range(8):
                par = ob % 2
                st = f"xnb{par}"
                gate("gpsimd", st)
                gc = dma("gpsimd", f"Gx{par}", ap(st), wo_h[ob * 128:(ob + 1) * 128, :])
                wait("sync", f"Gx{par}", gc)
                wait("sync", "M", mid[("xnT_free", NGRUN - 1)])
                dest = tn["w1"].ap().rearrange("p (d o) -> p d o", d=8)[:, :, ob * 128:(ob + 1) * 128]
                lc = dmat("sync", f"WP{par}", dest, ap(st))
                setfree(st, (f"WP{par}", lc))
            wo_WP = (cnt["WP0"], cnt["WP1"])
            for ob in range(8):
                par = ob % 2
                st = f"xnb{par}"
                gate("sync", st)
                wait("sync", "Qt", cnt["Qt"])
                ld = dma("sync", f"Lt{par}", ap(st), sp_t[ob * 128:(ob + 1) * 128, :])
                wait("scalar", f"Lt{par}", ld)
                dest = tn["w2"].ap().rearrange("p (d o) -> p d o", d=8)[:, :, ob * 128:(ob + 1) * 128]
                lc = dmat("scalar", f"Qw{par}", dest, ap(st))
                setfree(st, (f"Qw{par}", lc))
            w2_Q = (cnt["Qw0"], cnt["Qw1"])

            wait("sync", "C", cc_done)
            l_mx = dma("sync", "Lm", rowap("smf", 0, slice(10, 11)), mx_out)
            wait("vector", "Lm", l_mx)
            vop(lambda e, o=rowap("smf", 0, slice(11, 12)), a=rowap("smf", 0, slice(10, 11)):
                e.tensor_scalar(o, a, 1e-8, None, ALU.max))
            vop(lambda e, o=rowap("smf", 0, slice(12, 13)), a=rowap("smf", 0, slice(11, 12)):
                e.tensor_scalar(o, a, 1.0 / 127.0, None, ALU.mult))
            vop(lambda e, o=rowap("smf", 0, slice(13, 14)), a=rowap("smf", 0, slice(12, 13)):
                e.reciprocal(o, a))
            wait("vector", "V", v_al)
            vop(lambda e, o=rowap("smf", 0, slice(14, 15)), a=rowap("smf", 0, slice(9, 10)),
                b=rowap("smf", 0, slice(12, 13)): e.tensor_tensor(o, a, b, ALU.mult))
            v_as = vop(lambda e, o=rowap("smf", 0, slice(15, 16)), a=rowap("smf", 0, slice(14, 15)):
                       e.reciprocal(o, a))
            gate("tensor", "ps_s1")
            wait("tensor", "V", v_as)
            mc3 = mm(tn["mask_ut"][0:1, :], rowap("smf", 0, slice(13, 15)), tn["ps_s1"][:, 0:2], True, True)
            wait("vector", "M", mc3)
            v_bcs = vop(lambda e, o=ap("bcs"), a=tn["ps_s1"][:, 0:2]: e.tensor_copy(o, a))
            setfree("ps_s1", ("V", v_bcs))
            l_bo = dma("sync", "MS", rowap("xnf0", 0, slice(0, 1024)), bo_h)
            wait("vector", "MS", l_bo)
            vop(lambda e, o=rowap("xnf1", 0, slice(0, 1024)), a=rowap("xnf0", 0, slice(0, 1024)),
                s=rowap("smf", 0, slice(15, 16)): e.tensor_scalar(o, a, s, None, ALU.mult))
            vop(lambda e, o=tn["boeff"][0:1, :], a=rowap("xnf1", 0, slice(0, 1024)):
                e.tensor_copy(o, a))
            vop(lambda e, o=rowap("xnf0", 0, slice(0, 1024)), a=tn["boeff"][0:1, :]:
                e.tensor_copy(o, a))
            v_bo = vop(lambda e, o=tn["bvb"][0:1, :], a=rowap("xnf1", 0, slice(0, 1024)),
                       b=rowap("xnf0", 0, slice(0, 1024)): e.tensor_tensor(o, a, b, ALU.subtract))
            wait("sync", "M", mid[("xnT_free", NGRUN - 1)])
            wait("sync", "C", ag_done)
            wait("gpsimd", "C", ag_done)
            wait("gpsimd", "M", mid[("xnT_free", NGRUN - 1)])
            gate("sync", "qT")
            wait("vector", "Gx0", cnt["Gx0"])
            wait("vector", "Gx1", cnt["Gx1"])
            v_kv = 0
            for dkb in range(8):
                if dkb > 0:
                    wait("sync", "V", v_kv)
                for j in range(4):
                    dma("sync", "PS", tn["qT"][:, j * 1024:(j + 1) * 1024],
                        cc_out[j * 1025 + dkb * 128: j * 1025 + (dkb + 1) * 128, 0:1024])
                l_g = cnt["PS"]
                wait("vector", "PS", l_g)
                vop(lambda e, o=ap("u"), a=tn["qT"][:, 0:1024], m=ap("mselc", slice(0, 1)):
                    e.tensor_scalar(o, a, m, None, ALU.mult))
                vop(lambda e, o=ap("xnf0"), a=tn["qT"][:, 1024:2048], m=ap("mselc", slice(1, 2)), c=ap("u"):
                    e.scalar_tensor_tensor(o, a, m, c, ALU.mult, ALU.add))
                vop(lambda e, o=ap("u"), a=tn["qT"][:, 2048:3072], m=ap("mselc", slice(2, 3)), c=ap("xnf0"):
                    e.scalar_tensor_tensor(o, a, m, c, ALU.mult, ALU.add))
                v_kv = vop(lambda e, o=tn["kvr"][:, dkb * 1024:(dkb + 1) * 1024],
                           a=tn["qT"][:, 3072:4096], m=ap("mselc", slice(3, 4)), c=ap("u"):
                           e.scalar_tensor_tensor(o, a, m, c, ALU.mult, ALU.add))
                wait("gpsimd", "V", v_kv)
                dma("gpsimd", "Gk", tn["w3"][:, dkb * 1024:(dkb + 1) * 1024],
                    tn["kvr"][:, dkb * 1024:(dkb + 1) * 1024])
            kvb_G = cnt["Gk"]
            setfree("qT", ("V", cnt["V"]))
            gate("sync", "kT")
            wait("sync", "V", v_kv)
            for j in range(4):
                dma("sync", "PS", tn["kT"][0:1, j * 1024:(j + 1) * 1024],
                    cc_out[j * 1025 + 1024: j * 1025 + 1025, 0:1024])
            ksr_L = cnt["PS"]
            gate("tensor", "ps_s1")
            wait("tensor", "PS", ksr_L)
            for j in range(4):
                for dkb in range(8):
                    col = j * 8 + dkb
                    mc_k = mm(tn["kT"][0:1, j * 1024 + dkb * 128: j * 1024 + dkb * 128 + 128],
                              tn["ones_cb"][0:1, 0:1],
                              tn["ps_s1"][:, col:col + 1], True, True)
            setfree("kT", ("M", mc_k))
            wait("vector", "M", mc_k)
            v_ksl = vop(lambda e, o=ap("tc32"), a=tn["ps_s1"][:, 0:32]: e.tensor_copy(o, a))
            setfree("ps_s1", ("V", v_ksl))
            vop(lambda e, o=ap("ksB"), a=ap("tc32", slice(0, 8)), m=ap("mselc", slice(0, 1)):
                e.tensor_scalar(o, a, m, None, ALU.mult))
            vop(lambda e, o=ap("ksA"), a=ap("tc32", slice(8, 16)), m=ap("mselc", slice(1, 2)), c=ap("ksB"):
                e.scalar_tensor_tensor(o, a, m, c, ALU.mult, ALU.add))
            vop(lambda e, o=ap("ksB"), a=ap("tc32", slice(16, 24)), m=ap("mselc", slice(2, 3)), c=ap("ksA"):
                e.scalar_tensor_tensor(o, a, m, c, ALU.mult, ALU.add))
            vop(lambda e, o=ap("ksA"), a=ap("tc32", slice(24, 32)), m=ap("mselc", slice(3, 4)), c=ap("ksB"):
                e.scalar_tensor_tensor(o, a, m, c, ALU.mult, ALU.add))
            v_ksc = vop(lambda e, o=ap("kscb"), a=ap("ksA"): e.tensor_copy(o, a))

            bnd = dict(cnt)
            bnd_free = {
                "kT": ["M", "KT", "V"], "ktok": ["M"], "vtok": ["M", "Qv"],
                "aT0": ["Qa0"], "aT1": ["Qa1"], "xnT0": ["M"], "xnT1": ["M"],
                "xnf0": ["V", "Gx0"], "xnf1": ["V", "Gx1"],
                "xnb0": ["WP0", "Qw0"], "xnb1": ["WP1", "Qw1"],
                "x2t0": ["Qx0"], "x2t1": ["Qx1"],
                "xg0": ["S", "V"], "xg1": ["S", "V"], "xg2": ["S", "V"],
                "rT0": ["Qs0", "Gc"], "rT1": ["Qs1"],
                "rT0_s": ["Qs0", "Gc"], "rT1_s": ["Qs1"],
                "mn0": ["S"], "mn1": ["S"], "Rb0": ["V"], "Rb1": ["V"],
            }
            for key, ss in bnd_free.items():
                free[key] = {s: bnd[s] for s in ss}

        denr = rowap("u", 0, slice(0, 512))
        recr = rowap("u", 0, slice(512, 1024))
        inv_s_ap = ap("bcs", slice(0, 1))
        as_ap = ap("bcs", slice(1, 2))
        invas_ap = rowap("smf", 0, slice(15, 16))
        last_yq_M = 0
        attnT_v = {}
        for g in (range(NG) if STAGE >= 4 else []):
            pb2 = (g + 1) % 2
            qb = ["qT", "kT"][pb2]
            vb = ["ktok", "vtok"][pb2]
            ab = ["aT0", "aT1"][pb2]
            at = f"xnT{g % 2}"
            if g >= 1:
                for u16 in range(16):
                    dkb, eh = divmod(u16, 2)
                    sg = f"Rb{u16 % 2}"
                    gate("sync", sg)
                    ldS = dma("sync", f"Sg{u16 % 2}", tn[sg][:, 0:512],
                              sp_s[(g - 1) * 1024 + dkb * 128: (g - 1) * 1024 + (dkb + 1) * 128,
                                   eh * 512:(eh + 1) * 512])
                    wait("vector", f"Sg{u16 % 2}", ldS)
                    v_ad = vop(lambda e, o=tn["kvr"][:, dkb * 1024 + eh * 512: dkb * 1024 + (eh + 1) * 512],
                               a=tn[sg][:, 0:512]: e.tensor_tensor(o, o, a, ALU.add))
                    setfree(sg, ("V", v_ad))
                kv_adds_v = cnt["V"]
                src, dst_ = ("ksA", "ksB") if g % 2 == 1 else ("ksB", "ksA")
                vop(lambda e, o=ap(dst_), a=ap(src), b=ap("ksg", slice((g - 1) * 8, g * 8)):
                    e.tensor_tensor(o, a, b, ALU.add))
                gate("vector", "kscb")
                v_ksc = vop(lambda e, o=ap("kscb"), a=ap(dst_): e.tensor_copy(o, a))
                wait("gpsimd", "V", kv_adds_v)
                wait("gpsimd", "M", last_yq_M)
                for dkb in range(8):
                    dma("gpsimd", "Gk", tn["w3"][:, dkb * 1024:(dkb + 1) * 1024],
                        tn["kvr"][:, dkb * 1024:(dkb + 1) * 1024])
                kvb_G = cnt["Gk"]
            ldsem = f"Ld{pb2}"
            gate("sync", qb)
            l_q = dma("sync", ldsem, ap(qb), sp_q[g * 128:(g + 1) * 128, :])
            gate("sync", vb)
            l_v = dma("sync", ldsem, ap(vb), sp_v[g * 128:(g + 1) * 128, :])
            gate("sync", ab)
            l_a = dma("sync", ldsem, ap(ab), sp_a[g * 128:(g + 1) * 128, :])
            gate("tensor", "ps_a")
            wait("tensor", ldsem, l_a)
            wait("tensor", "V", v_ksc)
            for sb in range(4):
                mm(tn["ones_cb"][:, 0:1], tn[ab][:, PK[sb]:PK[sb] + PW[sb]],
                   tn["ps_a"][0:1, sb * 128:512], start=(sb == 0), stop=False)
            for dc in range(8):
                mmc = mm(ap("kscb", slice(dc, dc + 1)), tn[qb][:, dc * 512:(dc + 1) * 512],
                         tn["ps_a"][0:1, 0:512], start=False, stop=(dc == 7))
            setfree("kscb", ("M", mmc))
            wait("vector", "M", mmc)
            vop(lambda e, o=denr, a=tn["ps_a"][0:1, 0:512]:
                e.tensor_scalar(o, a, EPS_ATT, None, ALU.add))
            vop(lambda e, o=recr, a=denr: e.reciprocal(o, a))
            gate("vector", "rrowb")
            gate("vector", "sgb")
            vop(lambda e, o=denr, a=recr, s=invas_ap:
                e.tensor_scalar(o, a, s, None, ALU.mult))
            vop(lambda e, o=tn["rrowb"][0:1, :], a=denr: e.tensor_copy(o, a))
            vop(lambda e, o=recr, a=tn["rrowb"][0:1, :]: e.tensor_copy(o, a))
            v_rr = vop(lambda e, o=tn["sgb"][0:1, 0:512], a=denr, b=recr:
                       e.tensor_tensor(o, a, b, ALU.subtract))
            setfree("ps_a", ("V", cnt["V"]))
            gate("tensor", "ps_s0")
            wait("tensor", "V", v_rr)
            mm(tn["ones_rb"][0:1, :], tn["rrowb"][0:1, :], ap("ps_s0"), True, False)
            mcb = mm(tn["ones_rb"][0:1, :], tn["sgb"][0:1, 0:512], ap("ps_s0"), False, True)
            setfree("rrowb", ("M", mcb))
            setfree("sgb", ("M", mcb))
            wait("vector", "M", mcb)
            gate("vector", "mn0")
            v_bc = vop(lambda e, o=ap("mn0"), a=ap("ps_s0"): e.tensor_copy(o, a))
            setfree("ps_s0", ("V", v_bc))
            for eb in range(8):
                pq = f"ps_q{eb % 2}"
                gate("tensor", pq)
                if eb == 0:
                    wait("tensor", "Gk", kvb_G)
                for dc in range(8):
                    mm(tn["w3"][:, dc * 1024 + eb * 128: dc * 1024 + (eb + 1) * 128],
                       tn[qb][:, dc * 512:(dc + 1) * 512],
                       ap(pq), start=(dc == 0), stop=False)
                for sb in range(4):
                    mmc = mm(tn[vb][:, sb * 1024 + eb * 128: sb * 1024 + (eb + 1) * 128],
                             tn[ab][:, PK[sb]:PK[sb] + PW[sb]],
                             tn[pq][:, sb * 128:512], start=False, stop=(sb == 3))
                wait("vector", "M", mmc)
                if eb == 0:
                    gate("vector", at)
                v_at = vop(lambda e, o=tn[at][:, eb * 512:(eb + 1) * 512], p=ap(pq), b=ap("mn0"):
                           e.tensor_tensor(o, p, b, ALU.mult))
                setfree(pq, ("V", v_at))
                attnT_v[(g, eb)] = v_at
            last_yq_M = cnt["M"]
            setfree(qb, ("M", cnt["M"]))
            setfree(vb, ("M", cnt["M"]))
            setfree(ab, ("M", cnt["M"]))
            for i in range(4):
                ti = g * 4 + i
                x2 = f"x2t{i % 2}"
                rb = f"xnb{i % 2}"
                rt = f"rT{i % 2}"
                slot = f"xg{ti % 3}"
                gate("sync", x2)
                l_x2 = dma("sync", f"X2{i % 2}", ap(x2), sp_x2[ti * 128:(ti + 1) * 128, :])
                gate("vector", rb)
                wait("vector", f"X2{i % 2}", l_x2)
                for h in range(2):
                    v_t1 = vop(lambda e, o=ap("i32h"), a=tn[x2][:, h * 512:(h + 1) * 512], s=inv_s_ap:
                               e.tensor_scalar(o, a, s, None, ALU.mult))
                    v_rb = vop(lambda e, o=tn[rb][:, h * 512:(h + 1) * 512], a=ap("i32h"):
                               e.tensor_copy(o, a))
                setfree(x2, ("V", v_rb))
                gate("scalar", rt)
                wait("scalar", "V", v_rb)
                dest = tn[rt].ap().rearrange("p (d t) -> p d t", d=8)[:, :, 0:128]
                q_rt = dmat("scalar", f"Qr{i % 2}", dest, ap(rb))
                setfree(rb, (f"Qr{i % 2}", q_rt))
                gate("sync", slot)
                l_x = dma("sync", f"Lx{ti % 3}", ap(slot), x_h[ti * 128:(ti + 1) * 128, :])
                for oh in range(2):
                    idx = (2 * i + oh) % 2
                    pv = f"ps_v{idx}"
                    gate("tensor", pv)
                    if i == 0 and oh == 0:
                        for eb in range(8):
                            wait("tensor", "V", attnT_v[(g, eb)])
                        wait("tensor", "Qw0", w2_Q[0])
                        wait("tensor", "Qw1", w2_Q[1])
                        wait("tensor", "WP0", wo_WP[0])
                        wait("tensor", "WP1", wo_WP[1])
                        wait("tensor", "V", v_bo)
                    for ac in range(8):
                        mm(tn[at][:, ac * 512 + i * 128: ac * 512 + (i + 1) * 128],
                           tn["w1"][:, ac * 1024 + oh * 512: ac * 1024 + (oh + 1) * 512],
                           ap(pv), start=(ac == 0), stop=False)
                    if oh == 0:
                        wait("tensor", f"Qr{i % 2}", q_rt)
                    for ic in range(8):
                        mm(tn[rt][:, ic * 128:(ic + 1) * 128],
                           tn["w2"][:, ic * 1024 + oh * 512: ic * 1024 + (oh + 1) * 512],
                           ap(pv), start=False, stop=False)
                    mm(tn["ones_rb"][0:1, :], tn["boeff"][0:1, oh * 512:(oh + 1) * 512],
                       ap(pv), start=False, stop=False)
                    mmc = mm(tn["ones_rb"][0:1, :], tn["bvb"][0:1, oh * 512:(oh + 1) * 512],
                             ap(pv), start=False, stop=True)
                    if oh == 0:
                        gate("vector", f"xnf{i % 2}")
                    wait("vector", "M", mmc)
                    wait("vector", f"Lx{ti % 3}", l_x)
                    v_out = vop(lambda e, o=tn[f"xnf{i % 2}"][:, oh * 512:(oh + 1) * 512],
                                p=ap(pv), s=as_ap, x=tn[slot][:, oh * 512:(oh + 1) * 512]:
                                e.scalar_tensor_tensor(o, p, s, x, ALU.mult, ALU.add))
                    setfree(pv, ("V", v_out))
                setfree(slot, ("V", v_out))
                wait("scalar", "V", v_out)
                q_o = dma("scalar", f"Qo{i % 2}", out_h[ti * 128:(ti + 1) * 128, :], ap(f"xnf{i % 2}"))
                setfree(f"xnf{i % 2}", (f"Qo{i % 2}", q_o))
                setfree(rt, ("M", cnt["M"]))
            setfree(at, ("M", cnt["M"]))

        wait("gpsimd", "Qo0", cnt["Qo0"])
        wait("gpsimd", "Qo1", cnt["Qo1"])

        @block.sync
        def _(eng):
            for f in plan["sync"]:
                f(eng)

        @block.scalar
        def _(eng):
            for f in plan["scalar"]:
                f(eng)

        @block.vector
        def _(eng):
            for f in plan["vector"]:
                f(eng)

        @block.tensor
        def _(eng):
            for f in plan["tensor"]:
                f(eng)

        @block.gpsimd
        def _(eng):
            for f in plan["gpsimd"]:
                f(eng)

    return nc


_NC_CACHE = None


def kernel(L0_attn_chunk, L0_attn_Wq, L0_attn_bq, L0_attn_Wk, L0_attn_bk,
           L0_attn_Wv, L0_attn_bv, L0_attn_Wo, L0_attn_bo,
           L0_ffn_weight, L0_ffn_gamma, L0_ffn_beta):
    global _NC_CACHE
    if _NC_CACHE is None:
        _NC_CACHE = build()
    nc = _NC_CACHE
    c = np.ascontiguousarray

    x = np.asarray(L0_attn_chunk, dtype=np.float32)
    wq = c(np.asarray(L0_attn_Wq, np.float32))
    wk = c(np.asarray(L0_attn_Wk, np.float32))
    wv = c(np.asarray(L0_attn_Wv, np.float32))
    wo = c(np.asarray(L0_attn_Wo, np.float32))
    wf = c(np.asarray(L0_ffn_weight, np.float32))
    bqc = c(np.asarray(L0_attn_bq, np.float32).reshape(8, 128).T)
    bkc = c(np.asarray(L0_attn_bk, np.float32).reshape(8, 128).T)
    bvr = c(np.asarray(L0_attn_bv, np.float32).reshape(1, D))
    bor = c(np.asarray(L0_attn_bo, np.float32).reshape(1, D))
    gamb = c(np.broadcast_to(np.asarray(L0_ffn_gamma, np.float32).reshape(1, D), (128, D)))
    betb = c(np.broadcast_to(np.asarray(L0_ffn_beta, np.float32).reshape(1, D), (128, D)))

    identc = np.eye(128, dtype=np.float32)
    mtric = c(np.triu(np.ones((128, 128), np.float32)))
    in_maps = []
    for core in range(NCORE):
        b, q = core // 4, core % 4
        msel = np.zeros((128, 4), np.float32)
        msel[:, :q] = 1.0
        in_maps.append({
            "x": c(x[b, q * TC:(q + 1) * TC, :]),
            "wq": wq, "wk": wk, "wv": wv, "wo": wo, "wf": wf,
            "bqc": bqc, "bkc": bkc, "bvr": bvr, "bor": bor,
            "gamb": gamb, "betb": betb, "msel": msel,
            "identc": identc, "mtric": mtric,
        })
    res = run_bass_kernel_spmd(nc, in_maps, core_ids=list(range(NCORE)))
    out = np.empty((B, T, D), np.float32)
    for core in range(NCORE):
        b, q = core // 4, core % 4
        out[b, q * TC:(q + 1) * TC, :] = res.results[core]["out"]
    return out


# version 07 (final)
# speedup vs baseline: 1.0192x; 1.0073x over previous
import numpy as np
import concourse.bass as bass
import concourse.mybir as mybir
from concourse.bass_utils import run_bass_kernel_spmd

F32 = mybir.dt.float32
BF16 = mybir.dt.bfloat16
I32 = mybir.dt.int32
ALU = mybir.AluOpType
ACT = mybir.ActivationFunctionType
AX = mybir.AxisListType.X

STAGE = 4
NGRUN = 8
P1PART = 6
S_SUB = 3
TERN_ON = True
NCORE = 8
B, T, D = 2, 16384, 1024
TC = T // 4
NT = TC // 128
NG = 8
PK = [0, 512, 896, 1152]
PW = [512, 384, 256, 128]
EPS1 = 1e-5
EPS_ATT = 1e-6


def build():
    nc = bass.Bass("TRN2", target_bir_lowering=False)

    x_h = nc.dram_tensor("x", [TC, D], F32, kind="ExternalInput").ap()
    wq_h = nc.dram_tensor("wq", [D, D], F32, kind="ExternalInput").ap()
    wk_h = nc.dram_tensor("wk", [D, D], F32, kind="ExternalInput").ap()
    wv_h = nc.dram_tensor("wv", [D, D], F32, kind="ExternalInput").ap()
    wo_h = nc.dram_tensor("wo", [D, D], F32, kind="ExternalInput").ap()
    wf_h = nc.dram_tensor("wf", [D, D], F32, kind="ExternalInput").ap()
    bqc_h = nc.dram_tensor("bqc", [128, 8], F32, kind="ExternalInput").ap()
    bkc_h = nc.dram_tensor("bkc", [128, 8], F32, kind="ExternalInput").ap()
    bv_h = nc.dram_tensor("bvr", [1, D], F32, kind="ExternalInput").ap()
    bo_h = nc.dram_tensor("bor", [1, D], F32, kind="ExternalInput").ap()
    gam_h = nc.dram_tensor("gamb", [128, D], F32, kind="ExternalInput").ap()
    bet_h = nc.dram_tensor("betb", [128, D], F32, kind="ExternalInput").ap()
    msel_h = nc.dram_tensor("msel", [128, 4], F32, kind="ExternalInput").ap()
    identh = nc.dram_tensor("identc", [128, 128], F32, kind="ExternalInput").ap()
    mtrih = nc.dram_tensor("mtric", [128, 128], F32, kind="ExternalInput").ap()
    out_h = nc.dram_tensor("out", [TC, D], F32, kind="ExternalOutput").ap()

    sp_q = nc.dram_tensor("sp_q", [NG * 128, 4096], BF16).ap()
    sp_v = nc.dram_tensor("sp_v", [NG * 128, 4096], BF16).ap()
    sp_a = nc.dram_tensor("sp_a", [NG * 128, 1280], BF16).ap()
    sp_x2 = nc.dram_tensor("sp_x2", [TC, D], F32).ap()
    sp_s = nc.dram_tensor("sp_s", [NG * 1024, 1024], BF16).ap()
    sp_t = nc.dram_tensor("sp_t", [1024, 1024], BF16).ap()
    cc_in_t = nc.dram_tensor("cc_in", [1025, 1024], BF16)
    cc_out_t = nc.dram_tensor("cc_out", [4 * 1025, 1024], BF16)
    mx_in_t = nc.dram_tensor("mx_in", [1, 1], F32)
    mx_out_t = nc.dram_tensor("mx_out", [1, 1], F32)
    cc_in, cc_out = cc_in_t.ap(), cc_out_t.ap()
    mx_in, mx_out = mx_in_t.ap(), mx_out_t.ap()

    SB = [
        ("w1", 8192, BF16), ("w2", 8192, BF16), ("w3", 8192, BF16),
        ("kvr", 8192, F32), ("gbc", 1024, F32), ("bbc", 1024, F32),
        ("xg0", 1024, F32), ("xg1", 1024, F32), ("xg2", 1024, F32),
        ("xnf0", 1024, F32), ("xnf1", 1024, F32),
        ("xnb0", 1024, BF16), ("xnb1", 1024, BF16),
        ("xnT0", 4096, BF16), ("xnT1", 4096, BF16),
        ("qT", 4096, BF16), ("kT", 4096, BF16),
        ("ktok", 4096, BF16), ("vtok", 4096, BF16),
        ("aT0", 1280, BF16), ("aT1", 1280, BF16),
        ("mn0", 512, F32), ("mn1", 512, F32),
        ("Rb0", 512, BF16), ("Rb1", 512, BF16), ("sgb", 512, BF16),
        ("u", 1024, F32), ("i32h", 512, I32),
        ("x2t0", 1024, F32), ("x2t1", 1024, F32),
        ("rT0", 1024, BF16), ("rT1", 1024, BF16),
        ("bqc", 8, F32), ("bkc", 8, F32), ("bvb", 1024, BF16), ("boeff", 1024, BF16),
        ("mselc", 4, F32), ("stats", 96, F32), ("scr0", 8, F32), ("scr1", 8, F32),
        ("kscb2", 8, BF16),
        ("ksg", 64, F32), ("kst", 8, F32), ("ksA", 8, F32), ("ksB", 8, F32),
        ("kscb", 8, BF16), ("amx0", 1, F32), ("amx1", 1, F32),
        ("ident", 128, F32), ("mask_ut", 128, F32),
        ("ones_cb", 1, BF16), ("ones_rb", 128, BF16),
        ("rrowb", 512, BF16), ("smf", 16, F32),
        ("tw32", 32, F32), ("tn32", 32, F32), ("tc32", 32, F32),
        ("bcs", 2, F32), ("dcol", 1, F32),
        ("ta", 256, F32), ("tb2", 256, F32), ("tc2", 256, F32), ("td", 256, F32),
        ("tf", 256, BF16),
    ]
    used = sum(w * {F32: 4, BF16: 2, I32: 4}[d] for _, w, d in SB)
    assert used <= 207000, f"SBUF over budget: {used}"

    sems = ["M", "V", "S", "C",
            "MS", "Gc", "Lx0", "Lx1", "Lx2", "Gx0", "Gx1", "WP0", "WP1", "Ta", "Qt",
            "TT00", "TT01", "TT10", "TT11", "Qx0", "Qx1", "Qq", "Qv", "KT",
            "Qa0", "Qa1", "Qs0", "Qs1", "Qm", "Lt0", "Lt1", "Qw0", "Qw1",
            "Lm", "PS", "Gk",
            "Sg0", "Sg1", "Ld0", "Ld1", "X20", "X21", "Qr0", "Qr1", "Qo0", "Qo1"]
    cnt = {s: 0 for s in sems}
    plan = {"sync": [], "scalar": [], "vector": [], "tensor": [], "gpsimd": []}
    tn = {}
    free = {}

    def ap(name, sl=None):
        t = tn[name]
        return t[:, :] if sl is None else t[:, sl]

    def rowap(name, r, sl):
        return tn[name][r:r + 1, sl]

    def emit(eng, fn):
        plan[eng].append(fn)

    def wait(eng, sem, val):
        if val > 0:
            emit(eng, lambda e, s=sem, v=val: e.wait_ge(tn["sem_" + s], v))

    def gate(eng, key):
        for s, v in free.pop(key, {}).items():
            wait(eng, s, v)

    def setfree(key, *pairs):
        d = free.setdefault(key, {})
        for s, v in pairs:
            d[s] = max(d.get(s, 0), v)

    def dma(eng, sem, out, in_):
        emit(eng, lambda e, o=out, i=in_, s=sem: e.dma_start(out=o, in_=i).then_inc(tn["sem_" + s], 16))
        cnt[sem] += 16
        return cnt[sem]

    def dmat(eng, sem, out, in_):
        emit(eng, lambda e, o=out, i=in_, s=sem: e.dma_start_transpose(o, i).then_inc(tn["sem_" + s], 16))
        cnt[sem] += 16
        return cnt[sem]

    def mm(lhsT, rhs, out, start, stop):
        if stop:
            emit("tensor", lambda e, a=lhsT, b=rhs, o=out, st=start:
                 e.matmul(o, a, b, start=st, stop=True).then_inc(tn["sem_M"], 1))
            cnt["M"] += 1
        else:
            emit("tensor", lambda e, a=lhsT, b=rhs, o=out, st=start:
                 e.matmul(o, a, b, start=st, stop=False))
        return cnt["M"]

    def vop(fn):
        wait("vector", "V", cnt["V"])
        emit("vector", lambda e, f=fn: f(e).then_inc(tn["sem_V"], 1))
        cnt["V"] += 1
        return cnt["V"]

    def sop(fn):
        wait("scalar", "S", cnt["S"])
        emit("scalar", lambda e, f=fn: f(e).then_inc(tn["sem_S"], 1))
        cnt["S"] += 1
        return cnt["S"]

    def gop(fn):
        wait("gpsimd", "C", cnt["C"])
        emit("gpsimd", lambda e, f=fn: f(e).then_inc(tn["sem_C"], 1))
        cnt["C"] += 1
        return cnt["C"]

    import contextlib
    with contextlib.ExitStack() as stack:
        for name, w, dt in SB:
            tn[name] = stack.enter_context(nc.sbuf_tensor("sb_" + name, [128, w], dt))
        for pname in ["ps_q0", "ps_q1", "ps_v0", "ps_v1", "ps_a", "ps_s0", "ps_s1"]:
            tn[pname] = stack.enter_context(nc.psum_tensor(pname, [128, 512], F32))
        for s in sems:
            tn["sem_" + s] = stack.enter_context(nc.semaphore("sem" + s))
        block = stack.enter_context(nc.Block())

        dma("sync", "MS", ap("bqc"), bqc_h)
        dma("sync", "MS", ap("bkc"), bkc_h)
        dma("sync", "MS", ap("gbc"), gam_h)
        dma("sync", "MS", ap("bbc"), bet_h)
        dma("sync", "MS", ap("mselc"), msel_h)
        dma("sync", "MS", ap("ident"), identh)
        dma("sync", "MS", ap("mask_ut"), mtrih)
        lp = cnt["MS"]
        dma("gpsimd", "Gc", tn["bvb"][0:1, :], bv_h)

        for i in range(3):
            dma("sync", f"Lx{i}", ap(f"xg{i}"), x_h[i * 128:(i + 1) * 128, :])
        wait("scalar", "MS", lp)
        vop(lambda e: e.memset(ap("ones_cb"), 1.0))
        vop(lambda e: e.memset(tn["ones_rb"][0:2, :], 1.0))
        vop(lambda e: e.memset(ap("kvr"), 0.0))
        vop(lambda e: e.memset(ap("amx0"), 0.0))
        vop(lambda e: e.memset(ap("ksA"), 0.0))
        wait("vector", "MS", lp)

        for wi, (wh, wt) in enumerate([(wq_h, "w1"), (wk_h, "w2"), (wv_h, "w3")]):
            for ob in range(8):
                par = (wi * 8 + ob) % 2
                st = f"xnb{par}"
                gate("gpsimd", st)
                gc = dma("gpsimd", f"Gx{par}", ap(st), wh[ob * 128:(ob + 1) * 128, :])
                wait("sync", f"Gx{par}", gc)
                dest = tn[wt].ap().rearrange("p (d o) -> p d o", d=8)[:, :, ob * 128:(ob + 1) * 128]
                lc = dmat("sync", f"WP{par}", dest, ap(st))
                setfree(st, (f"WP{par}", lc))
        wp_L = (cnt["WP0"], cnt["WP1"])

        mid = {}

        def tern_pass1(j):
            ob, sub = divmod(j, 4)
            gate("sync", "ta")
            ld = dma("sync", "Ta", ap("ta"), wf_h[ob * 128:(ob + 1) * 128, sub * 256:(sub + 1) * 256])
            gate("scalar", "tb2")
            wait("scalar", "Ta", ld)
            sc = sop(lambda e, o=ap("tb2"), a=ap("ta"), acc=ap("tw32", slice(j, j + 1)):
                     e.activation(o, a, ACT.Abs, accum_out=acc))
            setfree("ta", ("S", sc))
            setfree("tb2", ("S", sc))
            mid["tp1_s"] = sc

        def tern_delta():
            gate("vector", "scr0")
            wait("vector", "S", mid["tp1_s"])
            v1 = vop(lambda e, o=ap("scr0", slice(0, 1)), a=ap("tw32"): e.reduce_sum(o, a, AX))
            gate("tensor", "ps_a")
            wait("tensor", "V", v1)
            wait("tensor", "MS", lp)
            mc = mm(ap("scr0", slice(0, 1)), ap("ident"), tn["ps_a"][0:1, 0:128], True, True)
            setfree("scr0", ("M", mc))
            wait("vector", "M", mc)
            vop(lambda e, o=rowap("td", 0, slice(0, 128)), a=tn["ps_a"][0:1, 0:128]:
                e.tensor_copy(o, a))
            vop(lambda e, o=rowap("smf", 0, slice(2, 3)), a=rowap("td", 0, slice(0, 128)):
                e.reduce_sum(o, a, AX))
            v2 = vop(lambda e, o=rowap("smf", 0, slice(3, 4)), a=rowap("smf", 0, slice(2, 3)):
                     e.tensor_scalar(o, a, 0.7 / (1024.0 * 1024.0), None, ALU.mult))
            setfree("ps_a", ("V", cnt["V"]))
            gate("tensor", "ps_a")
            wait("tensor", "V", v2)
            mc2 = mm(tn["mask_ut"][0:1, :], rowap("smf", 0, slice(3, 4)), tn["ps_a"][:, 0:1], True, True)
            wait("vector", "M", mc2)
            v3 = vop(lambda e, o=ap("dcol"), a=tn["ps_a"][:, 0:1]: e.tensor_copy(o, a))
            setfree("ps_a", ("V", v3))
            mid["dcol_v"] = v3

        def tern_pass2(j):
            ob, sub = divmod(j, 4)
            gate("sync", "ta")
            ld = dma("sync", "Ta", ap("ta"), wf_h[ob * 128:(ob + 1) * 128, sub * 256:(sub + 1) * 256])
            gate("scalar", "tb2")
            wait("scalar", "Ta", ld)
            s_ab = sop(lambda e, o=ap("tb2"), a=ap("ta"): e.activation(o, a, ACT.Abs))
            wait("vector", "S", s_ab)
            wait("vector", "V", mid["dcol_v"])
            vop(lambda e, o=ap("tc2"), a=ap("tb2"), d=ap("dcol"):
                e.tensor_scalar(o, a, d, None, ALU.is_gt))
            vop(lambda e, o=ap("tc32", slice(j, j + 1)), a=ap("tc2"): e.reduce_sum(o, a, AX))
            v_d = vop(lambda e, o=ap("td"), a=ap("tb2"), b=ap("tc2"):
                      e.tensor_tensor(o, a, b, ALU.mult))
            vop(lambda e, o=ap("tn32", slice(j, j + 1)), a=ap("td"): e.reduce_sum(o, a, AX))
            wait("scalar", "V", v_d)
            s_sg = sop(lambda e, o=ap("tb2"), a=ap("ta"): e.activation(o, a, ACT.Sign))
            setfree("ta", ("S", s_sg))
            gate("vector", "tf")
            wait("vector", "S", s_sg)
            v_t = vop(lambda e, o=ap("tf"), a=ap("tb2"), b=ap("tc2"):
                      e.tensor_tensor(o, a, b, ALU.mult))
            setfree("tb2", ("V", v_t))
            wait("scalar", "V", v_t)
            q_t = dma("scalar", "Qt", sp_t[ob * 128:(ob + 1) * 128, sub * 256:(sub + 1) * 256], ap("tf"))
            setfree("tf", ("Qt", q_t))

        qk_last_v = {}
        for g in range(NGRUN):
            gp = g % 2
            xnT = f"xnT{gp}"
            for i in range(4):
                ti = g * 4 + i
                slot = f"xg{ti % 3}"
                tp = f"xnf{i % 2}"
                tb = f"xnb{i % 2}"
                x2 = f"x2t{i % 2}"
                scr = f"scr{i % 2}"
                if ti >= 3:
                    gate("sync", slot)
                    ld = dma("sync", f"Lx{ti % 3}", ap(slot), x_h[ti * 128:(ti + 1) * 128, :])
                else:
                    ld = 16
                gate("scalar", tp)
                gate("scalar", scr)
                wait("scalar", f"Lx{ti % 3}", ld)
                sop(lambda e, o=ap(tp), x=ap(slot), a=ap(scr, slice(0, 1)):
                    e.activation(o, x, ACT.Identity, accum_out=a))
                s_st = sop(lambda e, o=ap(tp), x=ap(slot), a=ap(scr, slice(1, 2)):
                           e.activation(o, x, ACT.Square, accum_out=a))
                stc = ti * 3
                mneg = ap("stats", slice(stc, stc + 1))
                r1c = ap("stats", slice(stc + 1, stc + 2))
                r2c = ap("stats", slice(stc + 2, stc + 3))
                wait("vector", "S", s_st)
                vop(lambda e, o=mneg, a=ap(scr, slice(0, 1)):
                    e.tensor_scalar(o, a, -1.0 / 1024.0, None, ALU.mult))
                vop(lambda e, o=ap(scr, slice(2, 3)), a=ap(scr, slice(1, 2)):
                    e.tensor_scalar(o, a, 1.0 / 1024.0, None, ALU.mult))
                vop(lambda e, o=ap(scr, slice(3, 4)), a=mneg:
                    e.tensor_tensor(o, a, a, ALU.mult))
                v_var = vop(lambda e, o=ap(scr, slice(4, 5)), a=ap(scr, slice(2, 3)), b=ap(scr, slice(3, 4)):
                            e.tensor_tensor(o, a, b, ALU.subtract))
                vop(lambda e, o=ap(scr, slice(6, 7)), a=ap(scr, slice(4, 5)):
                    e.tensor_scalar(o, a, EPS1, None, ALU.add))
                wait("scalar", "V", cnt["V"])
                s_q1 = sop(lambda e, o=ap(scr, slice(5, 6)), a=ap(scr, slice(6, 7)):
                           e.activation(o, a, ACT.Sqrt))
                wait("vector", "S", s_q1)
                vop(lambda e, o=r1c, a=ap(scr, slice(5, 6)): e.reciprocal(o, a))
                vop(lambda e, o=ap(scr, slice(5, 6)), a=ap(scr, slice(6, 7)):
                    e.reciprocal(o, a))
                vop(lambda e, o=ap(scr, slice(7, 8)), a=ap(scr, slice(4, 5)), b=ap(scr, slice(5, 6)):
                    e.tensor_tensor(o, a, b, ALU.mult))
                v_v2 = vop(lambda e, o=ap(scr, slice(6, 7)), a=ap(scr, slice(7, 8)):
                           e.tensor_scalar(o, a, EPS1, None, ALU.add))
                wait("scalar", "V", v_v2)
                s_r2 = sop(lambda e, o=ap(scr, slice(5, 6)), a=ap(scr, slice(6, 7)):
                           e.activation(o, a, ACT.Sqrt))
                wait("vector", "S", s_r2)
                vop(lambda e, o=r2c, a=ap(scr, slice(5, 6)): e.reciprocal(o, a))
                v_xn = vop(lambda e, o=ap(tp), x=ap(slot), m=mneg, r=r1c:
                           e.tensor_scalar(o, x, m, r, ALU.add, ALU.mult))
                gate("gpsimd", tb)
                wait("gpsimd", "V", v_xn)
                g_c = dma("gpsimd", f"Gx{i % 2}", ap(tb), ap(tp))
                if i == 0:
                    gate("sync", xnT)
                wait("sync", f"Gx{i % 2}", g_c)
                dest = tn[xnT].ap().rearrange("p (d t) -> p d t", d=8)[:, :, i * 128:(i + 1) * 128]
                ttsem = f"TT{g % 2}{i % 2}"
                dmat("sync", ttsem, dest, ap(tb))
                setfree(tb, (ttsem, cnt[ttsem]))
                if i == 3:
                    mid[("tp_done", g)] = [(f"TT{g % 2}0", cnt[f"TT{g % 2}0"]),
                                           (f"TT{g % 2}1", cnt[f"TT{g % 2}1"])]
                v_u = vop(lambda e, o=ap("u"), a=ap(tp), b=ap("gbc"):
                          e.tensor_tensor(o, a, b, ALU.mult))
                gate("vector", x2)
                v_x2 = vop(lambda e, o=ap(x2), a=ap("u"), r=r2c, b=ap("bbc"):
                           e.scalar_tensor_tensor(o, a, r, b, ALU.mult, ALU.add))
                setfree(tp, (f"Gx{i % 2}", g_c), ("V", v_u))
                vop(lambda e, o=ap(scr, slice(0, 1)), a=ap(x2):
                    e.reduce_max(o, a, AX, apply_absolute_value=True))
                v_am = vop(lambda e, o=ap(f"amx{(ti + 1) % 2}"), a=ap(scr, slice(0, 1)), p=ap(f"amx{ti % 2}"):
                           e.tensor_scalar(o, a, p, None, ALU.max))
                setfree(scr, ("S", s_r2), ("V", v_am))
                wait("scalar", "V", v_x2)
                q_sp = dma("scalar", f"Qx{i % 2}", sp_x2[ti * 128:(ti + 1) * 128, :], ap(x2))
                setfree(x2, (f"Qx{i % 2}", q_sp))
                setfree(slot, ("V", v_xn), ("S", s_st))
            if TERN_ON:
                if g < NGRUN // 2 or NGRUN == 1:
                    for j in range(g * 8, g * 8 + 8):
                        tern_pass1(j)
                    if g == max(NGRUN // 2, 1) - 1:
                        tern_delta()
                else:
                    for j in range((g - NGRUN // 2) * 8, (g - NGRUN // 2) * 8 + 8):
                        tern_pass2(j)
            if P1PART >= 2:
                for proj, (wt, dst, bc_) in enumerate([("w1", "qT", "bqc"), ("w2", "kT", "bkc")]):
                    for ob in range(8):
                        pb = f"ps_q{ob % 2}"
                        gate("tensor", pb)
                        if ob == 0:
                            for sm, vv in mid[("tp_done", g)]:
                                wait("tensor", sm, vv)
                            if g == 0 and proj == 0:
                                wait("tensor", "WP0", wp_L[0])
                                wait("tensor", "WP1", wp_L[1])
                        for dc in range(8):
                            mmc = mm(tn[wt][:, dc * 1024 + ob * 128: dc * 1024 + (ob + 1) * 128],
                                     tn[xnT][:, dc * 512:(dc + 1) * 512],
                                     ap(pb), start=(dc == 0), stop=(dc == 7))
                        mnb = f"mn{ob % 2}"
                        rbb = f"Rb{ob % 2}"
                        gate("vector", mnb)
                        wait("vector", "M", mmc)
                        v_mn = vop(lambda e, o=ap(mnb), p=ap(pb), b=ap(bc_, slice(ob, ob + 1)):
                                   e.tensor_scalar(o, p, b, 0.0, ALU.add, ALU.min))
                        if ob == 0:
                            gate("scalar", dst)
                        wait("scalar", "V", v_mn)
                        s_e = sop(lambda e, o=tn[dst][:, ob * 512:(ob + 1) * 512], a=ap(mnb):
                                  e.activation(o, a, ACT.Exp))
                        setfree(mnb, ("S", s_e))
                        gate("scalar", rbb)
                        wait("scalar", "M", mmc)
                        s_r = sop(lambda e, o=ap(rbb), p=ap(pb), b=ap(bc_, slice(ob, ob + 1)):
                                  e.activation(o, p, ACT.Relu, bias=b, scale=1.0))
                        setfree(pb, ("V", v_mn), ("S", s_r))
                        wait("vector", "S", s_r)
                        v_q = vop(lambda e, o=tn[dst][:, ob * 512:(ob + 1) * 512], a=ap(rbb):
                                  e.tensor_tensor(o, o, a, ALU.add))
                        setfree(rbb, ("V", v_q))
                        qk_last_v[(g, dst)] = v_q
            if P1PART >= 3:
                for tb4 in range(4):
                    for eh in range(2):
                        idx = tb4 * 2 + eh
                        pb = f"ps_v{idx % 2}"
                        gate("tensor", pb)
                        if g == 0 and idx == 0:
                            wait("tensor", "Gc", 16)
                        for dc in range(8):
                            mm(tn[xnT][:, dc * 512 + tb4 * 128: dc * 512 + (tb4 + 1) * 128],
                               tn["w3"][:, dc * 1024 + eh * 512: dc * 1024 + (eh + 1) * 512],
                               ap(pb), start=(dc == 0), stop=False)
                        mmc = mm(tn["ones_rb"][0:1, :],
                                 tn["bvb"][0:1, eh * 512:(eh + 1) * 512],
                                 ap(pb), start=False, stop=True)
                        if idx == 0:
                            gate("scalar", "vtok")
                        wait("scalar", "M", mmc)
                        s_v = sop(lambda e, o=tn["vtok"][:, tb4 * 1024 + eh * 512: tb4 * 1024 + (eh + 1) * 512],
                                  p=ap(pb): e.copy(o, p))
                        setfree(pb, ("S", s_v))
                v_evac_s = cnt["S"]
                mid[("xnT_free", g)] = cnt["M"]
                setfree(xnT, ("M", cnt["M"]))
            if P1PART >= 4:
                for dkb in range(8):
                    vop(lambda e, o=ap("ksg", slice(g * 8 + dkb, g * 8 + dkb + 1)),
                        a=tn["kT"][:, dkb * 512:(dkb + 1) * 512]:
                        e.reduce_sum(o, a, AX))
                ks_v = cnt["V"]
                wait("scalar", "V", qk_last_v[(g, "qT")])
                q_spq = dma("scalar", "Qq", sp_q[g * 128:(g + 1) * 128, :], ap("qT"))
                wait("scalar", "S", v_evac_s)
                q_spv = dma("scalar", "Qv", sp_v[g * 128:(g + 1) * 128, :], ap("vtok"))
                if g == 0:
                    gate("sync", "ktok")
                wait("sync", "V", qk_last_v[(g, "kT")])
                for dkb in range(8):
                    dest = tn["ktok"].ap().rearrange("p (s d) -> p s d", s=4)[:, :, dkb * 128:(dkb + 1) * 128]
                    dmat("sync", "KT", dest, tn["kT"][:, dkb * 512:(dkb + 1) * 512])
                kt_L = cnt["KT"]
            if P1PART >= 5:
                aT = f"aT{gp}"
                gate("vector", aT)
                a_evac = []
                for sb in range(4):
                    pb = "ps_a" if sb % 2 == 0 else "ps_s0"
                    gate("tensor", pb)
                    if sb == 0:
                        wait("tensor", "V", qk_last_v[(g, "qT")])
                        wait("tensor", "V", qk_last_v[(g, "kT")])
                    for dc in range(8):
                        mmc = mm(tn["kT"][:, dc * 512 + sb * 128: dc * 512 + (sb + 1) * 128],
                                 tn["qT"][:, dc * 512 + sb * 128: (dc + 1) * 512],
                                 tn[pb][:, sb * 128:512], start=(dc == 0), stop=(dc == 7))
                    wait("vector", "M", mmc)
                    vop(lambda e, o=tn[aT][:, PK[sb]:PK[sb] + 128],
                        p=tn[pb][:, sb * 128:sb * 128 + 128], m=ap("mask_ut"):
                        e.tensor_tensor(o, p, m, ALU.mult))
                    if sb < 3:
                        v_a = vop(lambda e, o=tn[aT][:, PK[sb] + 128:PK[sb] + PW[sb]],
                                  p=tn[pb][:, sb * 128 + 128:512]:
                                  e.tensor_copy(o, p))
                    else:
                        v_a = cnt["V"]
                    setfree(pb, ("V", v_a))
                    a_evac.append(v_a)
                wait("scalar", "V", a_evac[-1])
                q_spa = dma("scalar", f"Qa{g % 2}", sp_a[g * 128:(g + 1) * 128, :], tn[aT][:, :])
                setfree(aT, (f"Qa{g % 2}", q_spa))
                setfree("qT", ("Qq", q_spq), ("M", cnt["M"]))
                setfree("kT", ("M", cnt["M"]), ("V", ks_v), ("KT", kt_L))
            if P1PART >= 6:
                for dkb in range(8):
                    for eh in range(2):
                        idx = dkb * 2 + eh
                        pb = f"ps_s{idx % 2}"
                        gate("tensor", pb)
                        if idx == 0:
                            wait("tensor", "KT", kt_L)
                            wait("tensor", "S", v_evac_s)
                        for sb in range(4):
                            mmc = mm(tn["ktok"][:, sb * 1024 + dkb * 128: sb * 1024 + (dkb + 1) * 128],
                                     tn["vtok"][:, sb * 1024 + eh * 512: sb * 1024 + (eh + 1) * 512],
                                     ap(pb), start=(sb == 0), stop=(sb == 3))
                        fr = []
                        if S_SUB >= 1:
                            wait("vector", "M", mmc)
                            v_kv = vop(lambda e, o=tn["kvr"][:, dkb * 1024 + eh * 512: dkb * 1024 + (eh + 1) * 512],
                                       p=ap(pb): e.tensor_tensor(o, o, p, ALU.add))
                            fr.append(("V", v_kv))
                        if S_SUB >= 2:
                            sg = f"rT{idx % 2}"
                            gate("vector", sg + "_s")
                            wait("vector", "M", mmc)
                            s_c = vop(lambda e, o=tn[sg][:, 0:512], p=ap(pb): e.tensor_copy(o, p))
                            fr.append(("V", s_c))
                        setfree(pb, *fr)
                        if S_SUB >= 3:
                            wait("scalar", "V", s_c)
                            q_s = dma("scalar", f"Qs{idx % 2}",
                                      sp_s[g * 1024 + dkb * 128: g * 1024 + (dkb + 1) * 128,
                                           eh * 512:(eh + 1) * 512],
                                      tn[sg][:, 0:512])
                            setfree(sg + "_s", (f"Qs{idx % 2}", q_s))
                setfree("ktok", ("M", cnt["M"]))
                setfree("vtok", ("M", cnt["M"]), ("Qv", q_spv))
                mid[("kvr_v", g)] = cnt["V"]

        if STAGE >= 3:
            vop(lambda e, o=ap("kst"), a=ap("ksg", slice(0, 8)), b=ap("ksg", slice(8, 16)):
                e.tensor_tensor(o, a, b, ALU.add))
            for g2 in range(2, 8):
                vop(lambda e, o=ap("kst"), a=ap("kst"), b=ap("ksg", slice(g2 * 8, g2 * 8 + 8)):
                    e.tensor_tensor(o, a, b, ALU.add))
            kst_v = cnt["V"]
            gate("tensor", "ps_a")
            wait("tensor", "V", cnt["V"])
            mc = mm(ap("amx0"), ap("ident"), tn["ps_a"][0:1, 0:128], True, True)
            wait("vector", "M", mc)
            vop(lambda e, o=rowap("td", 0, slice(0, 128)), a=tn["ps_a"][0:1, 0:128]:
                e.tensor_copy(o, a))
            v_mx = vop(lambda e, o=rowap("smf", 0, slice(0, 1)), a=rowap("td", 0, slice(0, 128)):
                       e.reduce_max(o, a, AX))
            setfree("ps_a", ("V", cnt["V"]))
            wait("scalar", "V", v_mx)
            q_mx = dma("scalar", "Qm", mx_in, rowap("smf", 0, slice(0, 1)))
            gate("vector", "scr0")
            vop(lambda e, o=ap("scr0", slice(0, 1)), a=ap("tn32"): e.reduce_sum(o, a, AX))
            v_nc = vop(lambda e, o=ap("scr0", slice(1, 2)), a=ap("tc32"): e.reduce_sum(o, a, AX))
            gate("tensor", "ps_a")
            wait("tensor", "V", v_nc)
            mm(ap("scr0", slice(0, 1)), ap("ident"), tn["ps_a"][0:1, 0:128], True, True)
            mc2 = mm(ap("scr0", slice(1, 2)), ap("ident"), tn["ps_a"][0:1, 128:256], True, True)
            wait("vector", "M", mc2)
            vop(lambda e, o=rowap("td", 0, slice(0, 256)), a=tn["ps_a"][0:1, 0:256]:
                e.tensor_copy(o, a))
            vop(lambda e, o=rowap("smf", 0, slice(5, 6)), a=rowap("td", 0, slice(0, 128)):
                e.reduce_sum(o, a, AX))
            vop(lambda e, o=rowap("smf", 0, slice(6, 7)), a=rowap("td", 0, slice(128, 256)):
                e.reduce_sum(o, a, AX))
            setfree("ps_a", ("V", cnt["V"]))
            vop(lambda e, o=rowap("smf", 0, slice(7, 8)), a=rowap("smf", 0, slice(6, 7)):
                e.tensor_scalar(o, a, 1.0, None, ALU.max))
            vop(lambda e, o=rowap("smf", 0, slice(8, 9)), a=rowap("smf", 0, slice(7, 8)):
                e.reciprocal(o, a))
            v_al = vop(lambda e, o=rowap("smf", 0, slice(9, 10)), a=rowap("smf", 0, slice(5, 6)),
                       b=rowap("smf", 0, slice(8, 9)): e.tensor_tensor(o, a, b, ALU.mult))
            wait("gpsimd", "V", mid[("kvr_v", NGRUN - 1)])
            for dkb in range(8):
                dma("gpsimd", "Gc", cc_in[dkb * 128:(dkb + 1) * 128, 0:1024],
                    tn["kvr"][:, dkb * 1024:(dkb + 1) * 1024])
            gate("tensor", "ps_s1")
            wait("tensor", "V", kst_v)
            mc_ks = mm(ap("kst"), ap("ident"), tn["ps_s1"][0:8, 0:128], True, True)
            wait("vector", "M", mc_ks)
            gate("vector", "rT0_s")
            v_ksr = vop(lambda e, o=tn["rT0"][0:8, 0:128], a=tn["ps_s1"][0:8, 0:128]:
                        e.tensor_copy(o, a))
            setfree("ps_s1", ("V", v_ksr))
            wait("gpsimd", "V", v_ksr)
            dma("gpsimd", "Gc",
                cc_in[1024:1025, :].rearrange("r (d p) -> (r d) p", d=8),
                tn["rT0"][0:8, 0:128])
            g_cc = cnt["Gc"]
            setfree("rT0_s", ("Gc", g_cc))
            wait("gpsimd", "Gc", g_cc)
            wait("gpsimd", "Qm", q_mx)
            cc_done = gop(lambda e: e.collective_compute(
                "AllReduce", ALU.max,
                replica_groups=[list(range(8))],
                ins=[mx_in_t.ap()], outs=[mx_out_t.ap()]))
            emit("gpsimd", lambda e: e.collective_compute(
                "AllGather", ALU.bypass,
                replica_groups=[[0, 1, 2, 3], [4, 5, 6, 7]],
                ins=[cc_in_t.ap()], outs=[cc_out_t.ap()]).then_inc(tn["sem_C"], 1))
            cnt["C"] += 1
            ag_done = cnt["C"]

            for ob in range(8):
                par = ob % 2
                st = f"xnb{par}"
                gate("gpsimd", st)
                gc = dma("gpsimd", f"Gx{par}", ap(st), wo_h[ob * 128:(ob + 1) * 128, :])
                wait("sync", f"Gx{par}", gc)
                wait("sync", "M", mid[("xnT_free", NGRUN - 1)])
                dest = tn["w1"].ap().rearrange("p (d o) -> p d o", d=8)[:, :, ob * 128:(ob + 1) * 128]
                lc = dmat("sync", f"WP{par}", dest, ap(st))
                setfree(st, (f"WP{par}", lc))
            wo_WP = (cnt["WP0"], cnt["WP1"])
            for ob in range(8):
                par = ob % 2
                st = f"xnb{par}"
                gate("sync", st)
                wait("sync", "Qt", cnt["Qt"])
                ld = dma("sync", f"Lt{par}", ap(st), sp_t[ob * 128:(ob + 1) * 128, :])
                wait("scalar", f"Lt{par}", ld)
                dest = tn["w2"].ap().rearrange("p (d o) -> p d o", d=8)[:, :, ob * 128:(ob + 1) * 128]
                lc = dmat("scalar", f"Qw{par}", dest, ap(st))
                setfree(st, (f"Qw{par}", lc))
            w2_Q = (cnt["Qw0"], cnt["Qw1"])

            wait("sync", "C", cc_done)
            l_mx = dma("sync", "Lm", rowap("smf", 0, slice(10, 11)), mx_out)
            wait("vector", "Lm", l_mx)
            vop(lambda e, o=rowap("smf", 0, slice(11, 12)), a=rowap("smf", 0, slice(10, 11)):
                e.tensor_scalar(o, a, 1e-8, None, ALU.max))
            vop(lambda e, o=rowap("smf", 0, slice(12, 13)), a=rowap("smf", 0, slice(11, 12)):
                e.tensor_scalar(o, a, 1.0 / 127.0, None, ALU.mult))
            vop(lambda e, o=rowap("smf", 0, slice(13, 14)), a=rowap("smf", 0, slice(12, 13)):
                e.reciprocal(o, a))
            wait("vector", "V", v_al)
            vop(lambda e, o=rowap("smf", 0, slice(14, 15)), a=rowap("smf", 0, slice(9, 10)),
                b=rowap("smf", 0, slice(12, 13)): e.tensor_tensor(o, a, b, ALU.mult))
            v_as = vop(lambda e, o=rowap("smf", 0, slice(15, 16)), a=rowap("smf", 0, slice(14, 15)):
                       e.reciprocal(o, a))
            gate("tensor", "ps_s1")
            wait("tensor", "V", v_as)
            mc3 = mm(tn["mask_ut"][0:1, :], rowap("smf", 0, slice(13, 15)), tn["ps_s1"][:, 0:2], True, True)
            wait("vector", "M", mc3)
            v_bcs = vop(lambda e, o=ap("bcs"), a=tn["ps_s1"][:, 0:2]: e.tensor_copy(o, a))
            setfree("ps_s1", ("V", v_bcs))
            l_bo = dma("sync", "MS", rowap("xnf0", 0, slice(0, 1024)), bo_h)
            wait("vector", "MS", l_bo)
            vop(lambda e, o=rowap("xnf1", 0, slice(0, 1024)), a=rowap("xnf0", 0, slice(0, 1024)),
                s=rowap("smf", 0, slice(15, 16)): e.tensor_scalar(o, a, s, None, ALU.mult))
            vop(lambda e, o=tn["boeff"][0:1, :], a=rowap("xnf1", 0, slice(0, 1024)):
                e.tensor_copy(o, a))
            vop(lambda e, o=rowap("xnf0", 0, slice(0, 1024)), a=tn["boeff"][0:1, :]:
                e.tensor_copy(o, a))
            v_bo = vop(lambda e, o=tn["bvb"][0:1, :], a=rowap("xnf1", 0, slice(0, 1024)),
                       b=rowap("xnf0", 0, slice(0, 1024)): e.tensor_tensor(o, a, b, ALU.subtract))
            wait("sync", "M", mid[("xnT_free", NGRUN - 1)])
            wait("sync", "C", ag_done)
            wait("gpsimd", "C", ag_done)
            wait("gpsimd", "M", mid[("xnT_free", NGRUN - 1)])
            gate("sync", "qT")
            wait("vector", "Gx0", cnt["Gx0"])
            wait("vector", "Gx1", cnt["Gx1"])
            v_kv = 0
            for dkb in range(8):
                if dkb > 0:
                    wait("sync", "V", v_kv)
                for j in range(4):
                    dma("sync", "PS", tn["qT"][:, j * 1024:(j + 1) * 1024],
                        cc_out[j * 1025 + dkb * 128: j * 1025 + (dkb + 1) * 128, 0:1024])
                l_g = cnt["PS"]
                wait("vector", "PS", l_g)
                vop(lambda e, o=ap("u"), a=tn["qT"][:, 0:1024], m=ap("mselc", slice(0, 1)):
                    e.tensor_scalar(o, a, m, None, ALU.mult))
                vop(lambda e, o=ap("xnf0"), a=tn["qT"][:, 1024:2048], m=ap("mselc", slice(1, 2)), c=ap("u"):
                    e.scalar_tensor_tensor(o, a, m, c, ALU.mult, ALU.add))
                vop(lambda e, o=ap("u"), a=tn["qT"][:, 2048:3072], m=ap("mselc", slice(2, 3)), c=ap("xnf0"):
                    e.scalar_tensor_tensor(o, a, m, c, ALU.mult, ALU.add))
                v_kv = vop(lambda e, o=tn["kvr"][:, dkb * 1024:(dkb + 1) * 1024],
                           a=tn["qT"][:, 3072:4096], m=ap("mselc", slice(3, 4)), c=ap("u"):
                           e.scalar_tensor_tensor(o, a, m, c, ALU.mult, ALU.add))
                wait("gpsimd", "V", v_kv)
                dma("gpsimd", "Gk", tn["w3"][:, dkb * 1024:(dkb + 1) * 1024],
                    tn["kvr"][:, dkb * 1024:(dkb + 1) * 1024])
            kvb_G = cnt["Gk"]
            setfree("qT", ("V", cnt["V"]))
            gate("sync", "kT")
            wait("sync", "V", v_kv)
            for j in range(4):
                dma("sync", "PS", tn["kT"][0:1, j * 1024:(j + 1) * 1024],
                    cc_out[j * 1025 + 1024: j * 1025 + 1025, 0:1024])
            ksr_L = cnt["PS"]
            gate("tensor", "ps_s1")
            wait("tensor", "PS", ksr_L)
            for j in range(4):
                for dkb in range(8):
                    col = j * 8 + dkb
                    mc_k = mm(tn["kT"][0:1, j * 1024 + dkb * 128: j * 1024 + dkb * 128 + 128],
                              tn["ones_cb"][0:1, 0:1],
                              tn["ps_s1"][:, col:col + 1], True, True)
            setfree("kT", ("M", mc_k))
            wait("vector", "M", mc_k)
            v_ksl = vop(lambda e, o=ap("tc32"), a=tn["ps_s1"][:, 0:32]: e.tensor_copy(o, a))
            setfree("ps_s1", ("V", v_ksl))
            vop(lambda e, o=ap("ksB"), a=ap("tc32", slice(0, 8)), m=ap("mselc", slice(0, 1)):
                e.tensor_scalar(o, a, m, None, ALU.mult))
            vop(lambda e, o=ap("ksA"), a=ap("tc32", slice(8, 16)), m=ap("mselc", slice(1, 2)), c=ap("ksB"):
                e.scalar_tensor_tensor(o, a, m, c, ALU.mult, ALU.add))
            vop(lambda e, o=ap("ksB"), a=ap("tc32", slice(16, 24)), m=ap("mselc", slice(2, 3)), c=ap("ksA"):
                e.scalar_tensor_tensor(o, a, m, c, ALU.mult, ALU.add))
            vop(lambda e, o=ap("ksA"), a=ap("tc32", slice(24, 32)), m=ap("mselc", slice(3, 4)), c=ap("ksB"):
                e.scalar_tensor_tensor(o, a, m, c, ALU.mult, ALU.add))
            v_ksc = vop(lambda e, o=ap("kscb"), a=ap("ksA"): e.tensor_copy(o, a))

            bnd = dict(cnt)
            bnd_free = {
                "kT": ["M", "KT", "V"], "ktok": ["M"], "vtok": ["M", "Qv"],
                "aT0": ["Qa0"], "aT1": ["Qa1"], "xnT0": ["M"], "xnT1": ["M"],
                "xnf0": ["V", "Gx0"], "xnf1": ["V", "Gx1"],
                "xnb0": ["WP0", "Qw0"], "xnb1": ["WP1", "Qw1"],
                "x2t0": ["Qx0"], "x2t1": ["Qx1"],
                "xg0": ["S", "V"], "xg1": ["S", "V"], "xg2": ["S", "V"],
                "rT0": ["Qs0", "Gc"], "rT1": ["Qs1"],
                "rT0_s": ["Qs0", "Gc"], "rT1_s": ["Qs1"],
                "mn0": ["S"], "mn1": ["S"], "Rb0": ["V"], "Rb1": ["V"],
            }
            for key, ss in bnd_free.items():
                free[key] = {s: bnd[s] for s in ss}

        denr = rowap("u", 0, slice(0, 512))
        recr = rowap("u", 0, slice(512, 1024))
        inv_s_ap = ap("bcs", slice(0, 1))
        as_ap = ap("bcs", slice(1, 2))
        invas_ap = rowap("smf", 0, slice(15, 16))
        attnT_v = {}
        ldL = {}
        kscv = {0: v_ksc if STAGE >= 3 else 0}
        kvbG = {0: kvb_G if STAGE >= 3 else 0}
        denM = {-1: 0}
        for g in (range(NG) if STAGE >= 4 else []):
            pb2 = (g + 1) % 2
            qb = ["qT", "kT"][pb2]
            vb = ["ktok", "vtok"][pb2]
            ab = ["aT0", "aT1"][pb2]
            at = f"xnT{g % 2}"
            mnb = f"mn{g % 2}"
            ksb = ["kscb", "kscb2"][g % 2]
            ldsem = f"Ld{pb2}"
            if g == 0:
                gate("sync", qb)
                dma("sync", ldsem, ap(qb), sp_q[0:128, :])
                gate("sync", vb)
                dma("sync", ldsem, ap(vb), sp_v[0:128, :])
                gate("sync", ab)
                ldL[0] = dma("sync", ldsem, ap(ab), sp_a[0:128, :])
            gate("tensor", "ps_a")
            wait("tensor", ldsem, ldL[g])
            wait("tensor", "V", kscv[g])
            for sb in range(4):
                mm(tn["ones_cb"][:, 0:1], tn[ab][:, PK[sb]:PK[sb] + PW[sb]],
                   tn["ps_a"][0:1, sb * 128:512], start=(sb == 0), stop=False)
            for dc in range(8):
                mmc = mm(ap(ksb, slice(dc, dc + 1)), tn[qb][:, dc * 512:(dc + 1) * 512],
                         tn["ps_a"][0:1, 0:512], start=False, stop=(dc == 7))
            denM[g] = mmc
            wait("vector", "M", mmc)
            vop(lambda e, o=denr, a=tn["ps_a"][0:1, 0:512]:
                e.tensor_scalar(o, a, EPS_ATT, None, ALU.add))
            vop(lambda e, o=recr, a=denr: e.reciprocal(o, a))
            gate("vector", "rrowb")
            gate("vector", "sgb")
            vop(lambda e, o=denr, a=recr, s=invas_ap:
                e.tensor_scalar(o, a, s, None, ALU.mult))
            vop(lambda e, o=tn["rrowb"][0:1, :], a=denr: e.tensor_copy(o, a))
            vop(lambda e, o=recr, a=tn["rrowb"][0:1, :]: e.tensor_copy(o, a))
            v_rr = vop(lambda e, o=tn["sgb"][0:1, 0:512], a=denr, b=recr:
                       e.tensor_tensor(o, a, b, ALU.subtract))
            setfree("ps_a", ("V", cnt["V"]))
            gate("tensor", "ps_s0")
            wait("tensor", "V", v_rr)
            mm(tn["ones_rb"][0:1, :], tn["rrowb"][0:1, :], ap("ps_s0"), True, False)
            mcb = mm(tn["ones_rb"][0:1, :], tn["sgb"][0:1, 0:512], ap("ps_s0"), False, True)
            setfree("rrowb", ("M", mcb))
            setfree("sgb", ("M", mcb))
            wait("vector", "M", mcb)
            gate("vector", mnb)
            v_bc = vop(lambda e, o=ap(mnb), a=ap("ps_s0"): e.tensor_copy(o, a))
            setfree("ps_s0", ("V", v_bc))
            if g + 1 < NG:
                qb2 = ["qT", "kT"][g % 2]
                vb2 = ["ktok", "vtok"][g % 2]
                ab2 = ["aT0", "aT1"][g % 2]
                ldsem2 = f"Ld{g % 2}"
                gate("sync", qb2)
                dma("sync", ldsem2, ap(qb2), sp_q[(g + 1) * 128:(g + 2) * 128, :])
                gate("sync", vb2)
                dma("sync", ldsem2, ap(vb2), sp_v[(g + 1) * 128:(g + 2) * 128, :])
                gate("sync", ab2)
                ldL[g + 1] = dma("sync", ldsem2, ap(ab2), sp_a[(g + 1) * 128:(g + 2) * 128, :])
            for eb in range(8):
                pq = f"ps_q{eb % 2}"
                gate("tensor", pq)
                if eb == 0:
                    wait("tensor", "Gk", kvbG[g])
                for dc in range(8):
                    mm(tn["w3"][:, dc * 1024 + eb * 128: dc * 1024 + (eb + 1) * 128],
                       tn[qb][:, dc * 512:(dc + 1) * 512],
                       ap(pq), start=(dc == 0), stop=False)
                for sb in range(4):
                    mmc = mm(tn[vb][:, sb * 1024 + eb * 128: sb * 1024 + (eb + 1) * 128],
                             tn[ab][:, PK[sb]:PK[sb] + PW[sb]],
                             tn[pq][:, sb * 128:512], start=False, stop=(sb == 3))
                wait("vector", "M", mmc)
                if eb == 0:
                    gate("vector", at)
                v_at = vop(lambda e, o=tn[at][:, eb * 512:(eb + 1) * 512], p=ap(pq), b=ap(mnb):
                           e.tensor_tensor(o, p, b, ALU.mult))
                setfree(pq, ("V", v_at))
                attnT_v[(g, eb)] = v_at
            last_yq_M = cnt["M"]
            setfree(qb, ("M", cnt["M"]))
            setfree(vb, ("M", cnt["M"]))
            setfree(ab, ("M", cnt["M"]))
            if g + 1 < NG:
                for u16 in range(16):
                    dkb, eh = divmod(u16, 2)
                    sg = f"Rb{u16 % 2}"
                    gate("sync", sg)
                    ldS = dma("sync", f"Sg{u16 % 2}", tn[sg][:, 0:512],
                              sp_s[g * 1024 + dkb * 128: g * 1024 + (dkb + 1) * 128,
                                   eh * 512:(eh + 1) * 512])
                    wait("vector", f"Sg{u16 % 2}", ldS)
                    if u16 == 0:
                        wait("vector", "Gk", kvbG[g])
                    v_ad = vop(lambda e, o=tn["kvr"][:, dkb * 1024 + eh * 512: dkb * 1024 + (eh + 1) * 512],
                               a=tn[sg][:, 0:512]: e.tensor_tensor(o, o, a, ALU.add))
                    setfree(sg, ("V", v_ad))
                kv_adds_v = cnt["V"]
                src, dst_ = ("ksA", "ksB") if (g + 1) % 2 == 1 else ("ksB", "ksA")
                vop(lambda e, o=ap(dst_), a=ap(src), b=ap("ksg", slice(g * 8, (g + 1) * 8)):
                    e.tensor_tensor(o, a, b, ALU.add))
                wait("vector", "M", denM[g - 1])
                kscv[g + 1] = vop(lambda e, o=ap(["kscb", "kscb2"][(g + 1) % 2]),
                                  a=ap(dst_): e.tensor_copy(o, a))
                wait("gpsimd", "V", kv_adds_v)
                wait("gpsimd", "M", last_yq_M)
                for dkb in range(8):
                    dma("gpsimd", "Gk", tn["w3"][:, dkb * 1024:(dkb + 1) * 1024],
                        tn["kvr"][:, dkb * 1024:(dkb + 1) * 1024])
                kvbG[g + 1] = cnt["Gk"]
            for i in range(4):
                ti = g * 4 + i
                x2 = f"x2t{i % 2}"
                rb = f"xnb{i % 2}"
                rt = f"rT{i % 2}"
                slot = f"xg{ti % 3}"
                gate("sync", x2)
                l_x2 = dma("sync", f"X2{i % 2}", ap(x2), sp_x2[ti * 128:(ti + 1) * 128, :])
                gate("vector", rb)
                wait("vector", f"X2{i % 2}", l_x2)
                for h in range(2):
                    v_t1 = vop(lambda e, o=ap("i32h"), a=tn[x2][:, h * 512:(h + 1) * 512], s=inv_s_ap:
                               e.tensor_scalar(o, a, s, None, ALU.mult))
                    v_rb = vop(lambda e, o=tn[rb][:, h * 512:(h + 1) * 512], a=ap("i32h"):
                               e.tensor_copy(o, a))
                setfree(x2, ("V", v_rb))
                gate("scalar", rt)
                wait("scalar", "V", v_rb)
                dest = tn[rt].ap().rearrange("p (d t) -> p d t", d=8)[:, :, 0:128]
                q_rt = dmat("scalar", f"Qr{i % 2}", dest, ap(rb))
                setfree(rb, (f"Qr{i % 2}", q_rt))
                gate("sync", slot)
                l_x = dma("sync", f"Lx{ti % 3}", ap(slot), x_h[ti * 128:(ti + 1) * 128, :])
                for oh in range(2):
                    idx = (2 * i + oh) % 2
                    pv = f"ps_v{idx}"
                    gate("tensor", pv)
                    if i == 0 and oh == 0:
                        for eb in range(8):
                            wait("tensor", "V", attnT_v[(g, eb)])
                        wait("tensor", "Qw0", w2_Q[0])
                        wait("tensor", "Qw1", w2_Q[1])
                        wait("tensor", "WP0", wo_WP[0])
                        wait("tensor", "WP1", wo_WP[1])
                        wait("tensor", "V", v_bo)
                    for ac in range(8):
                        mm(tn[at][:, ac * 512 + i * 128: ac * 512 + (i + 1) * 128],
                           tn["w1"][:, ac * 1024 + oh * 512: ac * 1024 + (oh + 1) * 512],
                           ap(pv), start=(ac == 0), stop=False)
                    if oh == 0:
                        wait("tensor", f"Qr{i % 2}", q_rt)
                    for ic in range(8):
                        mm(tn[rt][:, ic * 128:(ic + 1) * 128],
                           tn["w2"][:, ic * 1024 + oh * 512: ic * 1024 + (oh + 1) * 512],
                           ap(pv), start=False, stop=False)
                    mm(tn["ones_rb"][0:1, :], tn["boeff"][0:1, oh * 512:(oh + 1) * 512],
                       ap(pv), start=False, stop=False)
                    mmc = mm(tn["ones_rb"][0:1, :], tn["bvb"][0:1, oh * 512:(oh + 1) * 512],
                             ap(pv), start=False, stop=True)
                    if oh == 0:
                        gate("vector", f"xnf{i % 2}")
                    wait("vector", "M", mmc)
                    wait("vector", f"Lx{ti % 3}", l_x)
                    v_out = vop(lambda e, o=tn[f"xnf{i % 2}"][:, oh * 512:(oh + 1) * 512],
                                p=ap(pv), s=as_ap, x=tn[slot][:, oh * 512:(oh + 1) * 512]:
                                e.scalar_tensor_tensor(o, p, s, x, ALU.mult, ALU.add))
                    setfree(pv, ("V", v_out))
                setfree(slot, ("V", v_out))
                wait("scalar", "V", v_out)
                q_o = dma("scalar", f"Qo{i % 2}", out_h[ti * 128:(ti + 1) * 128, :], ap(f"xnf{i % 2}"))
                setfree(f"xnf{i % 2}", (f"Qo{i % 2}", q_o))
                setfree(rt, ("M", cnt["M"]))
            setfree(at, ("M", cnt["M"]))

        wait("gpsimd", "Qo0", cnt["Qo0"])
        wait("gpsimd", "Qo1", cnt["Qo1"])

        @block.sync
        def _(eng):
            for f in plan["sync"]:
                f(eng)

        @block.scalar
        def _(eng):
            for f in plan["scalar"]:
                f(eng)

        @block.vector
        def _(eng):
            for f in plan["vector"]:
                f(eng)

        @block.tensor
        def _(eng):
            for f in plan["tensor"]:
                f(eng)

        @block.gpsimd
        def _(eng):
            for f in plan["gpsimd"]:
                f(eng)

    return nc


_NC_CACHE = None


def kernel(L0_attn_chunk, L0_attn_Wq, L0_attn_bq, L0_attn_Wk, L0_attn_bk,
           L0_attn_Wv, L0_attn_bv, L0_attn_Wo, L0_attn_bo,
           L0_ffn_weight, L0_ffn_gamma, L0_ffn_beta):
    global _NC_CACHE
    if _NC_CACHE is None:
        _NC_CACHE = build()
    nc = _NC_CACHE
    c = np.ascontiguousarray

    x = np.asarray(L0_attn_chunk, dtype=np.float32)
    wq = c(np.asarray(L0_attn_Wq, np.float32))
    wk = c(np.asarray(L0_attn_Wk, np.float32))
    wv = c(np.asarray(L0_attn_Wv, np.float32))
    wo = c(np.asarray(L0_attn_Wo, np.float32))
    wf = c(np.asarray(L0_ffn_weight, np.float32))
    bqc = c(np.asarray(L0_attn_bq, np.float32).reshape(8, 128).T)
    bkc = c(np.asarray(L0_attn_bk, np.float32).reshape(8, 128).T)
    bvr = c(np.asarray(L0_attn_bv, np.float32).reshape(1, D))
    bor = c(np.asarray(L0_attn_bo, np.float32).reshape(1, D))
    gamb = c(np.broadcast_to(np.asarray(L0_ffn_gamma, np.float32).reshape(1, D), (128, D)))
    betb = c(np.broadcast_to(np.asarray(L0_ffn_beta, np.float32).reshape(1, D), (128, D)))

    identc = np.eye(128, dtype=np.float32)
    mtric = c(np.triu(np.ones((128, 128), np.float32)))
    in_maps = []
    for core in range(NCORE):
        b, q = core // 4, core % 4
        msel = np.zeros((128, 4), np.float32)
        msel[:, :q] = 1.0
        in_maps.append({
            "x": c(x[b, q * TC:(q + 1) * TC, :]),
            "wq": wq, "wk": wk, "wv": wv, "wo": wo, "wf": wf,
            "bqc": bqc, "bkc": bkc, "bvr": bvr, "bor": bor,
            "gamb": gamb, "betb": betb, "msel": msel,
            "identc": identc, "mtric": mtric,
        })
    res = run_bass_kernel_spmd(nc, in_maps, core_ids=list(range(NCORE)))
    out = np.empty((B, T, D), np.float32)
    for core in range(NCORE):
        b, q = core // 4, core % 4
        out[b, q * TC:(q + 1) * TC, :] = res.results[core]["out"]
    return out


# version 08
# speedup vs baseline: 1.0217x; 1.0024x over previous
import numpy as np
import concourse.bass as bass
import concourse.mybir as mybir
from concourse.bass_utils import run_bass_kernel_spmd

F32 = mybir.dt.float32
BF16 = mybir.dt.bfloat16
I32 = mybir.dt.int32
ALU = mybir.AluOpType
ACT = mybir.ActivationFunctionType
AX = mybir.AxisListType.X

STAGE = 4
NGRUN = 8
P1PART = 6
S_SUB = 3
TERN_ON = True
NCORE = 8
B, T, D = 2, 16384, 1024
TC = T // 4
NT = TC // 128
NG = 8
PK = [0, 512, 896, 1152]
PW = [512, 384, 256, 128]
EPS1 = 1e-5
EPS_ATT = 1e-6


def build():
    nc = bass.Bass("TRN2", target_bir_lowering=False)

    x_h = nc.dram_tensor("x", [TC, D], F32, kind="ExternalInput").ap()
    wq_h = nc.dram_tensor("wq", [D, D], F32, kind="ExternalInput").ap()
    wk_h = nc.dram_tensor("wk", [D, D], F32, kind="ExternalInput").ap()
    wv_h = nc.dram_tensor("wv", [D, D], F32, kind="ExternalInput").ap()
    wo_h = nc.dram_tensor("wo", [D, D], F32, kind="ExternalInput").ap()
    wf_h = nc.dram_tensor("wf", [D, D], F32, kind="ExternalInput").ap()
    bqc_h = nc.dram_tensor("bqc", [128, 8], F32, kind="ExternalInput").ap()
    bkc_h = nc.dram_tensor("bkc", [128, 8], F32, kind="ExternalInput").ap()
    bv_h = nc.dram_tensor("bvr", [1, D], F32, kind="ExternalInput").ap()
    bo_h = nc.dram_tensor("bor", [1, D], F32, kind="ExternalInput").ap()
    gam_h = nc.dram_tensor("gamb", [128, D], F32, kind="ExternalInput").ap()
    bet_h = nc.dram_tensor("betb", [128, D], F32, kind="ExternalInput").ap()
    msel_h = nc.dram_tensor("msel", [128, 4], F32, kind="ExternalInput").ap()
    identh = nc.dram_tensor("identc", [128, 128], F32, kind="ExternalInput").ap()
    mtrih = nc.dram_tensor("mtric", [128, 128], F32, kind="ExternalInput").ap()
    out_h = nc.dram_tensor("out", [TC, D], F32, kind="ExternalOutput").ap()

    sp_q = nc.dram_tensor("sp_q", [NG * 128, 4096], BF16).ap()
    sp_v = nc.dram_tensor("sp_v", [NG * 128, 4096], BF16).ap()
    sp_a = nc.dram_tensor("sp_a", [NG * 128, 1280], BF16).ap()
    sp_x2 = nc.dram_tensor("sp_x2", [TC, D], F32).ap()
    sp_s = nc.dram_tensor("sp_s", [NG * 1024, 1024], BF16).ap()
    sp_t = nc.dram_tensor("sp_t", [1024, 1024], BF16).ap()
    cc_in_t = nc.dram_tensor("cc_in", [1025, 1024], BF16)
    cc_out_t = nc.dram_tensor("cc_out", [4 * 1025, 1024], BF16)
    mx_in_t = nc.dram_tensor("mx_in", [1, 1], F32)
    mx_out_t = nc.dram_tensor("mx_out", [1, 1], F32)
    cc_in, cc_out = cc_in_t.ap(), cc_out_t.ap()
    mx_in, mx_out = mx_in_t.ap(), mx_out_t.ap()

    SB = [
        ("w1", 8192, BF16), ("w2", 8192, BF16), ("w3", 8192, BF16),
        ("kvr", 8192, F32), ("gbc", 1024, F32), ("bbc", 1024, F32),
        ("xg0", 1024, F32), ("xg1", 1024, F32), ("xg2", 1024, F32),
        ("xnf0", 1024, F32), ("xnf1", 1024, F32),
        ("xnb0", 1024, BF16), ("xnb1", 1024, BF16),
        ("xnT0", 4096, BF16), ("xnT1", 4096, BF16),
        ("qT", 4096, BF16), ("kT", 4096, BF16),
        ("ktok", 4096, BF16), ("vtok", 4096, BF16),
        ("aT0", 1280, BF16), ("aT1", 1280, BF16),
        ("mn0", 512, F32), ("mn1", 512, F32),
        ("Rb0", 512, BF16), ("Rb1", 512, BF16), ("sgb", 512, BF16),
        ("u", 1024, F32), ("i32h", 512, I32),
        ("x2t0", 1024, F32), ("x2t1", 1024, F32),
        ("rT0", 1024, BF16), ("rT1", 1024, BF16),
        ("bqc", 8, F32), ("bkc", 8, F32), ("bvb", 1024, BF16), ("boeff", 1024, BF16),
        ("mselc", 4, F32), ("stats", 96, F32), ("scr0", 8, F32), ("scr1", 8, F32),
        ("kscb2", 8, BF16),
        ("ksg", 64, F32), ("kst", 8, F32), ("ksA", 8, F32), ("ksB", 8, F32),
        ("kscb", 8, BF16), ("amx0", 1, F32), ("amx1", 1, F32),
        ("ident", 128, F32), ("mask_ut", 128, F32),
        ("ones_cb", 1, BF16), ("ones_rb", 128, BF16),
        ("rrowb", 512, BF16), ("smf", 16, F32),
        ("tw32", 32, F32), ("tn32", 32, F32), ("tc32", 32, F32),
        ("bcs", 2, F32), ("dcol", 1, F32),
        ("ta", 256, F32), ("tb2", 256, F32), ("tc2", 256, F32), ("td", 256, F32),
        ("tf", 256, BF16),
    ]
    used = sum(w * {F32: 4, BF16: 2, I32: 4}[d] for _, w, d in SB)
    assert used <= 207000, f"SBUF over budget: {used}"

    sems = ["M", "V", "S", "C",
            "MS", "Gc", "Lx0", "Lx1", "Lx2", "Gx0", "Gx1", "WP0", "WP1", "Ta", "Qt",
            "TT00", "TT01", "TT10", "TT11", "Qx0", "Qx1", "Qq", "Qv", "KT",
            "Qa0", "Qa1", "Qs0", "Qs1", "Qm", "Lt0", "Lt1", "Qw0", "Qw1",
            "Lm", "PS", "Gk",
            "Sg0", "Sg1", "Ld0", "Ld1", "X20", "X21", "Qr0", "Qr1", "Qo0", "Qo1"]
    cnt = {s: 0 for s in sems}
    plan = {"sync": [], "scalar": [], "vector": [], "tensor": [], "gpsimd": []}
    tn = {}
    free = {}

    def ap(name, sl=None):
        t = tn[name]
        return t[:, :] if sl is None else t[:, sl]

    def rowap(name, r, sl):
        return tn[name][r:r + 1, sl]

    def emit(eng, fn):
        plan[eng].append(fn)

    def wait(eng, sem, val):
        if val > 0:
            emit(eng, lambda e, s=sem, v=val: e.wait_ge(tn["sem_" + s], v))

    def gate(eng, key):
        for s, v in free.pop(key, {}).items():
            wait(eng, s, v)

    def setfree(key, *pairs):
        d = free.setdefault(key, {})
        for s, v in pairs:
            d[s] = max(d.get(s, 0), v)

    def dma(eng, sem, out, in_):
        emit(eng, lambda e, o=out, i=in_, s=sem: e.dma_start(out=o, in_=i).then_inc(tn["sem_" + s], 16))
        cnt[sem] += 16
        return cnt[sem]

    def dmat(eng, sem, out, in_):
        emit(eng, lambda e, o=out, i=in_, s=sem: e.dma_start_transpose(o, i).then_inc(tn["sem_" + s], 16))
        cnt[sem] += 16
        return cnt[sem]

    def mm(lhsT, rhs, out, start, stop):
        if stop:
            emit("tensor", lambda e, a=lhsT, b=rhs, o=out, st=start:
                 e.matmul(o, a, b, start=st, stop=True).then_inc(tn["sem_M"], 1))
            cnt["M"] += 1
        else:
            emit("tensor", lambda e, a=lhsT, b=rhs, o=out, st=start:
                 e.matmul(o, a, b, start=st, stop=False))
        return cnt["M"]

    def vop(fn):
        wait("vector", "V", cnt["V"])
        emit("vector", lambda e, f=fn: f(e).then_inc(tn["sem_V"], 1))
        cnt["V"] += 1
        return cnt["V"]

    def sop(fn):
        wait("scalar", "S", cnt["S"])
        emit("scalar", lambda e, f=fn: f(e).then_inc(tn["sem_S"], 1))
        cnt["S"] += 1
        return cnt["S"]

    def gop(fn):
        wait("gpsimd", "C", cnt["C"])
        emit("gpsimd", lambda e, f=fn: f(e).then_inc(tn["sem_C"], 1))
        cnt["C"] += 1
        return cnt["C"]

    import contextlib
    with contextlib.ExitStack() as stack:
        for name, w, dt in SB:
            tn[name] = stack.enter_context(nc.sbuf_tensor("sb_" + name, [128, w], dt))
        for pname in ["ps_q0", "ps_q1", "ps_v0", "ps_v1", "ps_a", "ps_s0", "ps_s1"]:
            tn[pname] = stack.enter_context(nc.psum_tensor(pname, [128, 512], F32))
        for s in sems:
            tn["sem_" + s] = stack.enter_context(nc.semaphore("sem" + s))
        block = stack.enter_context(nc.Block())

        dma("sync", "MS", ap("bqc"), bqc_h)
        dma("sync", "MS", ap("bkc"), bkc_h)
        dma("sync", "MS", ap("gbc"), gam_h)
        dma("sync", "MS", ap("bbc"), bet_h)
        dma("sync", "MS", ap("mselc"), msel_h)
        dma("sync", "MS", ap("ident"), identh)
        dma("sync", "MS", ap("mask_ut"), mtrih)
        lp = cnt["MS"]
        dma("gpsimd", "Gc", tn["bvb"][0:1, :], bv_h)

        for i in range(3):
            dma("sync", f"Lx{i}", ap(f"xg{i}"), x_h[i * 128:(i + 1) * 128, :])
        wait("scalar", "MS", lp)
        vop(lambda e: e.memset(ap("ones_cb"), 1.0))
        vop(lambda e: e.memset(tn["ones_rb"][0:2, :], 1.0))
        vop(lambda e: e.memset(ap("kvr"), 0.0))
        vop(lambda e: e.memset(ap("amx0"), 0.0))
        vop(lambda e: e.memset(ap("ksA"), 0.0))
        wait("vector", "MS", lp)

        for wi, (wh, wt) in enumerate([(wq_h, "w1"), (wk_h, "w2"), (wv_h, "w3")]):
            for ob in range(8):
                par = (wi * 8 + ob) % 2
                st = f"xnb{par}"
                gate("gpsimd", st)
                gc = dma("gpsimd", f"Gx{par}", ap(st), wh[ob * 128:(ob + 1) * 128, :])
                wait("sync", f"Gx{par}", gc)
                dest = tn[wt].ap().rearrange("p (d o) -> p d o", d=8)[:, :, ob * 128:(ob + 1) * 128]
                lc = dmat("sync", f"WP{par}", dest, ap(st))
                setfree(st, (f"WP{par}", lc))
        wp_L = (cnt["WP0"], cnt["WP1"])

        mid = {}

        def tern_pass1(j):
            ob, sub = divmod(j, 4)
            gate("sync", "ta")
            ld = dma("sync", "Ta", ap("ta"), wf_h[ob * 128:(ob + 1) * 128, sub * 256:(sub + 1) * 256])
            gate("scalar", "tb2")
            wait("scalar", "Ta", ld)
            sc = sop(lambda e, o=ap("tb2"), a=ap("ta"), acc=ap("tw32", slice(j, j + 1)):
                     e.activation(o, a, ACT.Abs, accum_out=acc))
            setfree("ta", ("S", sc))
            setfree("tb2", ("S", sc))
            mid["tp1_s"] = sc

        def tern_delta():
            gate("vector", "scr0")
            wait("vector", "S", mid["tp1_s"])
            v1 = vop(lambda e, o=ap("scr0", slice(0, 1)), a=ap("tw32"): e.reduce_sum(o, a, AX))
            gate("tensor", "ps_a")
            wait("tensor", "V", v1)
            wait("tensor", "MS", lp)
            mc = mm(ap("scr0", slice(0, 1)), ap("ident"), tn["ps_a"][0:1, 0:128], True, True)
            setfree("scr0", ("M", mc))
            wait("vector", "M", mc)
            vop(lambda e, o=rowap("td", 0, slice(0, 128)), a=tn["ps_a"][0:1, 0:128]:
                e.tensor_copy(o, a))
            vop(lambda e, o=rowap("smf", 0, slice(2, 3)), a=rowap("td", 0, slice(0, 128)):
                e.reduce_sum(o, a, AX))
            v2 = vop(lambda e, o=rowap("smf", 0, slice(3, 4)), a=rowap("smf", 0, slice(2, 3)):
                     e.tensor_scalar(o, a, 0.7 / (1024.0 * 1024.0), None, ALU.mult))
            setfree("ps_a", ("V", cnt["V"]))
            gate("tensor", "ps_a")
            wait("tensor", "V", v2)
            mc2 = mm(tn["mask_ut"][0:1, :], rowap("smf", 0, slice(3, 4)), tn["ps_a"][:, 0:1], True, True)
            wait("vector", "M", mc2)
            v3 = vop(lambda e, o=ap("dcol"), a=tn["ps_a"][:, 0:1]: e.tensor_copy(o, a))
            setfree("ps_a", ("V", v3))
            mid["dcol_v"] = v3

        def tern_pass2(j):
            ob, sub = divmod(j, 4)
            gate("sync", "ta")
            ld = dma("sync", "Ta", ap("ta"), wf_h[ob * 128:(ob + 1) * 128, sub * 256:(sub + 1) * 256])
            gate("scalar", "tb2")
            wait("scalar", "Ta", ld)
            s_ab = sop(lambda e, o=ap("tb2"), a=ap("ta"): e.activation(o, a, ACT.Abs))
            wait("vector", "S", s_ab)
            wait("vector", "V", mid["dcol_v"])
            vop(lambda e, o=ap("tc2"), a=ap("tb2"), d=ap("dcol"):
                e.tensor_scalar(o, a, d, None, ALU.is_gt))
            vop(lambda e, o=ap("tc32", slice(j, j + 1)), a=ap("tc2"): e.reduce_sum(o, a, AX))
            v_d = vop(lambda e, o=ap("td"), a=ap("tb2"), b=ap("tc2"):
                      e.tensor_tensor(o, a, b, ALU.mult))
            vop(lambda e, o=ap("tn32", slice(j, j + 1)), a=ap("td"): e.reduce_sum(o, a, AX))
            wait("scalar", "V", v_d)
            s_sg = sop(lambda e, o=ap("tb2"), a=ap("ta"): e.activation(o, a, ACT.Sign))
            setfree("ta", ("S", s_sg))
            gate("vector", "tf")
            wait("vector", "S", s_sg)
            v_t = vop(lambda e, o=ap("tf"), a=ap("tb2"), b=ap("tc2"):
                      e.tensor_tensor(o, a, b, ALU.mult))
            setfree("tb2", ("V", v_t))
            wait("scalar", "V", v_t)
            q_t = dma("scalar", "Qt", sp_t[ob * 128:(ob + 1) * 128, sub * 256:(sub + 1) * 256], ap("tf"))
            setfree("tf", ("Qt", q_t))

        qk_last_v = {}
        for g in range(NGRUN):
            gp = g % 2
            xnT = f"xnT{gp}"
            for i in range(4):
                ti = g * 4 + i
                slot = f"xg{ti % 3}"
                tp = f"xnf{i % 2}"
                tb = f"xnb{i % 2}"
                x2 = f"x2t{i % 2}"
                scr = f"scr{i % 2}"
                if ti >= 3:
                    gate("sync", slot)
                    ld = dma("sync", f"Lx{ti % 3}", ap(slot), x_h[ti * 128:(ti + 1) * 128, :])
                else:
                    ld = 16
                gate("scalar", tp)
                gate("scalar", scr)
                wait("scalar", f"Lx{ti % 3}", ld)
                sop(lambda e, o=ap(tp), x=ap(slot), a=ap(scr, slice(0, 1)):
                    e.activation(o, x, ACT.Identity, accum_out=a))
                s_st = sop(lambda e, o=ap(tp), x=ap(slot), a=ap(scr, slice(1, 2)):
                           e.activation(o, x, ACT.Square, accum_out=a))
                stc = ti * 3
                mneg = ap("stats", slice(stc, stc + 1))
                r1c = ap("stats", slice(stc + 1, stc + 2))
                r2c = ap("stats", slice(stc + 2, stc + 3))
                wait("vector", "S", s_st)
                vop(lambda e, o=mneg, a=ap(scr, slice(0, 1)):
                    e.tensor_scalar(o, a, -1.0 / 1024.0, None, ALU.mult))
                vop(lambda e, o=ap(scr, slice(2, 3)), a=ap(scr, slice(1, 2)):
                    e.tensor_scalar(o, a, 1.0 / 1024.0, None, ALU.mult))
                vop(lambda e, o=ap(scr, slice(3, 4)), a=mneg:
                    e.tensor_tensor(o, a, a, ALU.mult))
                v_var = vop(lambda e, o=ap(scr, slice(4, 5)), a=ap(scr, slice(2, 3)), b=ap(scr, slice(3, 4)):
                            e.tensor_tensor(o, a, b, ALU.subtract))
                vop(lambda e, o=ap(scr, slice(6, 7)), a=ap(scr, slice(4, 5)):
                    e.tensor_scalar(o, a, EPS1, None, ALU.add))
                wait("scalar", "V", cnt["V"])
                s_q1 = sop(lambda e, o=ap(scr, slice(5, 6)), a=ap(scr, slice(6, 7)):
                           e.activation(o, a, ACT.Sqrt))
                wait("vector", "S", s_q1)
                vop(lambda e, o=r1c, a=ap(scr, slice(5, 6)): e.reciprocal(o, a))
                vop(lambda e, o=ap(scr, slice(5, 6)), a=ap(scr, slice(6, 7)):
                    e.reciprocal(o, a))
                vop(lambda e, o=ap(scr, slice(7, 8)), a=ap(scr, slice(4, 5)), b=ap(scr, slice(5, 6)):
                    e.tensor_tensor(o, a, b, ALU.mult))
                v_v2 = vop(lambda e, o=ap(scr, slice(6, 7)), a=ap(scr, slice(7, 8)):
                           e.tensor_scalar(o, a, EPS1, None, ALU.add))
                wait("scalar", "V", v_v2)
                s_r2 = sop(lambda e, o=ap(scr, slice(5, 6)), a=ap(scr, slice(6, 7)):
                           e.activation(o, a, ACT.Sqrt))
                wait("vector", "S", s_r2)
                vop(lambda e, o=r2c, a=ap(scr, slice(5, 6)): e.reciprocal(o, a))
                v_xn = vop(lambda e, o=ap(tp), x=ap(slot), m=mneg, r=r1c:
                           e.tensor_scalar(o, x, m, r, ALU.add, ALU.mult))
                gate("gpsimd", tb)
                wait("gpsimd", "V", v_xn)
                g_c = dma("gpsimd", f"Gx{i % 2}", ap(tb), ap(tp))
                if i == 0:
                    gate("sync", xnT)
                wait("sync", f"Gx{i % 2}", g_c)
                dest = tn[xnT].ap().rearrange("p (d t) -> p d t", d=8)[:, :, i * 128:(i + 1) * 128]
                ttsem = f"TT{g % 2}{i % 2}"
                dmat("sync", ttsem, dest, ap(tb))
                setfree(tb, (ttsem, cnt[ttsem]))
                if i == 3:
                    mid[("tp_done", g)] = [(f"TT{g % 2}0", cnt[f"TT{g % 2}0"]),
                                           (f"TT{g % 2}1", cnt[f"TT{g % 2}1"])]
                v_u = vop(lambda e, o=ap("u"), a=ap(tp), b=ap("gbc"):
                          e.tensor_tensor(o, a, b, ALU.mult))
                gate("vector", x2)
                v_x2 = vop(lambda e, o=ap(x2), a=ap("u"), r=r2c, b=ap("bbc"):
                           e.scalar_tensor_tensor(o, a, r, b, ALU.mult, ALU.add))
                setfree(tp, (f"Gx{i % 2}", g_c), ("V", v_u))
                vop(lambda e, o=ap(scr, slice(0, 1)), a=ap(x2):
                    e.reduce_max(o, a, AX, apply_absolute_value=True))
                v_am = vop(lambda e, o=ap(f"amx{(ti + 1) % 2}"), a=ap(scr, slice(0, 1)), p=ap(f"amx{ti % 2}"):
                           e.tensor_scalar(o, a, p, None, ALU.max))
                setfree(scr, ("S", s_r2), ("V", v_am))
                wait("scalar", "V", v_x2)
                q_sp = dma("scalar", f"Qx{i % 2}", sp_x2[ti * 128:(ti + 1) * 128, :], ap(x2))
                setfree(x2, (f"Qx{i % 2}", q_sp))
                setfree(slot, ("V", v_xn), ("S", s_st))
            if TERN_ON:
                if g < NGRUN // 2 or NGRUN == 1:
                    for j in range(g * 8, g * 8 + 8):
                        tern_pass1(j)
                    if g == max(NGRUN // 2, 1) - 1:
                        tern_delta()
                else:
                    for j in range((g - NGRUN // 2) * 8, (g - NGRUN // 2) * 8 + 8):
                        tern_pass2(j)
            if P1PART >= 2:
                for proj, (wt, dst, bc_) in enumerate([("w1", "qT", "bqc"), ("w2", "kT", "bkc")]):
                    for ob in range(8):
                        pb = f"ps_q{ob % 2}"
                        gate("tensor", pb)
                        if ob == 0:
                            for sm, vv in mid[("tp_done", g)]:
                                wait("tensor", sm, vv)
                            if g == 0 and proj == 0:
                                wait("tensor", "WP0", wp_L[0])
                                wait("tensor", "WP1", wp_L[1])
                        for dc in range(8):
                            mmc = mm(tn[wt][:, dc * 1024 + ob * 128: dc * 1024 + (ob + 1) * 128],
                                     tn[xnT][:, dc * 512:(dc + 1) * 512],
                                     ap(pb), start=(dc == 0), stop=(dc == 7))
                        mnb = f"mn{ob % 2}"
                        rbb = f"Rb{ob % 2}"
                        gate("vector", mnb)
                        wait("vector", "M", mmc)
                        v_mn = vop(lambda e, o=ap(mnb), p=ap(pb), b=ap(bc_, slice(ob, ob + 1)):
                                   e.tensor_scalar(o, p, b, 0.0, ALU.add, ALU.min))
                        if ob == 0:
                            gate("scalar", dst)
                        wait("scalar", "V", v_mn)
                        s_e = sop(lambda e, o=tn[dst][:, ob * 512:(ob + 1) * 512], a=ap(mnb):
                                  e.activation(o, a, ACT.Exp))
                        setfree(mnb, ("S", s_e))
                        gate("scalar", rbb)
                        wait("scalar", "M", mmc)
                        s_r = sop(lambda e, o=ap(rbb), p=ap(pb), b=ap(bc_, slice(ob, ob + 1)):
                                  e.activation(o, p, ACT.Relu, bias=b, scale=1.0))
                        setfree(pb, ("V", v_mn), ("S", s_r))
                        wait("vector", "S", s_r)
                        v_q = vop(lambda e, o=tn[dst][:, ob * 512:(ob + 1) * 512], a=ap(rbb):
                                  e.tensor_tensor(o, o, a, ALU.add))
                        setfree(rbb, ("V", v_q))
                        qk_last_v[(g, dst)] = v_q
            if P1PART >= 3:
                for tb4 in range(4):
                    for eh in range(2):
                        idx = tb4 * 2 + eh
                        pb = f"ps_v{idx % 2}"
                        gate("tensor", pb)
                        if g == 0 and idx == 0:
                            wait("tensor", "Gc", 16)
                        for dc in range(8):
                            mm(tn[xnT][:, dc * 512 + tb4 * 128: dc * 512 + (tb4 + 1) * 128],
                               tn["w3"][:, dc * 1024 + eh * 512: dc * 1024 + (eh + 1) * 512],
                               ap(pb), start=(dc == 0), stop=False)
                        mmc = mm(tn["ones_rb"][0:1, :],
                                 tn["bvb"][0:1, eh * 512:(eh + 1) * 512],
                                 ap(pb), start=False, stop=True)
                        if idx == 0:
                            gate("scalar", "vtok")
                        wait("scalar", "M", mmc)
                        s_v = sop(lambda e, o=tn["vtok"][:, tb4 * 1024 + eh * 512: tb4 * 1024 + (eh + 1) * 512],
                                  p=ap(pb): e.copy(o, p))
                        setfree(pb, ("S", s_v))
                v_evac_s = cnt["S"]
                mid[("xnT_free", g)] = cnt["M"]
                setfree(xnT, ("M", cnt["M"]))
            if P1PART >= 4:
                for dkb in range(8):
                    vop(lambda e, o=ap("ksg", slice(g * 8 + dkb, g * 8 + dkb + 1)),
                        a=tn["kT"][:, dkb * 512:(dkb + 1) * 512]:
                        e.reduce_sum(o, a, AX))
                ks_v = cnt["V"]
                wait("scalar", "V", qk_last_v[(g, "qT")])
                q_spq = dma("scalar", "Qq", sp_q[g * 128:(g + 1) * 128, :], ap("qT"))
                wait("scalar", "S", v_evac_s)
                q_spv = dma("scalar", "Qv", sp_v[g * 128:(g + 1) * 128, :], ap("vtok"))
                if g == 0:
                    gate("sync", "ktok")
                wait("sync", "V", qk_last_v[(g, "kT")])
                for dkb in range(8):
                    dest = tn["ktok"].ap().rearrange("p (s d) -> p s d", s=4)[:, :, dkb * 128:(dkb + 1) * 128]
                    dmat("sync", "KT", dest, tn["kT"][:, dkb * 512:(dkb + 1) * 512])
                kt_L = cnt["KT"]
            if P1PART >= 5:
                aT = f"aT{gp}"
                gate("vector", aT)
                a_evac = []
                for sb in range(4):
                    pb = "ps_a" if sb % 2 == 0 else "ps_s0"
                    gate("tensor", pb)
                    if sb == 0:
                        wait("tensor", "V", qk_last_v[(g, "qT")])
                        wait("tensor", "V", qk_last_v[(g, "kT")])
                    for dc in range(8):
                        mmc = mm(tn["kT"][:, dc * 512 + sb * 128: dc * 512 + (sb + 1) * 128],
                                 tn["qT"][:, dc * 512 + sb * 128: (dc + 1) * 512],
                                 tn[pb][:, sb * 128:512], start=(dc == 0), stop=(dc == 7))
                    wait("vector", "M", mmc)
                    vop(lambda e, o=tn[aT][:, PK[sb]:PK[sb] + 128],
                        p=tn[pb][:, sb * 128:sb * 128 + 128], m=ap("mask_ut"):
                        e.tensor_tensor(o, p, m, ALU.mult))
                    if sb < 3:
                        v_a = vop(lambda e, o=tn[aT][:, PK[sb] + 128:PK[sb] + PW[sb]],
                                  p=tn[pb][:, sb * 128 + 128:512]:
                                  e.tensor_copy(o, p))
                    else:
                        v_a = cnt["V"]
                    setfree(pb, ("V", v_a))
                    a_evac.append(v_a)
                wait("scalar", "V", a_evac[-1])
                q_spa = dma("scalar", f"Qa{g % 2}", sp_a[g * 128:(g + 1) * 128, :], tn[aT][:, :])
                setfree(aT, (f"Qa{g % 2}", q_spa))
                setfree("qT", ("Qq", q_spq), ("M", cnt["M"]))
                setfree("kT", ("M", cnt["M"]), ("V", ks_v), ("KT", kt_L))
            if P1PART >= 6:
                for dkb in range(8):
                    for eh in range(2):
                        idx = dkb * 2 + eh
                        pb = f"ps_s{idx % 2}"
                        gate("tensor", pb)
                        if idx == 0:
                            wait("tensor", "KT", kt_L)
                            wait("tensor", "S", v_evac_s)
                        for sb in range(4):
                            mmc = mm(tn["ktok"][:, sb * 1024 + dkb * 128: sb * 1024 + (dkb + 1) * 128],
                                     tn["vtok"][:, sb * 1024 + eh * 512: sb * 1024 + (eh + 1) * 512],
                                     ap(pb), start=(sb == 0), stop=(sb == 3))
                        fr = []
                        if S_SUB >= 1:
                            wait("vector", "M", mmc)
                            v_kv = vop(lambda e, o=tn["kvr"][:, dkb * 1024 + eh * 512: dkb * 1024 + (eh + 1) * 512],
                                       p=ap(pb): e.tensor_tensor(o, o, p, ALU.add))
                            fr.append(("V", v_kv))
                        if S_SUB >= 2:
                            sg = f"rT{idx % 2}"
                            gate("vector", sg + "_s")
                            wait("vector", "M", mmc)
                            s_c = vop(lambda e, o=tn[sg][:, 0:512], p=ap(pb): e.tensor_copy(o, p))
                            fr.append(("V", s_c))
                        setfree(pb, *fr)
                        if S_SUB >= 3:
                            wait("scalar", "V", s_c)
                            q_s = dma("scalar", f"Qs{idx % 2}",
                                      sp_s[g * 1024 + dkb * 128: g * 1024 + (dkb + 1) * 128,
                                           eh * 512:(eh + 1) * 512],
                                      tn[sg][:, 0:512])
                            setfree(sg + "_s", (f"Qs{idx % 2}", q_s))
                setfree("ktok", ("M", cnt["M"]))
                setfree("vtok", ("M", cnt["M"]), ("Qv", q_spv))
                mid[("kvr_v", g)] = cnt["V"]

        if STAGE >= 3:
            vop(lambda e, o=ap("kst"), a=ap("ksg", slice(0, 8)), b=ap("ksg", slice(8, 16)):
                e.tensor_tensor(o, a, b, ALU.add))
            for g2 in range(2, 8):
                vop(lambda e, o=ap("kst"), a=ap("kst"), b=ap("ksg", slice(g2 * 8, g2 * 8 + 8)):
                    e.tensor_tensor(o, a, b, ALU.add))
            kst_v = cnt["V"]
            gate("tensor", "ps_a")
            wait("tensor", "V", cnt["V"])
            mc = mm(ap("amx0"), ap("ident"), tn["ps_a"][0:1, 0:128], True, True)
            wait("vector", "M", mc)
            vop(lambda e, o=rowap("td", 0, slice(0, 128)), a=tn["ps_a"][0:1, 0:128]:
                e.tensor_copy(o, a))
            v_mx = vop(lambda e, o=rowap("smf", 0, slice(0, 1)), a=rowap("td", 0, slice(0, 128)):
                       e.reduce_max(o, a, AX))
            setfree("ps_a", ("V", cnt["V"]))
            wait("scalar", "V", v_mx)
            q_mx = dma("scalar", "Qm", mx_in, rowap("smf", 0, slice(0, 1)))
            gate("vector", "scr0")
            vop(lambda e, o=ap("scr0", slice(0, 1)), a=ap("tn32"): e.reduce_sum(o, a, AX))
            v_nc = vop(lambda e, o=ap("scr0", slice(1, 2)), a=ap("tc32"): e.reduce_sum(o, a, AX))
            gate("tensor", "ps_a")
            wait("tensor", "V", v_nc)
            mm(ap("scr0", slice(0, 1)), ap("ident"), tn["ps_a"][0:1, 0:128], True, True)
            mc2 = mm(ap("scr0", slice(1, 2)), ap("ident"), tn["ps_a"][0:1, 128:256], True, True)
            wait("vector", "M", mc2)
            vop(lambda e, o=rowap("td", 0, slice(0, 256)), a=tn["ps_a"][0:1, 0:256]:
                e.tensor_copy(o, a))
            vop(lambda e, o=rowap("smf", 0, slice(5, 6)), a=rowap("td", 0, slice(0, 128)):
                e.reduce_sum(o, a, AX))
            vop(lambda e, o=rowap("smf", 0, slice(6, 7)), a=rowap("td", 0, slice(128, 256)):
                e.reduce_sum(o, a, AX))
            setfree("ps_a", ("V", cnt["V"]))
            vop(lambda e, o=rowap("smf", 0, slice(7, 8)), a=rowap("smf", 0, slice(6, 7)):
                e.tensor_scalar(o, a, 1.0, None, ALU.max))
            vop(lambda e, o=rowap("smf", 0, slice(8, 9)), a=rowap("smf", 0, slice(7, 8)):
                e.reciprocal(o, a))
            v_al = vop(lambda e, o=rowap("smf", 0, slice(9, 10)), a=rowap("smf", 0, slice(5, 6)),
                       b=rowap("smf", 0, slice(8, 9)): e.tensor_tensor(o, a, b, ALU.mult))
            wait("gpsimd", "V", mid[("kvr_v", NGRUN - 1)])
            for dkb in range(8):
                dma("gpsimd", "Gc", cc_in[dkb * 128:(dkb + 1) * 128, 0:1024],
                    tn["kvr"][:, dkb * 1024:(dkb + 1) * 1024])
            gate("tensor", "ps_s1")
            wait("tensor", "V", kst_v)
            mc_ks = mm(ap("kst"), ap("ident"), tn["ps_s1"][0:8, 0:128], True, True)
            wait("vector", "M", mc_ks)
            gate("vector", "rT0_s")
            v_ksr = vop(lambda e, o=tn["rT0"][0:8, 0:128], a=tn["ps_s1"][0:8, 0:128]:
                        e.tensor_copy(o, a))
            setfree("ps_s1", ("V", v_ksr))
            wait("gpsimd", "V", v_ksr)
            dma("gpsimd", "Gc",
                cc_in[1024:1025, :].rearrange("r (d p) -> (r d) p", d=8),
                tn["rT0"][0:8, 0:128])
            g_cc = cnt["Gc"]
            setfree("rT0_s", ("Gc", g_cc))
            wait("gpsimd", "Gc", g_cc)
            wait("gpsimd", "Qm", q_mx)
            cc_done = gop(lambda e: e.collective_compute(
                "AllReduce", ALU.max,
                replica_groups=[list(range(8))],
                ins=[mx_in_t.ap()], outs=[mx_out_t.ap()]))
            emit("gpsimd", lambda e: e.collective_compute(
                "AllGather", ALU.bypass,
                replica_groups=[[0, 1, 2, 3], [4, 5, 6, 7]],
                ins=[cc_in_t.ap()], outs=[cc_out_t.ap()]).then_inc(tn["sem_C"], 1))
            cnt["C"] += 1
            ag_done = cnt["C"]

            for ob in range(8):
                par = ob % 2
                st = f"xnb{par}"
                gate("gpsimd", st)
                gc = dma("gpsimd", f"Gx{par}", ap(st), wo_h[ob * 128:(ob + 1) * 128, :])
                wait("sync", f"Gx{par}", gc)
                wait("sync", "M", mid[("xnT_free", NGRUN - 1)])
                dest = tn["w1"].ap().rearrange("p (d o) -> p d o", d=8)[:, :, ob * 128:(ob + 1) * 128]
                lc = dmat("sync", f"WP{par}", dest, ap(st))
                setfree(st, (f"WP{par}", lc))
            wo_WP = (cnt["WP0"], cnt["WP1"])
            for ob in range(8):
                par = ob % 2
                st = f"xnb{par}"
                gate("sync", st)
                wait("sync", "Qt", cnt["Qt"])
                ld = dma("sync", f"Lt{par}", ap(st), sp_t[ob * 128:(ob + 1) * 128, :])
                wait("scalar", f"Lt{par}", ld)
                dest = tn["w2"].ap().rearrange("p (d o) -> p d o", d=8)[:, :, ob * 128:(ob + 1) * 128]
                lc = dmat("scalar", f"Qw{par}", dest, ap(st))
                setfree(st, (f"Qw{par}", lc))
            w2_Q = (cnt["Qw0"], cnt["Qw1"])

            wait("sync", "C", cc_done)
            l_mx = dma("sync", "Lm", rowap("smf", 0, slice(10, 11)), mx_out)
            wait("vector", "Lm", l_mx)
            vop(lambda e, o=rowap("smf", 0, slice(11, 12)), a=rowap("smf", 0, slice(10, 11)):
                e.tensor_scalar(o, a, 1e-8, None, ALU.max))
            vop(lambda e, o=rowap("smf", 0, slice(12, 13)), a=rowap("smf", 0, slice(11, 12)):
                e.tensor_scalar(o, a, 1.0 / 127.0, None, ALU.mult))
            vop(lambda e, o=rowap("smf", 0, slice(13, 14)), a=rowap("smf", 0, slice(12, 13)):
                e.reciprocal(o, a))
            wait("vector", "V", v_al)
            vop(lambda e, o=rowap("smf", 0, slice(14, 15)), a=rowap("smf", 0, slice(9, 10)),
                b=rowap("smf", 0, slice(12, 13)): e.tensor_tensor(o, a, b, ALU.mult))
            v_as = vop(lambda e, o=rowap("smf", 0, slice(15, 16)), a=rowap("smf", 0, slice(14, 15)):
                       e.reciprocal(o, a))
            gate("tensor", "ps_s1")
            wait("tensor", "V", v_as)
            mc3 = mm(tn["mask_ut"][0:1, :], rowap("smf", 0, slice(13, 15)), tn["ps_s1"][:, 0:2], True, True)
            wait("vector", "M", mc3)
            v_bcs = vop(lambda e, o=ap("bcs"), a=tn["ps_s1"][:, 0:2]: e.tensor_copy(o, a))
            setfree("ps_s1", ("V", v_bcs))
            l_bo = dma("sync", "MS", rowap("xnf0", 0, slice(0, 1024)), bo_h)
            wait("vector", "MS", l_bo)
            vop(lambda e, o=rowap("xnf1", 0, slice(0, 1024)), a=rowap("xnf0", 0, slice(0, 1024)),
                s=rowap("smf", 0, slice(15, 16)): e.tensor_scalar(o, a, s, None, ALU.mult))
            vop(lambda e, o=tn["boeff"][0:1, :], a=rowap("xnf1", 0, slice(0, 1024)):
                e.tensor_copy(o, a))
            vop(lambda e, o=rowap("xnf0", 0, slice(0, 1024)), a=tn["boeff"][0:1, :]:
                e.tensor_copy(o, a))
            v_bo = vop(lambda e, o=tn["bvb"][0:1, :], a=rowap("xnf1", 0, slice(0, 1024)),
                       b=rowap("xnf0", 0, slice(0, 1024)): e.tensor_tensor(o, a, b, ALU.subtract))
            wait("sync", "M", mid[("xnT_free", NGRUN - 1)])
            wait("sync", "C", ag_done)
            wait("gpsimd", "C", ag_done)
            wait("gpsimd", "M", mid[("xnT_free", NGRUN - 1)])
            gate("sync", "qT")
            wait("vector", "Gx0", cnt["Gx0"])
            wait("vector", "Gx1", cnt["Gx1"])
            v_kv = 0
            for dkb in range(8):
                if dkb > 0:
                    wait("sync", "V", v_kv)
                for j in range(4):
                    dma("sync", "PS", tn["qT"][:, j * 1024:(j + 1) * 1024],
                        cc_out[j * 1025 + dkb * 128: j * 1025 + (dkb + 1) * 128, 0:1024])
                l_g = cnt["PS"]
                wait("vector", "PS", l_g)
                vop(lambda e, o=ap("u"), a=tn["qT"][:, 0:1024], m=ap("mselc", slice(0, 1)):
                    e.tensor_scalar(o, a, m, None, ALU.mult))
                vop(lambda e, o=ap("xnf0"), a=tn["qT"][:, 1024:2048], m=ap("mselc", slice(1, 2)), c=ap("u"):
                    e.scalar_tensor_tensor(o, a, m, c, ALU.mult, ALU.add))
                vop(lambda e, o=ap("u"), a=tn["qT"][:, 2048:3072], m=ap("mselc", slice(2, 3)), c=ap("xnf0"):
                    e.scalar_tensor_tensor(o, a, m, c, ALU.mult, ALU.add))
                v_kv = vop(lambda e, o=tn["kvr"][:, dkb * 1024:(dkb + 1) * 1024],
                           a=tn["qT"][:, 3072:4096], m=ap("mselc", slice(3, 4)), c=ap("u"):
                           e.scalar_tensor_tensor(o, a, m, c, ALU.mult, ALU.add))
                wait("gpsimd", "V", v_kv)
                dma("gpsimd", "Gk", tn["w3"][:, dkb * 1024:(dkb + 1) * 1024],
                    tn["kvr"][:, dkb * 1024:(dkb + 1) * 1024])
            kvb_G = cnt["Gk"]
            setfree("qT", ("V", cnt["V"]))
            gate("sync", "kT")
            wait("sync", "V", v_kv)
            for j in range(4):
                dma("sync", "PS", tn["kT"][0:1, j * 1024:(j + 1) * 1024],
                    cc_out[j * 1025 + 1024: j * 1025 + 1025, 0:1024])
            ksr_L = cnt["PS"]
            gate("tensor", "ps_s1")
            wait("tensor", "PS", ksr_L)
            for j in range(4):
                for dkb in range(8):
                    col = j * 8 + dkb
                    mc_k = mm(tn["kT"][0:1, j * 1024 + dkb * 128: j * 1024 + dkb * 128 + 128],
                              tn["ones_cb"][0:1, 0:1],
                              tn["ps_s1"][:, col:col + 1], True, True)
            setfree("kT", ("M", mc_k))
            wait("vector", "M", mc_k)
            v_ksl = vop(lambda e, o=ap("tc32"), a=tn["ps_s1"][:, 0:32]: e.tensor_copy(o, a))
            setfree("ps_s1", ("V", v_ksl))
            vop(lambda e, o=ap("ksB"), a=ap("tc32", slice(0, 8)), m=ap("mselc", slice(0, 1)):
                e.tensor_scalar(o, a, m, None, ALU.mult))
            vop(lambda e, o=ap("ksA"), a=ap("tc32", slice(8, 16)), m=ap("mselc", slice(1, 2)), c=ap("ksB"):
                e.scalar_tensor_tensor(o, a, m, c, ALU.mult, ALU.add))
            vop(lambda e, o=ap("ksB"), a=ap("tc32", slice(16, 24)), m=ap("mselc", slice(2, 3)), c=ap("ksA"):
                e.scalar_tensor_tensor(o, a, m, c, ALU.mult, ALU.add))
            vop(lambda e, o=ap("ksA"), a=ap("tc32", slice(24, 32)), m=ap("mselc", slice(3, 4)), c=ap("ksB"):
                e.scalar_tensor_tensor(o, a, m, c, ALU.mult, ALU.add))
            v_ksc = vop(lambda e, o=ap("kscb"), a=ap("ksA"): e.tensor_copy(o, a))

            bnd = dict(cnt)
            bnd_free = {
                "kT": ["M", "KT", "V"], "ktok": ["M"], "vtok": ["M", "Qv"],
                "aT0": ["Qa0"], "aT1": ["Qa1"], "xnT0": ["M"], "xnT1": ["M"],
                "xnf0": ["V", "Gx0"], "xnf1": ["V", "Gx1"],
                "xnb0": ["WP0", "Qw0"], "xnb1": ["WP1", "Qw1"],
                "x2t0": ["Qx0"], "x2t1": ["Qx1"],
                "xg0": ["S", "V"], "xg1": ["S", "V"], "xg2": ["S", "V"],
                "rT0": ["Qs0", "Gc"], "rT1": ["Qs1"],
                "rT0_s": ["Qs0", "Gc"], "rT1_s": ["Qs1"],
                "mn0": ["S"], "mn1": ["S"], "Rb0": ["V"], "Rb1": ["V"],
            }
            for key, ss in bnd_free.items():
                free[key] = {s: bnd[s] for s in ss}

        denr = rowap("u", 0, slice(0, 512))
        recr = rowap("u", 0, slice(512, 1024))
        inv_s_ap = ap("bcs", slice(0, 1))
        as_ap = ap("bcs", slice(1, 2))
        invas_ap = rowap("smf", 0, slice(15, 16))
        attnT_v = {}
        ldL = {}
        kscv = {0: v_ksc if STAGE >= 3 else 0}
        kvbG = {0: kvb_G if STAGE >= 3 else 0}
        denM = {-1: 0, -2: 0}
        vrr = {}

        def p2_loads(gg):
            par = (gg + 1) % 2
            qb_ = ["qT", "kT"][par]
            vb_ = ["ktok", "vtok"][par]
            ab_ = ["aT0", "aT1"][par]
            lds = f"Ld{par}"
            gate("sync", qb_)
            dma("sync", lds, ap(qb_), sp_q[gg * 128:(gg + 1) * 128, :])
            gate("sync", vb_)
            dma("sync", lds, ap(vb_), sp_v[gg * 128:(gg + 1) * 128, :])
            gate("sync", ab_)
            ldL[gg] = dma("sync", lds, ap(ab_), sp_a[gg * 128:(gg + 1) * 128, :])

        def p2_den(gg):
            par = (gg + 1) % 2
            qb_ = ["qT", "kT"][par]
            ab_ = ["aT0", "aT1"][par]
            ksb_ = ["kscb", "kscb2"][gg % 2]
            gate("tensor", "ps_a")
            wait("tensor", f"Ld{par}", ldL[gg])
            wait("tensor", "V", kscv[gg])
            for sb in range(4):
                mm(tn["ones_cb"][:, 0:1], tn[ab_][:, PK[sb]:PK[sb] + PW[sb]],
                   tn["ps_a"][0:1, sb * 128:512], start=(sb == 0), stop=False)
            for dc in range(8):
                mmc_ = mm(ap(ksb_, slice(dc, dc + 1)), tn[qb_][:, dc * 512:(dc + 1) * 512],
                          tn["ps_a"][0:1, 0:512], start=False, stop=(dc == 7))
            denM[gg] = mmc_
            wait("vector", "M", mmc_)
            vop(lambda e, o=denr, a=tn["ps_a"][0:1, 0:512]:
                e.tensor_scalar(o, a, EPS_ATT, None, ALU.add))
            vop(lambda e, o=recr, a=denr: e.reciprocal(o, a))
            gate("vector", "rrowb")
            gate("vector", "sgb")
            vop(lambda e, o=denr, a=recr, s=invas_ap:
                e.tensor_scalar(o, a, s, None, ALU.mult))
            vop(lambda e, o=tn["rrowb"][0:1, :], a=denr: e.tensor_copy(o, a))
            vop(lambda e, o=recr, a=tn["rrowb"][0:1, :]: e.tensor_copy(o, a))
            vrr[gg] = vop(lambda e, o=tn["sgb"][0:1, 0:512], a=denr, b=recr:
                          e.tensor_tensor(o, a, b, ALU.subtract))
            setfree("ps_a", ("V", cnt["V"]))

        def p2_bcast(gg):
            mnb_ = f"mn{gg % 2}"
            gate("tensor", "ps_s0")
            wait("tensor", "V", vrr[gg])
            mm(tn["ones_rb"][0:1, :], tn["rrowb"][0:1, :], ap("ps_s0"), True, False)
            mcb_ = mm(tn["ones_rb"][0:1, :], tn["sgb"][0:1, 0:512], ap("ps_s0"), False, True)
            setfree("rrowb", ("M", mcb_))
            setfree("sgb", ("M", mcb_))
            wait("vector", "M", mcb_)
            gate("vector", mnb_)
            v_bc_ = vop(lambda e, o=ap(mnb_), a=ap("ps_s0"): e.tensor_copy(o, a))
            setfree("ps_s0", ("V", v_bc_))

        def p2_kschain(gg):
            src, dst_ = ("ksA", "ksB") if gg % 2 == 1 else ("ksB", "ksA")
            vop(lambda e, o=ap(dst_), a=ap(src), b=ap("ksg", slice((gg - 1) * 8, gg * 8)):
                e.tensor_tensor(o, a, b, ALU.add))
            wait("vector", "M", denM[gg - 2])
            kscv[gg] = vop(lambda e, o=ap(["kscb", "kscb2"][gg % 2]),
                           a=ap(dst_): e.tensor_copy(o, a))

        if STAGE >= 4:
            p2_loads(0)
            p2_loads(1)
            p2_den(0)
            p2_bcast(0)
        for g in (range(NG) if STAGE >= 4 else []):
            pb2 = (g + 1) % 2
            qb = ["qT", "kT"][pb2]
            vb = ["ktok", "vtok"][pb2]
            ab = ["aT0", "aT1"][pb2]
            at = f"xnT{g % 2}"
            mnb = f"mn{g % 2}"
            if g + 1 < NG:
                p2_kschain(g + 1)
                p2_den(g + 1)
            for eb in range(8):
                pq = f"ps_q{eb % 2}"
                gate("tensor", pq)
                if eb == 0:
                    wait("tensor", "Gk", kvbG[g])
                for dc in range(8):
                    mm(tn["w3"][:, dc * 1024 + eb * 128: dc * 1024 + (eb + 1) * 128],
                       tn[qb][:, dc * 512:(dc + 1) * 512],
                       ap(pq), start=(dc == 0), stop=False)
                for sb in range(4):
                    mmc = mm(tn[vb][:, sb * 1024 + eb * 128: sb * 1024 + (eb + 1) * 128],
                             tn[ab][:, PK[sb]:PK[sb] + PW[sb]],
                             tn[pq][:, sb * 128:512], start=False, stop=(sb == 3))
                wait("vector", "M", mmc)
                if eb == 0:
                    gate("vector", at)
                v_at = vop(lambda e, o=tn[at][:, eb * 512:(eb + 1) * 512], p=ap(pq), b=ap(mnb):
                           e.tensor_tensor(o, p, b, ALU.mult))
                setfree(pq, ("V", v_at))
                attnT_v[(g, eb)] = v_at
            last_yq_M = cnt["M"]
            setfree(qb, ("M", cnt["M"]))
            setfree(vb, ("M", cnt["M"]))
            setfree(ab, ("M", cnt["M"]))
            if g + 2 < NG:
                p2_loads(g + 2)
            if g + 1 < NG:
                for u16 in range(16):
                    dkb, eh = divmod(u16, 2)
                    sg = f"Rb{u16 % 2}"
                    gate("sync", sg)
                    ldS = dma("sync", f"Sg{u16 % 2}", tn[sg][:, 0:512],
                              sp_s[g * 1024 + dkb * 128: g * 1024 + (dkb + 1) * 128,
                                   eh * 512:(eh + 1) * 512])
                    wait("vector", f"Sg{u16 % 2}", ldS)
                    if u16 == 0:
                        wait("vector", "Gk", kvbG[g])
                    v_ad = vop(lambda e, o=tn["kvr"][:, dkb * 1024 + eh * 512: dkb * 1024 + (eh + 1) * 512],
                               a=tn[sg][:, 0:512]: e.tensor_tensor(o, o, a, ALU.add))
                    setfree(sg, ("V", v_ad))
                kv_adds_v = cnt["V"]
                wait("gpsimd", "V", kv_adds_v)
                wait("gpsimd", "M", last_yq_M)
                for dkb in range(8):
                    dma("gpsimd", "Gk", tn["w3"][:, dkb * 1024:(dkb + 1) * 1024],
                        tn["kvr"][:, dkb * 1024:(dkb + 1) * 1024])
                kvbG[g + 1] = cnt["Gk"]
            for i in range(4):
                ti = g * 4 + i
                x2 = f"x2t{i % 2}"
                rb = f"xnb{i % 2}"
                rt = f"rT{i % 2}"
                slot = f"xg{ti % 3}"
                gate("sync", x2)
                l_x2 = dma("sync", f"X2{i % 2}", ap(x2), sp_x2[ti * 128:(ti + 1) * 128, :])
                gate("vector", rb)
                wait("vector", f"X2{i % 2}", l_x2)
                for h in range(2):
                    v_t1 = vop(lambda e, o=ap("i32h"), a=tn[x2][:, h * 512:(h + 1) * 512], s=inv_s_ap:
                               e.tensor_scalar(o, a, s, None, ALU.mult))
                    v_rb = vop(lambda e, o=tn[rb][:, h * 512:(h + 1) * 512], a=ap("i32h"):
                               e.tensor_copy(o, a))
                setfree(x2, ("V", v_rb))
                gate("scalar", rt)
                wait("scalar", "V", v_rb)
                dest = tn[rt].ap().rearrange("p (d t) -> p d t", d=8)[:, :, 0:128]
                q_rt = dmat("scalar", f"Qr{i % 2}", dest, ap(rb))
                setfree(rb, (f"Qr{i % 2}", q_rt))
                gate("sync", slot)
                l_x = dma("sync", f"Lx{ti % 3}", ap(slot), x_h[ti * 128:(ti + 1) * 128, :])
                for oh in range(2):
                    idx = (2 * i + oh) % 2
                    pv = f"ps_v{idx}"
                    gate("tensor", pv)
                    if i == 0 and oh == 0:
                        for eb in range(8):
                            wait("tensor", "V", attnT_v[(g, eb)])
                        wait("tensor", "Qw0", w2_Q[0])
                        wait("tensor", "Qw1", w2_Q[1])
                        wait("tensor", "WP0", wo_WP[0])
                        wait("tensor", "WP1", wo_WP[1])
                        wait("tensor", "V", v_bo)
                    for ac in range(8):
                        mm(tn[at][:, ac * 512 + i * 128: ac * 512 + (i + 1) * 128],
                           tn["w1"][:, ac * 1024 + oh * 512: ac * 1024 + (oh + 1) * 512],
                           ap(pv), start=(ac == 0), stop=False)
                    if oh == 0:
                        wait("tensor", f"Qr{i % 2}", q_rt)
                    for ic in range(8):
                        mm(tn[rt][:, ic * 128:(ic + 1) * 128],
                           tn["w2"][:, ic * 1024 + oh * 512: ic * 1024 + (oh + 1) * 512],
                           ap(pv), start=False, stop=False)
                    mm(tn["ones_rb"][0:1, :], tn["boeff"][0:1, oh * 512:(oh + 1) * 512],
                       ap(pv), start=False, stop=False)
                    mmc = mm(tn["ones_rb"][0:1, :], tn["bvb"][0:1, oh * 512:(oh + 1) * 512],
                             ap(pv), start=False, stop=True)
                    if oh == 0:
                        gate("vector", f"xnf{i % 2}")
                    wait("vector", "M", mmc)
                    wait("vector", f"Lx{ti % 3}", l_x)
                    v_out = vop(lambda e, o=tn[f"xnf{i % 2}"][:, oh * 512:(oh + 1) * 512],
                                p=ap(pv), s=as_ap, x=tn[slot][:, oh * 512:(oh + 1) * 512]:
                                e.scalar_tensor_tensor(o, p, s, x, ALU.mult, ALU.add))
                    setfree(pv, ("V", v_out))
                setfree(slot, ("V", v_out))
                wait("scalar", "V", v_out)
                q_o = dma("scalar", f"Qo{i % 2}", out_h[ti * 128:(ti + 1) * 128, :], ap(f"xnf{i % 2}"))
                setfree(f"xnf{i % 2}", (f"Qo{i % 2}", q_o))
                setfree(rt, ("M", cnt["M"]))
            setfree(at, ("M", cnt["M"]))
            if g + 1 < NG:
                p2_bcast(g + 1)

        wait("gpsimd", "Qo0", cnt["Qo0"])
        wait("gpsimd", "Qo1", cnt["Qo1"])

        @block.sync
        def _(eng):
            for f in plan["sync"]:
                f(eng)

        @block.scalar
        def _(eng):
            for f in plan["scalar"]:
                f(eng)

        @block.vector
        def _(eng):
            for f in plan["vector"]:
                f(eng)

        @block.tensor
        def _(eng):
            for f in plan["tensor"]:
                f(eng)

        @block.gpsimd
        def _(eng):
            for f in plan["gpsimd"]:
                f(eng)

    return nc


_NC_CACHE = None


def kernel(L0_attn_chunk, L0_attn_Wq, L0_attn_bq, L0_attn_Wk, L0_attn_bk,
           L0_attn_Wv, L0_attn_bv, L0_attn_Wo, L0_attn_bo,
           L0_ffn_weight, L0_ffn_gamma, L0_ffn_beta):
    global _NC_CACHE
    if _NC_CACHE is None:
        _NC_CACHE = build()
    nc = _NC_CACHE
    c = np.ascontiguousarray

    x = np.asarray(L0_attn_chunk, dtype=np.float32)
    wq = c(np.asarray(L0_attn_Wq, np.float32))
    wk = c(np.asarray(L0_attn_Wk, np.float32))
    wv = c(np.asarray(L0_attn_Wv, np.float32))
    wo = c(np.asarray(L0_attn_Wo, np.float32))
    wf = c(np.asarray(L0_ffn_weight, np.float32))
    bqc = c(np.asarray(L0_attn_bq, np.float32).reshape(8, 128).T)
    bkc = c(np.asarray(L0_attn_bk, np.float32).reshape(8, 128).T)
    bvr = c(np.asarray(L0_attn_bv, np.float32).reshape(1, D))
    bor = c(np.asarray(L0_attn_bo, np.float32).reshape(1, D))
    gamb = c(np.broadcast_to(np.asarray(L0_ffn_gamma, np.float32).reshape(1, D), (128, D)))
    betb = c(np.broadcast_to(np.asarray(L0_ffn_beta, np.float32).reshape(1, D), (128, D)))

    identc = np.eye(128, dtype=np.float32)
    mtric = c(np.triu(np.ones((128, 128), np.float32)))
    in_maps = []
    for core in range(NCORE):
        b, q = core // 4, core % 4
        msel = np.zeros((128, 4), np.float32)
        msel[:, :q] = 1.0
        in_maps.append({
            "x": c(x[b, q * TC:(q + 1) * TC, :]),
            "wq": wq, "wk": wk, "wv": wv, "wo": wo, "wf": wf,
            "bqc": bqc, "bkc": bkc, "bvr": bvr, "bor": bor,
            "gamb": gamb, "betb": betb, "msel": msel,
            "identc": identc, "mtric": mtric,
        })
    res = run_bass_kernel_spmd(nc, in_maps, core_ids=list(range(NCORE)))
    out = np.empty((B, T, D), np.float32)
    for core in range(NCORE):
        b, q = core // 4, core % 4
        out[b, q * TC:(q + 1) * TC, :] = res.results[core]["out"]
    return out


# version 11
# speedup vs baseline: 1.0619x; 1.0393x over previous
import numpy as np
import concourse.bass as bass
import concourse.mybir as mybir
from concourse.bass_utils import run_bass_kernel_spmd

F32 = mybir.dt.float32
BF16 = mybir.dt.bfloat16
I32 = mybir.dt.int32
ALU = mybir.AluOpType
ACT = mybir.ActivationFunctionType
AX = mybir.AxisListType.X

STAGE = 4
NGRUN = 8
P1PART = 6
S_SUB = 3
TERN_ON = True
NCORE = 8
B, T, D = 2, 16384, 1024
TC = T // 4
NT = TC // 128
NG = 8
PK = [0, 512, 896, 1152]
PW = [512, 384, 256, 128]
EPS1 = 1e-5
EPS_ATT = 1e-6


def build():
    nc = bass.Bass("TRN2", target_bir_lowering=False)

    x_h = nc.dram_tensor("x", [TC, D], F32, kind="ExternalInput").ap()
    wq_h = nc.dram_tensor("wq", [D, D], F32, kind="ExternalInput").ap()
    wk_h = nc.dram_tensor("wk", [D, D], F32, kind="ExternalInput").ap()
    wv_h = nc.dram_tensor("wv", [D, D], F32, kind="ExternalInput").ap()
    wo_h = nc.dram_tensor("wo", [D, D], F32, kind="ExternalInput").ap()
    wf_h = nc.dram_tensor("wf", [D, D], F32, kind="ExternalInput").ap()
    bqc_h = nc.dram_tensor("bqc", [128, 8], F32, kind="ExternalInput").ap()
    bkc_h = nc.dram_tensor("bkc", [128, 8], F32, kind="ExternalInput").ap()
    bv_h = nc.dram_tensor("bvr", [1, D], F32, kind="ExternalInput").ap()
    bo_h = nc.dram_tensor("bor", [1, D], F32, kind="ExternalInput").ap()
    gam_h = nc.dram_tensor("gamb", [128, D], F32, kind="ExternalInput").ap()
    bet_h = nc.dram_tensor("betb", [128, D], F32, kind="ExternalInput").ap()
    msel_h = nc.dram_tensor("msel", [128, 4], F32, kind="ExternalInput").ap()
    identh = nc.dram_tensor("identc", [128, 128], F32, kind="ExternalInput").ap()
    mtrih = nc.dram_tensor("mtric", [128, 128], F32, kind="ExternalInput").ap()
    out_h = nc.dram_tensor("out", [TC, D], F32, kind="ExternalOutput").ap()

    sp_q = nc.dram_tensor("sp_q", [NG * 128, 4096], BF16).ap()
    sp_v = nc.dram_tensor("sp_v", [NG * 128, 4096], BF16).ap()
    sp_a = nc.dram_tensor("sp_a", [NG * 128, 1280], BF16).ap()
    sp_x2 = nc.dram_tensor("sp_x2", [TC, D], F32).ap()
    sp_s = nc.dram_tensor("sp_s", [NG * 1024, 1024], BF16).ap()
    sp_rt = nc.dram_tensor("sp_rt", [NT * 128, 1024], BF16).ap()
    sp_t = nc.dram_tensor("sp_t", [1024, 1024], BF16).ap()
    cc_in_t = nc.dram_tensor("cc_in", [1025, 1024], BF16)
    cc_out_t = nc.dram_tensor("cc_out", [4 * 1025, 1024], BF16)
    mx_in_t = nc.dram_tensor("mx_in", [1, 1], F32)
    mx_out_t = nc.dram_tensor("mx_out", [1, 1], F32)
    cc_in, cc_out = cc_in_t.ap(), cc_out_t.ap()
    mx_in, mx_out = mx_in_t.ap(), mx_out_t.ap()

    SB = [
        ("w1", 8192, BF16), ("w2", 8192, BF16), ("w3", 8192, BF16),
        ("kvr", 8192, F32), ("gbc", 1024, F32), ("bbc", 1024, F32),
        ("xg0", 1024, F32), ("xg1", 1024, F32), ("xg2", 1024, F32),
        ("xnf0", 1024, F32), ("xnf1", 1024, F32),
        ("xnb0", 1024, BF16), ("xnb1", 1024, BF16),
        ("xnT0", 4096, BF16), ("xnT1", 4096, BF16),
        ("qT", 4096, BF16), ("kT", 4096, BF16),
        ("ktok", 4096, BF16), ("vtok", 4096, BF16),
        ("aT0", 1280, BF16), ("aT1", 1280, BF16),
        ("mn0", 512, F32), ("mn1", 512, F32),
        ("Rb0", 512, BF16), ("Rb1", 512, BF16), ("sgb", 512, BF16),
        ("u", 1024, F32), ("i32h", 512, I32),
        ("x2t0", 1024, F32), ("x2t1", 1024, F32),
        ("rT0", 1024, BF16), ("rT1", 1024, BF16),
        ("bqc", 8, F32), ("bkc", 8, F32), ("bvb", 1024, BF16), ("boeff", 1024, BF16),
        ("mselc", 4, F32), ("stats", 96, F32), ("scr0", 8, F32), ("scr1", 8, F32),
        ("kscb2", 8, BF16),
        ("ksg", 64, F32), ("kst", 8, F32), ("ksA", 8, F32), ("ksB", 8, F32),
        ("kscb", 8, BF16), ("amx0", 1, F32), ("amx1", 1, F32),
        ("ident", 128, F32), ("mask_ut", 128, F32),
        ("ones_cb", 1, BF16), ("ones_rb", 128, BF16),
        ("rrowb", 512, BF16), ("smf", 16, F32),
        ("tw32", 32, F32), ("tn32", 32, F32), ("tc32", 32, F32),
        ("bcs", 2, F32), ("dcol", 1, F32),
        ("ta", 256, F32), ("tb2", 256, F32), ("tc2", 256, F32), ("td", 256, F32),
        ("tf", 256, BF16),
    ]
    used = sum(w * {F32: 4, BF16: 2, I32: 4}[d] for _, w, d in SB)
    assert used <= 207000, f"SBUF over budget: {used}"

    sems = ["M", "V", "S", "C",
            "MS", "Gc", "Lx0", "Lx1", "Lx2", "Gx0", "Gx1", "WP0", "WP1", "Ta", "Qt",
            "TT00", "TT01", "TT10", "TT11", "Qx0", "Qx1", "Qq", "Qv", "KT",
            "Qa0", "Qa1", "Qs0", "Qs1", "Qm", "Lt0", "Lt1", "Qw0", "Qw1",
            "Lm", "PS", "Gk", "Cr",
            "Sg0", "Sg1", "Ld0", "Ld1", "X20", "X21", "Qr0", "Qr1", "Qo0", "Qo1",
            "Lr0", "Lr1", "Rs"]
    cnt = {s: 0 for s in sems}
    plan = {"sync": [], "scalar": [], "vector": [], "tensor": [], "gpsimd": []}
    tn = {}
    free = {}

    def ap(name, sl=None):
        t = tn[name]
        return t[:, :] if sl is None else t[:, sl]

    def rowap(name, r, sl):
        return tn[name][r:r + 1, sl]

    def emit(eng, fn):
        plan[eng].append(fn)

    def wait(eng, sem, val):
        if val > 0:
            emit(eng, lambda e, s=sem, v=val: e.wait_ge(tn["sem_" + s], v))

    def gate(eng, key):
        for s, v in free.pop(key, {}).items():
            wait(eng, s, v)

    def setfree(key, *pairs):
        d = free.setdefault(key, {})
        for s, v in pairs:
            d[s] = max(d.get(s, 0), v)

    def dma(eng, sem, out, in_):
        emit(eng, lambda e, o=out, i=in_, s=sem: e.dma_start(out=o, in_=i).then_inc(tn["sem_" + s], 16))
        cnt[sem] += 16
        return cnt[sem]

    def dmat(eng, sem, out, in_):
        emit(eng, lambda e, o=out, i=in_, s=sem: e.dma_start_transpose(o, i).then_inc(tn["sem_" + s], 16))
        cnt[sem] += 16
        return cnt[sem]

    def mm(lhsT, rhs, out, start, stop):
        if stop:
            emit("tensor", lambda e, a=lhsT, b=rhs, o=out, st=start:
                 e.matmul(o, a, b, start=st, stop=True).then_inc(tn["sem_M"], 1))
            cnt["M"] += 1
        else:
            emit("tensor", lambda e, a=lhsT, b=rhs, o=out, st=start:
                 e.matmul(o, a, b, start=st, stop=False))
        return cnt["M"]

    def vop(fn):
        wait("vector", "V", cnt["V"])
        emit("vector", lambda e, f=fn: f(e).then_inc(tn["sem_V"], 1))
        cnt["V"] += 1
        return cnt["V"]

    def sop(fn):
        wait("scalar", "S", cnt["S"])
        emit("scalar", lambda e, f=fn: f(e).then_inc(tn["sem_S"], 1))
        cnt["S"] += 1
        return cnt["S"]

    def gop(fn):
        wait("gpsimd", "C", cnt["C"])
        emit("gpsimd", lambda e, f=fn: f(e).then_inc(tn["sem_C"], 1))
        cnt["C"] += 1
        return cnt["C"]

    import contextlib
    with contextlib.ExitStack() as stack:
        for name, w, dt in SB:
            tn[name] = stack.enter_context(nc.sbuf_tensor("sb_" + name, [128, w], dt))
        for pname in ["ps_q0", "ps_q1", "ps_v0", "ps_v1", "ps_a", "ps_s0", "ps_s1"]:
            tn[pname] = stack.enter_context(nc.psum_tensor(pname, [128, 512], F32))
        for s in sems:
            tn["sem_" + s] = stack.enter_context(nc.semaphore("sem" + s))
        block = stack.enter_context(nc.Block())

        dma("sync", "MS", ap("bqc"), bqc_h)
        dma("sync", "MS", ap("bkc"), bkc_h)
        dma("sync", "MS", ap("gbc"), gam_h)
        dma("sync", "MS", ap("bbc"), bet_h)
        dma("sync", "MS", ap("mselc"), msel_h)
        dma("sync", "MS", ap("ident"), identh)
        dma("sync", "MS", ap("mask_ut"), mtrih)
        lp = cnt["MS"]
        dma("gpsimd", "Gc", tn["bvb"][0:1, :], bv_h)

        for i in range(3):
            dma("sync", f"Lx{i}", ap(f"xg{i}"), x_h[i * 128:(i + 1) * 128, :])
        wait("scalar", "MS", lp)
        vop(lambda e: e.memset(ap("ones_cb"), 1.0))
        vop(lambda e: e.memset(tn["ones_rb"][0:2, :], 1.0))
        vop(lambda e: e.memset(ap("kvr"), 0.0))
        vop(lambda e: e.memset(ap("amx0"), 0.0))
        vop(lambda e: e.memset(ap("ksA"), 0.0))
        wait("vector", "MS", lp)

        for wi, (wh, wt) in enumerate([(wq_h, "w1"), (wk_h, "w2"), (wv_h, "w3")]):
            for ob in range(8):
                par = (wi * 8 + ob) % 2
                st = f"xnb{par}"
                gate("gpsimd", st)
                gc = dma("gpsimd", f"Gx{par}", ap(st), wh[ob * 128:(ob + 1) * 128, :])
                wait("sync", f"Gx{par}", gc)
                dest = tn[wt].ap().rearrange("p (d o) -> p d o", d=8)[:, :, ob * 128:(ob + 1) * 128]
                lc = dmat("sync", f"WP{par}", dest, ap(st))
                setfree(st, (f"WP{par}", lc))
        wp_L = (cnt["WP0"], cnt["WP1"])

        mid = {}

        def tern_pass1(j):
            ob, sub = divmod(j, 4)
            gate("sync", "ta")
            ld = dma("sync", "Ta", ap("ta"), wf_h[ob * 128:(ob + 1) * 128, sub * 256:(sub + 1) * 256])
            gate("scalar", "tb2")
            wait("scalar", "Ta", ld)
            sc = sop(lambda e, o=ap("tb2"), a=ap("ta"), acc=ap("tw32", slice(j, j + 1)):
                     e.activation(o, a, ACT.Abs, accum_out=acc))
            setfree("ta", ("S", sc))
            setfree("tb2", ("S", sc))
            mid["tp1_s"] = sc

        def tern_delta():
            gate("vector", "scr0")
            wait("vector", "S", mid["tp1_s"])
            v1 = vop(lambda e, o=ap("scr0", slice(0, 1)), a=ap("tw32"): e.reduce_sum(o, a, AX))
            gate("tensor", "ps_a")
            wait("tensor", "V", v1)
            wait("tensor", "MS", lp)
            mc = mm(ap("scr0", slice(0, 1)), ap("ident"), tn["ps_a"][0:1, 0:128], True, True)
            setfree("scr0", ("M", mc))
            wait("vector", "M", mc)
            vop(lambda e, o=rowap("td", 0, slice(0, 128)), a=tn["ps_a"][0:1, 0:128]:
                e.tensor_copy(o, a))
            vop(lambda e, o=rowap("smf", 0, slice(2, 3)), a=rowap("td", 0, slice(0, 128)):
                e.reduce_sum(o, a, AX))
            v2 = vop(lambda e, o=rowap("smf", 0, slice(3, 4)), a=rowap("smf", 0, slice(2, 3)):
                     e.tensor_scalar(o, a, 0.7 / (1024.0 * 1024.0), None, ALU.mult))
            setfree("ps_a", ("V", cnt["V"]))
            gate("tensor", "ps_a")
            wait("tensor", "V", v2)
            mc2 = mm(tn["mask_ut"][0:1, :], rowap("smf", 0, slice(3, 4)), tn["ps_a"][:, 0:1], True, True)
            wait("vector", "M", mc2)
            v3 = vop(lambda e, o=ap("dcol"), a=tn["ps_a"][:, 0:1]: e.tensor_copy(o, a))
            setfree("ps_a", ("V", v3))
            mid["dcol_v"] = v3

        def tern_pass2(j):
            ob, sub = divmod(j, 4)
            gate("sync", "ta")
            ld = dma("sync", "Ta", ap("ta"), wf_h[ob * 128:(ob + 1) * 128, sub * 256:(sub + 1) * 256])
            gate("scalar", "tb2")
            wait("scalar", "Ta", ld)
            s_ab = sop(lambda e, o=ap("tb2"), a=ap("ta"): e.activation(o, a, ACT.Abs))
            wait("vector", "S", s_ab)
            wait("vector", "V", mid["dcol_v"])
            vop(lambda e, o=ap("tc2"), a=ap("tb2"), d=ap("dcol"):
                e.tensor_scalar(o, a, d, None, ALU.is_gt))
            vop(lambda e, o=ap("tc32", slice(j, j + 1)), a=ap("tc2"): e.reduce_sum(o, a, AX))
            v_d = vop(lambda e, o=ap("td"), a=ap("tb2"), b=ap("tc2"):
                      e.tensor_tensor(o, a, b, ALU.mult))
            vop(lambda e, o=ap("tn32", slice(j, j + 1)), a=ap("td"): e.reduce_sum(o, a, AX))
            wait("scalar", "V", v_d)
            s_sg = sop(lambda e, o=ap("tb2"), a=ap("ta"): e.activation(o, a, ACT.Sign))
            setfree("ta", ("S", s_sg))
            gate("vector", "tf")
            wait("vector", "S", s_sg)
            v_t = vop(lambda e, o=ap("tf"), a=ap("tb2"), b=ap("tc2"):
                      e.tensor_tensor(o, a, b, ALU.mult))
            setfree("tb2", ("V", v_t))
            wait("scalar", "V", v_t)
            q_t = dma("scalar", "Qt", sp_t[ob * 128:(ob + 1) * 128, sub * 256:(sub + 1) * 256], ap("tf"))
            setfree("tf", ("Qt", q_t))

        qk_last_v = {}
        for g in range(NGRUN):
            gp = g % 2
            xnT = f"xnT{gp}"
            for i in range(4):
                ti = g * 4 + i
                slot = f"xg{ti % 3}"
                tp = f"xnf{i % 2}"
                tb = f"xnb{i % 2}"
                x2 = f"x2t{i % 2}"
                scr = f"scr{i % 2}"
                if ti >= 3:
                    gate("sync", slot)
                    ld = dma("sync", f"Lx{ti % 3}", ap(slot), x_h[ti * 128:(ti + 1) * 128, :])
                else:
                    ld = 16
                gate("scalar", tp)
                gate("scalar", scr)
                wait("scalar", f"Lx{ti % 3}", ld)
                sop(lambda e, o=ap(tp), x=ap(slot), a=ap(scr, slice(0, 1)):
                    e.activation(o, x, ACT.Identity, accum_out=a))
                s_st = sop(lambda e, o=ap(tp), x=ap(slot), a=ap(scr, slice(1, 2)):
                           e.activation(o, x, ACT.Square, accum_out=a))
                stc = ti * 3
                mneg = ap("stats", slice(stc, stc + 1))
                r1c = ap("stats", slice(stc + 1, stc + 2))
                r2c = ap("stats", slice(stc + 2, stc + 3))
                wait("vector", "S", s_st)
                vop(lambda e, o=mneg, a=ap(scr, slice(0, 1)):
                    e.tensor_scalar(o, a, -1.0 / 1024.0, None, ALU.mult))
                vop(lambda e, o=ap(scr, slice(2, 3)), a=ap(scr, slice(1, 2)):
                    e.tensor_scalar(o, a, 1.0 / 1024.0, None, ALU.mult))
                vop(lambda e, o=ap(scr, slice(3, 4)), a=mneg:
                    e.tensor_tensor(o, a, a, ALU.mult))
                v_var = vop(lambda e, o=ap(scr, slice(4, 5)), a=ap(scr, slice(2, 3)), b=ap(scr, slice(3, 4)):
                            e.tensor_tensor(o, a, b, ALU.subtract))
                vop(lambda e, o=ap(scr, slice(6, 7)), a=ap(scr, slice(4, 5)):
                    e.tensor_scalar(o, a, EPS1, None, ALU.add))
                wait("scalar", "V", cnt["V"])
                s_q1 = sop(lambda e, o=ap(scr, slice(5, 6)), a=ap(scr, slice(6, 7)):
                           e.activation(o, a, ACT.Sqrt))
                wait("vector", "S", s_q1)
                vop(lambda e, o=r1c, a=ap(scr, slice(5, 6)): e.reciprocal(o, a))
                vop(lambda e, o=ap(scr, slice(5, 6)), a=ap(scr, slice(6, 7)):
                    e.reciprocal(o, a))
                vop(lambda e, o=ap(scr, slice(7, 8)), a=ap(scr, slice(4, 5)), b=ap(scr, slice(5, 6)):
                    e.tensor_tensor(o, a, b, ALU.mult))
                v_v2 = vop(lambda e, o=ap(scr, slice(6, 7)), a=ap(scr, slice(7, 8)):
                           e.tensor_scalar(o, a, EPS1, None, ALU.add))
                wait("scalar", "V", v_v2)
                s_r2 = sop(lambda e, o=ap(scr, slice(5, 6)), a=ap(scr, slice(6, 7)):
                           e.activation(o, a, ACT.Sqrt))
                wait("vector", "S", s_r2)
                vop(lambda e, o=r2c, a=ap(scr, slice(5, 6)): e.reciprocal(o, a))
                v_xn = vop(lambda e, o=ap(tp), x=ap(slot), m=mneg, r=r1c:
                           e.tensor_scalar(o, x, m, r, ALU.add, ALU.mult))
                gate("gpsimd", tb)
                wait("gpsimd", "V", v_xn)
                g_c = dma("gpsimd", f"Gx{i % 2}", ap(tb), ap(tp))
                if i == 0:
                    gate("sync", xnT)
                wait("sync", f"Gx{i % 2}", g_c)
                dest = tn[xnT].ap().rearrange("p (d t) -> p d t", d=8)[:, :, i * 128:(i + 1) * 128]
                ttsem = f"TT{g % 2}{i % 2}"
                dmat("sync", ttsem, dest, ap(tb))
                setfree(tb, (ttsem, cnt[ttsem]))
                if i == 3:
                    mid[("tp_done", g)] = [(f"TT{g % 2}0", cnt[f"TT{g % 2}0"]),
                                           (f"TT{g % 2}1", cnt[f"TT{g % 2}1"])]
                v_u = vop(lambda e, o=ap("u"), a=ap(tp), b=ap("gbc"):
                          e.tensor_tensor(o, a, b, ALU.mult))
                gate("vector", x2)
                v_x2 = vop(lambda e, o=ap(x2), a=ap("u"), r=r2c, b=ap("bbc"):
                           e.scalar_tensor_tensor(o, a, r, b, ALU.mult, ALU.add))
                setfree(tp, (f"Gx{i % 2}", g_c), ("V", v_u))
                vop(lambda e, o=ap(scr, slice(0, 1)), a=ap(x2):
                    e.reduce_max(o, a, AX, apply_absolute_value=True))
                v_am = vop(lambda e, o=ap(f"amx{(ti + 1) % 2}"), a=ap(scr, slice(0, 1)), p=ap(f"amx{ti % 2}"):
                           e.tensor_scalar(o, a, p, None, ALU.max))
                setfree(scr, ("S", s_r2), ("V", v_am))
                wait("scalar", "V", v_x2)
                q_sp = dma("scalar", f"Qx{i % 2}", sp_x2[ti * 128:(ti + 1) * 128, :], ap(x2))
                setfree(x2, (f"Qx{i % 2}", q_sp))
                setfree(slot, ("V", v_xn), ("S", s_st))
            if TERN_ON:
                if g < NGRUN // 2 or NGRUN == 1:
                    for j in range(g * 8, g * 8 + 8):
                        tern_pass1(j)
                    if g == max(NGRUN // 2, 1) - 1:
                        tern_delta()
                else:
                    for j in range((g - NGRUN // 2) * 8, (g - NGRUN // 2) * 8 + 8):
                        tern_pass2(j)
            if P1PART >= 2:
                for proj, (wt, dst, bc_) in enumerate([("w1", "qT", "bqc"), ("w2", "kT", "bkc")]):
                    for ob in range(8):
                        pb = f"ps_q{ob % 2}"
                        gate("tensor", pb)
                        if ob == 0:
                            for sm, vv in mid[("tp_done", g)]:
                                wait("tensor", sm, vv)
                            if g == 0 and proj == 0:
                                wait("tensor", "WP0", wp_L[0])
                                wait("tensor", "WP1", wp_L[1])
                        for dc in range(8):
                            mmc = mm(tn[wt][:, dc * 1024 + ob * 128: dc * 1024 + (ob + 1) * 128],
                                     tn[xnT][:, dc * 512:(dc + 1) * 512],
                                     ap(pb), start=(dc == 0), stop=(dc == 7))
                        mnb = f"mn{ob % 2}"
                        rbb = f"Rb{ob % 2}"
                        gate("vector", mnb)
                        wait("vector", "M", mmc)
                        v_mn = vop(lambda e, o=ap(mnb), p=ap(pb), b=ap(bc_, slice(ob, ob + 1)):
                                   e.tensor_scalar(o, p, b, 0.0, ALU.add, ALU.min))
                        if ob == 0:
                            gate("scalar", dst)
                        wait("scalar", "V", v_mn)
                        s_e = sop(lambda e, o=tn[dst][:, ob * 512:(ob + 1) * 512], a=ap(mnb):
                                  e.activation(o, a, ACT.Exp))
                        setfree(mnb, ("S", s_e))
                        gate("scalar", rbb)
                        wait("scalar", "M", mmc)
                        s_r = sop(lambda e, o=ap(rbb), p=ap(pb), b=ap(bc_, slice(ob, ob + 1)):
                                  e.activation(o, p, ACT.Relu, bias=b, scale=1.0))
                        setfree(pb, ("V", v_mn), ("S", s_r))
                        wait("vector", "S", s_r)
                        v_q = vop(lambda e, o=tn[dst][:, ob * 512:(ob + 1) * 512], a=ap(rbb):
                                  e.tensor_tensor(o, o, a, ALU.add))
                        setfree(rbb, ("V", v_q))
                        qk_last_v[(g, dst)] = v_q
            if P1PART >= 3:
                for tb4 in range(4):
                    for eh in range(2):
                        idx = tb4 * 2 + eh
                        pb = f"ps_v{idx % 2}"
                        gate("tensor", pb)
                        if g == 0 and idx == 0:
                            wait("tensor", "Gc", 16)
                        for dc in range(8):
                            mm(tn[xnT][:, dc * 512 + tb4 * 128: dc * 512 + (tb4 + 1) * 128],
                               tn["w3"][:, dc * 1024 + eh * 512: dc * 1024 + (eh + 1) * 512],
                               ap(pb), start=(dc == 0), stop=False)
                        mmc = mm(tn["ones_rb"][0:1, :],
                                 tn["bvb"][0:1, eh * 512:(eh + 1) * 512],
                                 ap(pb), start=False, stop=True)
                        if idx == 0:
                            gate("scalar", "vtok")
                        wait("scalar", "M", mmc)
                        s_v = sop(lambda e, o=tn["vtok"][:, tb4 * 1024 + eh * 512: tb4 * 1024 + (eh + 1) * 512],
                                  p=ap(pb): e.copy(o, p))
                        setfree(pb, ("S", s_v))
                v_evac_s = cnt["S"]
                mid[("xnT_free", g)] = cnt["M"]
                setfree(xnT, ("M", cnt["M"]))
            if P1PART >= 4:
                for dkb in range(8):
                    vop(lambda e, o=ap("ksg", slice(g * 8 + dkb, g * 8 + dkb + 1)),
                        a=tn["kT"][:, dkb * 512:(dkb + 1) * 512]:
                        e.reduce_sum(o, a, AX))
                ks_v = cnt["V"]
                wait("scalar", "V", qk_last_v[(g, "qT")])
                q_spq = dma("scalar", "Qq", sp_q[g * 128:(g + 1) * 128, :], ap("qT"))
                wait("scalar", "S", v_evac_s)
                q_spv = dma("scalar", "Qv", sp_v[g * 128:(g + 1) * 128, :], ap("vtok"))
                if g == 0:
                    gate("sync", "ktok")
                wait("sync", "V", qk_last_v[(g, "kT")])
                for dkb in range(8):
                    dest = tn["ktok"].ap().rearrange("p (s d) -> p s d", s=4)[:, :, dkb * 128:(dkb + 1) * 128]
                    dmat("sync", "KT", dest, tn["kT"][:, dkb * 512:(dkb + 1) * 512])
                kt_L = cnt["KT"]
            if P1PART >= 5:
                aT = f"aT{gp}"
                gate("vector", aT)
                a_evac = []
                for sb in range(4):
                    pb = "ps_a" if sb % 2 == 0 else "ps_s0"
                    gate("tensor", pb)
                    if sb == 0:
                        wait("tensor", "V", qk_last_v[(g, "qT")])
                        wait("tensor", "V", qk_last_v[(g, "kT")])
                    for dc in range(8):
                        mmc = mm(tn["kT"][:, dc * 512 + sb * 128: dc * 512 + (sb + 1) * 128],
                                 tn["qT"][:, dc * 512 + sb * 128: (dc + 1) * 512],
                                 tn[pb][:, sb * 128:512], start=(dc == 0), stop=(dc == 7))
                    wait("vector", "M", mmc)
                    vop(lambda e, o=tn[aT][:, PK[sb]:PK[sb] + 128],
                        p=tn[pb][:, sb * 128:sb * 128 + 128], m=ap("mask_ut"):
                        e.tensor_tensor(o, p, m, ALU.mult))
                    if sb < 3:
                        v_a = vop(lambda e, o=tn[aT][:, PK[sb] + 128:PK[sb] + PW[sb]],
                                  p=tn[pb][:, sb * 128 + 128:512]:
                                  e.tensor_copy(o, p))
                    else:
                        v_a = cnt["V"]
                    setfree(pb, ("V", v_a))
                    a_evac.append(v_a)
                wait("scalar", "V", a_evac[-1])
                q_spa = dma("scalar", f"Qa{g % 2}", sp_a[g * 128:(g + 1) * 128, :], tn[aT][:, :])
                setfree(aT, (f"Qa{g % 2}", q_spa))
                setfree("qT", ("Qq", q_spq), ("M", cnt["M"]))
                setfree("kT", ("M", cnt["M"]), ("V", ks_v), ("KT", kt_L))
            if P1PART >= 6:
                for dkb in range(8):
                    for eh in range(2):
                        idx = dkb * 2 + eh
                        pb = f"ps_s{idx % 2}"
                        gate("tensor", pb)
                        if idx == 0:
                            wait("tensor", "KT", kt_L)
                            wait("tensor", "S", v_evac_s)
                        for sb in range(4):
                            mmc = mm(tn["ktok"][:, sb * 1024 + dkb * 128: sb * 1024 + (dkb + 1) * 128],
                                     tn["vtok"][:, sb * 1024 + eh * 512: sb * 1024 + (eh + 1) * 512],
                                     ap(pb), start=(sb == 0), stop=(sb == 3))
                        fr = []
                        if S_SUB >= 1:
                            wait("vector", "M", mmc)
                            v_kv = vop(lambda e, o=tn["kvr"][:, dkb * 1024 + eh * 512: dkb * 1024 + (eh + 1) * 512],
                                       p=ap(pb): e.tensor_tensor(o, o, p, ALU.add))
                            fr.append(("V", v_kv))
                        if S_SUB >= 2:
                            sg = f"rT{idx % 2}"
                            gate("vector", sg + "_s")
                            wait("vector", "M", mmc)
                            s_c = vop(lambda e, o=tn[sg][:, 0:512], p=ap(pb): e.tensor_copy(o, p))
                            fr.append(("V", s_c))
                        setfree(pb, *fr)
                        if S_SUB >= 3:
                            wait("scalar", "V", s_c)
                            q_s = dma("scalar", f"Qs{idx % 2}",
                                      sp_s[g * 1024 + dkb * 128: g * 1024 + (dkb + 1) * 128,
                                           eh * 512:(eh + 1) * 512],
                                      tn[sg][:, 0:512])
                            setfree(sg + "_s", (f"Qs{idx % 2}", q_s))
                setfree("ktok", ("M", cnt["M"]))
                setfree("vtok", ("M", cnt["M"]), ("Qv", q_spv))
                mid[("kvr_v", g)] = cnt["V"]

        if STAGE >= 3:
            vop(lambda e, o=ap("kst"), a=ap("ksg", slice(0, 8)), b=ap("ksg", slice(8, 16)):
                e.tensor_tensor(o, a, b, ALU.add))
            for g2 in range(2, 8):
                vop(lambda e, o=ap("kst"), a=ap("kst"), b=ap("ksg", slice(g2 * 8, g2 * 8 + 8)):
                    e.tensor_tensor(o, a, b, ALU.add))
            kst_v = cnt["V"]
            gate("tensor", "ps_a")
            wait("tensor", "V", cnt["V"])
            mc = mm(ap("amx0"), ap("ident"), tn["ps_a"][0:1, 0:128], True, True)
            wait("vector", "M", mc)
            vop(lambda e, o=rowap("td", 0, slice(0, 128)), a=tn["ps_a"][0:1, 0:128]:
                e.tensor_copy(o, a))
            v_mx = vop(lambda e, o=rowap("smf", 0, slice(0, 1)), a=rowap("td", 0, slice(0, 128)):
                       e.reduce_max(o, a, AX))
            setfree("ps_a", ("V", cnt["V"]))
            wait("scalar", "V", v_mx)
            q_mx = dma("scalar", "Qm", mx_in, rowap("smf", 0, slice(0, 1)))
            gate("vector", "scr0")
            vop(lambda e, o=ap("scr0", slice(0, 1)), a=ap("tn32"): e.reduce_sum(o, a, AX))
            v_nc = vop(lambda e, o=ap("scr0", slice(1, 2)), a=ap("tc32"): e.reduce_sum(o, a, AX))
            gate("tensor", "ps_a")
            wait("tensor", "V", v_nc)
            mm(ap("scr0", slice(0, 1)), ap("ident"), tn["ps_a"][0:1, 0:128], True, True)
            mc2 = mm(ap("scr0", slice(1, 2)), ap("ident"), tn["ps_a"][0:1, 128:256], True, True)
            wait("vector", "M", mc2)
            vop(lambda e, o=rowap("td", 0, slice(0, 256)), a=tn["ps_a"][0:1, 0:256]:
                e.tensor_copy(o, a))
            vop(lambda e, o=rowap("smf", 0, slice(5, 6)), a=rowap("td", 0, slice(0, 128)):
                e.reduce_sum(o, a, AX))
            vop(lambda e, o=rowap("smf", 0, slice(6, 7)), a=rowap("td", 0, slice(128, 256)):
                e.reduce_sum(o, a, AX))
            setfree("ps_a", ("V", cnt["V"]))
            vop(lambda e, o=rowap("smf", 0, slice(7, 8)), a=rowap("smf", 0, slice(6, 7)):
                e.tensor_scalar(o, a, 1.0, None, ALU.max))
            vop(lambda e, o=rowap("smf", 0, slice(8, 9)), a=rowap("smf", 0, slice(7, 8)):
                e.reciprocal(o, a))
            v_al = vop(lambda e, o=rowap("smf", 0, slice(9, 10)), a=rowap("smf", 0, slice(5, 6)),
                       b=rowap("smf", 0, slice(8, 9)): e.tensor_tensor(o, a, b, ALU.mult))
            wait("gpsimd", "V", mid[("kvr_v", NGRUN - 1)])
            for dkb in range(8):
                dma("gpsimd", "Gc", cc_in[dkb * 128:(dkb + 1) * 128, 0:1024],
                    tn["kvr"][:, dkb * 1024:(dkb + 1) * 1024])
            gate("tensor", "ps_s1")
            wait("tensor", "V", kst_v)
            mc_ks = mm(ap("kst"), ap("ident"), tn["ps_s1"][0:8, 0:128], True, True)
            wait("vector", "M", mc_ks)
            gate("vector", "rT0_s")
            v_ksr = vop(lambda e, o=tn["rT0"][0:8, 0:128], a=tn["ps_s1"][0:8, 0:128]:
                        e.tensor_copy(o, a))
            setfree("ps_s1", ("V", v_ksr))
            wait("gpsimd", "V", v_ksr)
            dma("gpsimd", "Gc",
                cc_in[1024:1025, :].rearrange("r (d p) -> (r d) p", d=8),
                tn["rT0"][0:8, 0:128])
            g_cc = cnt["Gc"]
            setfree("rT0_s", ("Gc", g_cc))
            wait("gpsimd", "Gc", g_cc)
            wait("gpsimd", "Qm", q_mx)
            emit("gpsimd", lambda e: e.collective_compute(
                "AllReduce", ALU.max,
                replica_groups=[list(range(8))],
                ins=[mx_in_t.ap()], outs=[mx_out_t.ap()]).then_inc(tn["sem_Cr"], 1))
            cnt["Cr"] += 1
            cc_done = cnt["Cr"]
            emit("gpsimd", lambda e: e.collective_compute(
                "AllGather", ALU.bypass,
                replica_groups=[[0, 1, 2, 3], [4, 5, 6, 7]],
                ins=[cc_in_t.ap()], outs=[cc_out_t.ap()]).then_inc(tn["sem_C"], 1))
            cnt["C"] += 1
            ag_done = cnt["C"]

            for ob in range(8):
                par = ob % 2
                st = f"xnb{par}"
                gate("gpsimd", st)
                gc = dma("gpsimd", f"Gx{par}", ap(st), wo_h[ob * 128:(ob + 1) * 128, :])
                wait("sync", f"Gx{par}", gc)
                wait("sync", "M", mid[("xnT_free", NGRUN - 1)])
                dest = tn["w1"].ap().rearrange("p (d o) -> p d o", d=8)[:, :, ob * 128:(ob + 1) * 128]
                lc = dmat("sync", f"WP{par}", dest, ap(st))
                setfree(st, (f"WP{par}", lc))
            wo_WP = (cnt["WP0"], cnt["WP1"])
            for ob in range(8):
                par = ob % 2
                st = f"xnb{par}"
                gate("sync", st)
                wait("sync", "Qt", cnt["Qt"])
                ld = dma("sync", f"Lt{par}", ap(st), sp_t[ob * 128:(ob + 1) * 128, :])
                wait("scalar", f"Lt{par}", ld)
                dest = tn["w2"].ap().rearrange("p (d o) -> p d o", d=8)[:, :, ob * 128:(ob + 1) * 128]
                lc = dmat("scalar", f"Qw{par}", dest, ap(st))
                setfree(st, (f"Qw{par}", lc))
            w2_Q = (cnt["Qw0"], cnt["Qw1"])

            wait("gpsimd", "Cr", cc_done)
            l_mx = dma("gpsimd", "Lm", rowap("smf", 0, slice(10, 11)), mx_out)
            wait("vector", "Lm", l_mx)
            vop(lambda e, o=rowap("smf", 0, slice(11, 12)), a=rowap("smf", 0, slice(10, 11)):
                e.tensor_scalar(o, a, 1e-8, None, ALU.max))
            vop(lambda e, o=rowap("smf", 0, slice(12, 13)), a=rowap("smf", 0, slice(11, 12)):
                e.tensor_scalar(o, a, 1.0 / 127.0, None, ALU.mult))
            vop(lambda e, o=rowap("smf", 0, slice(13, 14)), a=rowap("smf", 0, slice(12, 13)):
                e.reciprocal(o, a))
            wait("vector", "V", v_al)
            vop(lambda e, o=rowap("smf", 0, slice(14, 15)), a=rowap("smf", 0, slice(9, 10)),
                b=rowap("smf", 0, slice(12, 13)): e.tensor_tensor(o, a, b, ALU.mult))
            v_as = vop(lambda e, o=rowap("smf", 0, slice(15, 16)), a=rowap("smf", 0, slice(14, 15)):
                       e.reciprocal(o, a))
            gate("tensor", "ps_s1")
            wait("tensor", "V", v_as)
            mc3 = mm(tn["mask_ut"][0:1, :], rowap("smf", 0, slice(13, 15)), tn["ps_s1"][:, 0:2], True, True)
            wait("vector", "M", mc3)
            v_bcs = vop(lambda e, o=ap("bcs"), a=tn["ps_s1"][:, 0:2]: e.tensor_copy(o, a))
            setfree("ps_s1", ("V", v_bcs))
            l_bo = dma("sync", "MS", rowap("xnf0", 0, slice(0, 1024)), bo_h)
            wait("vector", "MS", l_bo)
            vop(lambda e, o=rowap("xnf1", 0, slice(0, 1024)), a=rowap("xnf0", 0, slice(0, 1024)),
                s=rowap("smf", 0, slice(15, 16)): e.tensor_scalar(o, a, s, None, ALU.mult))
            vop(lambda e, o=tn["boeff"][0:1, :], a=rowap("xnf1", 0, slice(0, 1024)):
                e.tensor_copy(o, a))
            vop(lambda e, o=rowap("xnf0", 0, slice(0, 1024)), a=tn["boeff"][0:1, :]:
                e.tensor_copy(o, a))
            v_bo = vop(lambda e, o=tn["bvb"][0:1, :], a=rowap("xnf1", 0, slice(0, 1024)),
                       b=rowap("xnf0", 0, slice(0, 1024)): e.tensor_tensor(o, a, b, ALU.subtract))
            wait("sync", "M", mid[("xnT_free", NGRUN - 1)])
            wait("sync", "C", ag_done)
            wait("gpsimd", "C", ag_done)
            wait("gpsimd", "M", mid[("xnT_free", NGRUN - 1)])
            gate("sync", "qT")
            wait("vector", "Gx0", cnt["Gx0"])
            wait("vector", "Gx1", cnt["Gx1"])
            v_kv = 0
            for dkb in range(8):
                if dkb > 0:
                    wait("sync", "V", v_kv)
                for j in range(4):
                    dma("sync", "PS", tn["qT"][:, j * 1024:(j + 1) * 1024],
                        cc_out[j * 1025 + dkb * 128: j * 1025 + (dkb + 1) * 128, 0:1024])
                l_g = cnt["PS"]
                wait("vector", "PS", l_g)
                vop(lambda e, o=ap("u"), a=tn["qT"][:, 0:1024], m=ap("mselc", slice(0, 1)):
                    e.tensor_scalar(o, a, m, None, ALU.mult))
                vop(lambda e, o=ap("xnf0"), a=tn["qT"][:, 1024:2048], m=ap("mselc", slice(1, 2)), c=ap("u"):
                    e.scalar_tensor_tensor(o, a, m, c, ALU.mult, ALU.add))
                vop(lambda e, o=ap("u"), a=tn["qT"][:, 2048:3072], m=ap("mselc", slice(2, 3)), c=ap("xnf0"):
                    e.scalar_tensor_tensor(o, a, m, c, ALU.mult, ALU.add))
                v_kv = vop(lambda e, o=tn["kvr"][:, dkb * 1024:(dkb + 1) * 1024],
                           a=tn["qT"][:, 3072:4096], m=ap("mselc", slice(3, 4)), c=ap("u"):
                           e.scalar_tensor_tensor(o, a, m, c, ALU.mult, ALU.add))
                wait("gpsimd", "V", v_kv)
                dma("gpsimd", "Gk", tn["w3"][:, dkb * 1024:(dkb + 1) * 1024],
                    tn["kvr"][:, dkb * 1024:(dkb + 1) * 1024])
            kvb_G = cnt["Gk"]
            setfree("qT", ("V", cnt["V"]))
            gate("sync", "kT")
            wait("sync", "V", v_kv)
            for j in range(4):
                dma("sync", "PS", tn["kT"][0:1, j * 1024:(j + 1) * 1024],
                    cc_out[j * 1025 + 1024: j * 1025 + 1025, 0:1024])
            ksr_L = cnt["PS"]
            gate("tensor", "ps_s1")
            wait("tensor", "PS", ksr_L)
            for j in range(4):
                for dkb in range(8):
                    col = j * 8 + dkb
                    mc_k = mm(tn["kT"][0:1, j * 1024 + dkb * 128: j * 1024 + dkb * 128 + 128],
                              tn["ones_cb"][0:1, 0:1],
                              tn["ps_s1"][:, col:col + 1], True, True)
            setfree("kT", ("M", mc_k))
            wait("vector", "M", mc_k)
            v_ksl = vop(lambda e, o=ap("tc32"), a=tn["ps_s1"][:, 0:32]: e.tensor_copy(o, a))
            setfree("ps_s1", ("V", v_ksl))
            vop(lambda e, o=ap("ksB"), a=ap("tc32", slice(0, 8)), m=ap("mselc", slice(0, 1)):
                e.tensor_scalar(o, a, m, None, ALU.mult))
            vop(lambda e, o=ap("ksA"), a=ap("tc32", slice(8, 16)), m=ap("mselc", slice(1, 2)), c=ap("ksB"):
                e.scalar_tensor_tensor(o, a, m, c, ALU.mult, ALU.add))
            vop(lambda e, o=ap("ksB"), a=ap("tc32", slice(16, 24)), m=ap("mselc", slice(2, 3)), c=ap("ksA"):
                e.scalar_tensor_tensor(o, a, m, c, ALU.mult, ALU.add))
            vop(lambda e, o=ap("ksA"), a=ap("tc32", slice(24, 32)), m=ap("mselc", slice(3, 4)), c=ap("ksB"):
                e.scalar_tensor_tensor(o, a, m, c, ALU.mult, ALU.add))
            v_ksc = vop(lambda e, o=ap("kscb"), a=ap("ksA"): e.tensor_copy(o, a))

            bnd = dict(cnt)
            bnd_free = {
                "kT": ["M", "KT", "V"], "ktok": ["M"], "vtok": ["M", "Qv"],
                "aT0": ["Qa0"], "aT1": ["Qa1"], "xnT0": ["M"], "xnT1": ["M"],
                "xnf0": ["V", "Gx0"], "xnf1": ["V", "Gx1"],
                "xnb0": ["WP0", "Qw0"], "xnb1": ["WP1", "Qw1"],
                "x2t0": ["Qx0"], "x2t1": ["Qx1"],
                "xg0": ["S", "V"], "xg1": ["S", "V"], "xg2": ["S", "V"],
                "rT0": ["Qs0", "Gc"], "rT1": ["Qs1"],
                "rT0_s": ["Qs0", "Gc"], "rT1_s": ["Qs1"],
                "mn0": ["S"], "mn1": ["S"], "Rb0": ["V"], "Rb1": ["V"],
            }
            for key, ss in bnd_free.items():
                free[key] = {s: bnd[s] for s in ss}

        denr = rowap("u", 0, slice(0, 512))
        recr = rowap("u", 0, slice(512, 1024))
        inv_s_ap = ap("bcs", slice(0, 1))
        as_ap = ap("bcs", slice(1, 2))
        invas_ap = rowap("smf", 0, slice(15, 16))
        attnT_v = {}
        ldL = {}
        kscv = {0: v_ksc if STAGE >= 3 else 0}
        kvbG = {0: kvb_G if STAGE >= 3 else 0}
        denM = {-1: 0, -2: 0}
        vrr = {}

        def p2_loads(gg):
            par = (gg + 1) % 2
            qb_ = ["qT", "kT"][par]
            vb_ = ["ktok", "vtok"][par]
            ab_ = ["aT0", "aT1"][par]
            lds = f"Ld{par}"
            gate("sync", qb_)
            dma("sync", lds, ap(qb_), sp_q[gg * 128:(gg + 1) * 128, :])
            gate("sync", vb_)
            dma("sync", lds, ap(vb_), sp_v[gg * 128:(gg + 1) * 128, :])
            gate("sync", ab_)
            ldL[gg] = dma("sync", lds, ap(ab_), sp_a[gg * 128:(gg + 1) * 128, :])

        def p2_den(gg):
            par = (gg + 1) % 2
            qb_ = ["qT", "kT"][par]
            ab_ = ["aT0", "aT1"][par]
            ksb_ = ["kscb", "kscb2"][gg % 2]
            gate("tensor", "ps_a")
            wait("tensor", f"Ld{par}", ldL[gg])
            wait("tensor", "V", kscv[gg])
            for sb in range(4):
                mm(tn["ones_cb"][:, 0:1], tn[ab_][:, PK[sb]:PK[sb] + PW[sb]],
                   tn["ps_a"][0:1, sb * 128:512], start=(sb == 0), stop=False)
            for dc in range(8):
                mmc_ = mm(ap(ksb_, slice(dc, dc + 1)), tn[qb_][:, dc * 512:(dc + 1) * 512],
                          tn["ps_a"][0:1, 0:512], start=False, stop=(dc == 7))
            denM[gg] = mmc_
            wait("vector", "M", mmc_)
            vop(lambda e, o=denr, a=tn["ps_a"][0:1, 0:512]:
                e.tensor_scalar(o, a, EPS_ATT, None, ALU.add))
            vop(lambda e, o=recr, a=denr: e.reciprocal(o, a))
            gate("vector", "rrowb")
            gate("vector", "sgb")
            vop(lambda e, o=denr, a=recr, s=invas_ap:
                e.tensor_scalar(o, a, s, None, ALU.mult))
            vop(lambda e, o=tn["rrowb"][0:1, :], a=denr: e.tensor_copy(o, a))
            vop(lambda e, o=recr, a=tn["rrowb"][0:1, :]: e.tensor_copy(o, a))
            vrr[gg] = vop(lambda e, o=tn["sgb"][0:1, 0:512], a=denr, b=recr:
                          e.tensor_tensor(o, a, b, ALU.subtract))
            setfree("ps_a", ("V", cnt["V"]))

        def p2_bcast(gg):
            mnb_ = f"mn{gg % 2}"
            gate("tensor", "ps_s0")
            wait("tensor", "V", vrr[gg])
            mm(tn["ones_rb"][0:1, :], tn["rrowb"][0:1, :], ap("ps_s0"), True, False)
            mcb_ = mm(tn["ones_rb"][0:1, :], tn["sgb"][0:1, 0:512], ap("ps_s0"), False, True)
            setfree("rrowb", ("M", mcb_))
            setfree("sgb", ("M", mcb_))
            wait("vector", "M", mcb_)
            gate("vector", mnb_)
            v_bc_ = vop(lambda e, o=ap(mnb_), a=ap("ps_s0"): e.tensor_copy(o, a))
            setfree("ps_s0", ("V", v_bc_))

        def p2_kschain(gg):
            src, dst_ = ("ksA", "ksB") if gg % 2 == 1 else ("ksB", "ksA")
            vop(lambda e, o=ap(dst_), a=ap(src), b=ap("ksg", slice((gg - 1) * 8, gg * 8)):
                e.tensor_tensor(o, a, b, ALU.add))
            wait("vector", "M", denM[gg - 2])
            kscv[gg] = vop(lambda e, o=ap(["kscb", "kscb2"][gg % 2]),
                           a=ap(dst_): e.tensor_copy(o, a))

        if STAGE >= 4:
            p2_loads(0)
            p2_loads(1)
            p2_den(0)
            p2_bcast(0)
        for g in (range(NG) if STAGE >= 4 else []):
            pb2 = (g + 1) % 2
            qb = ["qT", "kT"][pb2]
            vb = ["ktok", "vtok"][pb2]
            ab = ["aT0", "aT1"][pb2]
            at = f"xnT{g % 2}"
            mnb = f"mn{g % 2}"
            if g + 1 < NG:
                p2_kschain(g + 1)
                p2_den(g + 1)
            q_rt_pre = {}
            for i in (0, 1):
                ti = g * 4 + i
                x2 = f"x2t{i % 2}"
                rb = f"xnb{i % 2}"
                rt = f"rT{i % 2}"
                gate("sync", x2)
                l_x2 = dma("sync", f"X2{i % 2}", ap(x2), sp_x2[ti * 128:(ti + 1) * 128, :])
                gate("vector", rb)
                wait("vector", f"X2{i % 2}", l_x2)
                for h in range(2):
                    vop(lambda e, o=ap("i32h"), a=tn[x2][:, h * 512:(h + 1) * 512],
                        s=inv_s_ap: e.tensor_scalar(o, a, s, None, ALU.mult))
                    v_rb = vop(lambda e, o=tn[rb][:, h * 512:(h + 1) * 512], a=ap("i32h"):
                               e.tensor_copy(o, a))
                setfree(x2, ("V", v_rb))
                gate("scalar", rt)
                wait("scalar", "V", v_rb)
                dest = tn[rt].ap().rearrange("p (d t) -> p d t", d=8)[:, :, 0:128]
                q_rt_pre[i] = dmat("scalar", f"Qr{i % 2}", dest, ap(rb))
                setfree(rb, (f"Qr{i % 2}", q_rt_pre[i]))
            for eb in range(8):
                pq = f"ps_q{eb % 2}"
                gate("tensor", pq)
                if eb == 0:
                    wait("tensor", "Gk", kvbG[g])
                for dc in range(8):
                    mm(tn["w3"][:, dc * 1024 + eb * 128: dc * 1024 + (eb + 1) * 128],
                       tn[qb][:, dc * 512:(dc + 1) * 512],
                       ap(pq), start=(dc == 0), stop=False)
                for sb in range(4):
                    mmc = mm(tn[vb][:, sb * 1024 + eb * 128: sb * 1024 + (eb + 1) * 128],
                             tn[ab][:, PK[sb]:PK[sb] + PW[sb]],
                             tn[pq][:, sb * 128:512], start=False, stop=(sb == 3))
                wait("vector", "M", mmc)
                if eb == 0:
                    gate("vector", at)
                v_at = vop(lambda e, o=tn[at][:, eb * 512:(eb + 1) * 512], p=ap(pq), b=ap(mnb):
                           e.tensor_tensor(o, p, b, ALU.mult))
                setfree(pq, ("V", v_at))
                attnT_v[(g, eb)] = v_at
            last_yq_M = cnt["M"]
            setfree(qb, ("M", cnt["M"]))
            setfree(vb, ("M", cnt["M"]))
            setfree(ab, ("M", cnt["M"]))
            if g + 2 < NG:
                p2_loads(g + 2)
            if g + 1 < NG:
                for u16 in range(16):
                    dkb, eh = divmod(u16, 2)
                    sg = f"Rb{u16 % 2}"
                    gate("sync", sg)
                    ldS = dma("sync", f"Sg{u16 % 2}", tn[sg][:, 0:512],
                              sp_s[g * 1024 + dkb * 128: g * 1024 + (dkb + 1) * 128,
                                   eh * 512:(eh + 1) * 512])
                    wait("vector", f"Sg{u16 % 2}", ldS)
                    if u16 == 0:
                        wait("vector", "Gk", kvbG[g])
                    v_ad = vop(lambda e, o=tn["kvr"][:, dkb * 1024 + eh * 512: dkb * 1024 + (eh + 1) * 512],
                               a=tn[sg][:, 0:512]: e.tensor_tensor(o, o, a, ALU.add))
                    setfree(sg, ("V", v_ad))
                kv_adds_v = cnt["V"]
                wait("gpsimd", "V", kv_adds_v)
                wait("gpsimd", "M", last_yq_M)
                for dkb in range(8):
                    dma("gpsimd", "Gk", tn["w3"][:, dkb * 1024:(dkb + 1) * 1024],
                        tn["kvr"][:, dkb * 1024:(dkb + 1) * 1024])
                kvbG[g + 1] = cnt["Gk"]
            for i in range(4):
                ti = g * 4 + i
                rt = f"rT{i % 2}"
                slot = f"xg{ti % 3}"
                if i >= 2:
                    x2 = f"x2t{i % 2}"
                    rb = f"xnb{i % 2}"
                    gate("sync", x2)
                    l_x2 = dma("sync", f"X2{i % 2}", ap(x2), sp_x2[ti * 128:(ti + 1) * 128, :])
                    gate("vector", rb)
                    wait("vector", f"X2{i % 2}", l_x2)
                    for h in range(2):
                        vop(lambda e, o=ap("i32h"), a=tn[x2][:, h * 512:(h + 1) * 512],
                            s=inv_s_ap: e.tensor_scalar(o, a, s, None, ALU.mult))
                        v_rb = vop(lambda e, o=tn[rb][:, h * 512:(h + 1) * 512], a=ap("i32h"):
                                   e.tensor_copy(o, a))
                    setfree(x2, ("V", v_rb))
                    gate("scalar", rt)
                    wait("scalar", "V", v_rb)
                    dest = tn[rt].ap().rearrange("p (d t) -> p d t", d=8)[:, :, 0:128]
                    q_rt_pre[i] = dmat("scalar", f"Qr{i % 2}", dest, ap(rb))
                    setfree(rb, (f"Qr{i % 2}", q_rt_pre[i]))
                gate("sync", slot)
                l_x = dma("sync", f"Lx{ti % 3}", ap(slot), x_h[ti * 128:(ti + 1) * 128, :])
                for oh in range(2):
                    idx = (2 * i + oh) % 2
                    pv = f"ps_v{idx}"
                    gate("tensor", pv)
                    if i == 0 and oh == 0:
                        for eb in range(8):
                            wait("tensor", "V", attnT_v[(g, eb)])
                        wait("tensor", "Qw0", w2_Q[0])
                        wait("tensor", "Qw1", w2_Q[1])
                        wait("tensor", "WP0", wo_WP[0])
                        wait("tensor", "WP1", wo_WP[1])
                        wait("tensor", "V", v_bo)
                    for ac in range(8):
                        mm(tn[at][:, ac * 512 + i * 128: ac * 512 + (i + 1) * 128],
                           tn["w1"][:, ac * 1024 + oh * 512: ac * 1024 + (oh + 1) * 512],
                           ap(pv), start=(ac == 0), stop=False)
                    if oh == 0:
                        wait("tensor", f"Qr{i % 2}", q_rt_pre[i])
                    for ic in range(8):
                        mm(tn[rt][:, ic * 128:(ic + 1) * 128],
                           tn["w2"][:, ic * 1024 + oh * 512: ic * 1024 + (oh + 1) * 512],
                           ap(pv), start=False, stop=False)
                    mm(tn["ones_rb"][0:1, :], tn["boeff"][0:1, oh * 512:(oh + 1) * 512],
                       ap(pv), start=False, stop=False)
                    mmc = mm(tn["ones_rb"][0:1, :], tn["bvb"][0:1, oh * 512:(oh + 1) * 512],
                             ap(pv), start=False, stop=True)
                    if oh == 0:
                        gate("vector", f"xnf{i % 2}")
                    wait("vector", "M", mmc)
                    wait("vector", f"Lx{ti % 3}", l_x)
                    v_out = vop(lambda e, o=tn[f"xnf{i % 2}"][:, oh * 512:(oh + 1) * 512],
                                p=ap(pv), s=as_ap, x=tn[slot][:, oh * 512:(oh + 1) * 512]:
                                e.scalar_tensor_tensor(o, p, s, x, ALU.mult, ALU.add))
                    setfree(pv, ("V", v_out))
                setfree(slot, ("V", v_out))
                wait("scalar", "V", v_out)
                q_o = dma("scalar", f"Qo{i % 2}", out_h[ti * 128:(ti + 1) * 128, :], ap(f"xnf{i % 2}"))
                setfree(f"xnf{i % 2}", (f"Qo{i % 2}", q_o))
                setfree(rt, ("M", cnt["M"]))
            setfree(at, ("M", cnt["M"]))
            if g + 1 < NG:
                p2_bcast(g + 1)

        wait("gpsimd", "Qo0", cnt["Qo0"])
        wait("gpsimd", "Qo1", cnt["Qo1"])

        @block.sync
        def _(eng):
            for f in plan["sync"]:
                f(eng)

        @block.scalar
        def _(eng):
            for f in plan["scalar"]:
                f(eng)

        @block.vector
        def _(eng):
            for f in plan["vector"]:
                f(eng)

        @block.tensor
        def _(eng):
            for f in plan["tensor"]:
                f(eng)

        @block.gpsimd
        def _(eng):
            for f in plan["gpsimd"]:
                f(eng)

    return nc


_NC_CACHE = None


def kernel(L0_attn_chunk, L0_attn_Wq, L0_attn_bq, L0_attn_Wk, L0_attn_bk,
           L0_attn_Wv, L0_attn_bv, L0_attn_Wo, L0_attn_bo,
           L0_ffn_weight, L0_ffn_gamma, L0_ffn_beta):
    global _NC_CACHE
    if _NC_CACHE is None:
        _NC_CACHE = build()
    nc = _NC_CACHE
    c = np.ascontiguousarray

    x = np.asarray(L0_attn_chunk, dtype=np.float32)
    wq = c(np.asarray(L0_attn_Wq, np.float32))
    wk = c(np.asarray(L0_attn_Wk, np.float32))
    wv = c(np.asarray(L0_attn_Wv, np.float32))
    wo = c(np.asarray(L0_attn_Wo, np.float32))
    wf = c(np.asarray(L0_ffn_weight, np.float32))
    bqc = c(np.asarray(L0_attn_bq, np.float32).reshape(8, 128).T)
    bkc = c(np.asarray(L0_attn_bk, np.float32).reshape(8, 128).T)
    bvr = c(np.asarray(L0_attn_bv, np.float32).reshape(1, D))
    bor = c(np.asarray(L0_attn_bo, np.float32).reshape(1, D))
    gamb = c(np.broadcast_to(np.asarray(L0_ffn_gamma, np.float32).reshape(1, D), (128, D)))
    betb = c(np.broadcast_to(np.asarray(L0_ffn_beta, np.float32).reshape(1, D), (128, D)))

    identc = np.eye(128, dtype=np.float32)
    mtric = c(np.triu(np.ones((128, 128), np.float32)))
    in_maps = []
    for core in range(NCORE):
        b, q = core // 4, core % 4
        msel = np.zeros((128, 4), np.float32)
        msel[:, :q] = 1.0
        in_maps.append({
            "x": c(x[b, q * TC:(q + 1) * TC, :]),
            "wq": wq, "wk": wk, "wv": wv, "wo": wo, "wf": wf,
            "bqc": bqc, "bkc": bkc, "bvr": bvr, "bor": bor,
            "gamb": gamb, "betb": betb, "msel": msel,
            "identc": identc, "mtric": mtric,
        })
    res = run_bass_kernel_spmd(nc, in_maps, core_ids=list(range(NCORE)))
    out = np.empty((B, T, D), np.float32)
    for core in range(NCORE):
        b, q = core // 4, core % 4
        out[b, q * TC:(q + 1) * TC, :] = res.results[core]["out"]
    return out
